# version 04
# baseline (speedup 1.0000x reference)
import contextlib
import numpy as np
import ml_dtypes
import concourse.bass as bass
import concourse.mybir as mybir
import concourse.bass_utils as _bass_utils
from concourse.bass_utils import run_bass_kernel_spmd



_JITS = {}


class _Res:
    def __init__(self, results):
        self.results = results


def run_spmd_cached(nc, in_maps, core_ids):
    return _bass_utils.run_bass_kernel_spmd(nc, in_maps, core_ids=core_ids)


def _run_spmd_cached_unused(nc, in_maps, core_ids):
    import jax
    from jax.sharding import Mesh, PartitionSpec
    from jax.experimental.shard_map import shard_map
    from concourse import bass2jax, mybir as _mb
    n_cores = len(core_ids)
    key = id(nc)
    if key not in _JITS:
        bass2jax.install_neuronx_cc_hook()
        assert nc.dbg_addr is None
        partition_name = (nc.partition_id_tensor.name
                          if nc.partition_id_tensor else None)
        in_names, out_names, out_avals, zero_shapes = [], [], [], []
        for alloc in nc.m.functions[0].allocations:
            if not isinstance(alloc, _mb.MemoryLocationSet):
                continue
            name = alloc.memorylocations[0].name
            if alloc.kind == "ExternalInput":
                if name != partition_name:
                    in_names.append(name)
            elif alloc.kind == "ExternalOutput":
                shape = tuple(alloc.tensor_shape)
                dtype = _mb.dt.np(alloc.dtype)
                out_avals.append(jax.core.ShapedArray(shape, dtype))
                out_names.append(name)
                zero_shapes.append((shape, dtype))
        n_params = len(in_names)
        n_outs = len(out_names)
        all_in = list(in_names) + list(out_names)
        if partition_name is not None:
            all_in.append(partition_name)

        def _body(*args):
            operands = list(args)
            if partition_name is not None:
                operands.append(bass2jax.partition_id_tensor())
            outs = bass2jax._bass_exec_p.bind(
                *operands,
                out_avals=tuple(out_avals),
                in_names=tuple(all_in),
                out_names=tuple(out_names),
                lowering_input_output_aliases=(),
                sim_require_finite=True,
                sim_require_nnan=True,
                nc=nc,
            )
            return tuple(outs)
        donate = tuple(range(n_params, n_params + n_outs))
        devices = jax.devices()[:n_cores]
        mesh = Mesh(np.asarray(devices), ("core",))
        sharded = jax.jit(
            shard_map(_body, mesh=mesh,
                      in_specs=(PartitionSpec("core"),) * (n_params + n_outs),
                      out_specs=(PartitionSpec("core"),) * n_outs,
                      check_rep=False),
            donate_argnums=donate, keep_unused=True)
        _JITS[key] = (sharded, in_names, out_names, out_avals, zero_shapes,
                      n_cores)
    sharded, in_names, out_names, out_avals, zero_shapes, nc_prev = _JITS[key]
    assert nc_prev == n_cores
    concat_in = [np.concatenate([np.asarray(in_maps[c][nm])
                                 for c in range(n_cores)], axis=0)
                 for nm in in_names]
    concat_zeros = [np.zeros((n_cores * s[0], *s[1:]), d)
                    for s, d in zero_shapes]
    out_arrs = sharded(*concat_in, *concat_zeros)
    results = [
        {nm: np.asarray(out_arrs[i]).reshape(
            n_cores, *out_avals[i].shape)[c]
         for i, nm in enumerate(out_names)}
        for c in range(n_cores)
    ]
    return _Res(results)

F32 = mybir.dt.float32
BF16 = mybir.dt.bfloat16
AF = mybir.ActivationFunctionType
ALU = mybir.AluOpType
AX = mybir.AxisListType
BF = ml_dtypes.bfloat16

NCORES = 8
T = 2048
D = 1024
NCH = 8
ENGINES = ("sync", "tensor", "vector", "scalar", "gpsimd")

NEG_BIG = -1.0e30



def fm_pack(x2d):
    t = x2d.shape[0]
    return np.ascontiguousarray(
        x2d.reshape(t, NCH, 128).transpose(2, 1, 0)).reshape(128, NCH * t)


def fm_unpack(xfm, t):
    return np.ascontiguousarray(
        xfm.reshape(128, NCH, t).transpose(2, 1, 0)).reshape(t, D)


def col_pack(v):
    return np.ascontiguousarray(v.reshape(NCH, 128).T)


def row_pack(v):
    return np.ascontiguousarray(v.reshape(NCH, 128))


def pack_wio(w, mcols=128):
    din, dout = w.shape
    kc = din // 128
    nm = dout // mcols
    return np.ascontiguousarray(
        w.reshape(kc, 128, nm, mcols).transpose(2, 1, 0, 3)
        .reshape(nm, 128, kc * mcols))




class Prog:
    def __init__(self, name="prog"):
        self.name = name
        self.nc = bass.Bass("TRN2", target_bir_lowering=False)
        self.st = contextlib.ExitStack()
        self.q = {e: [] for e in ENGINES}
        self.semh = {}
        self.semc = {}
        self._buf_names = set()
        self.bank_dep = [("ev", 0) for _ in range(8)]
        self._waited = {e: {} for e in ENGINES}
        self.sem_owner = {}

    def sem(self, name):
        if name not in self.semh:
            self.semh[name] = self.st.enter_context(self.nc.semaphore(name))
            self.semc[name] = 0
        return self.semh[name]

    def cnt(self, name):
        self.sem(name)
        return self.semc[name]

    def bump(self, name, n=1):
        self.sem(name)
        self.semc[name] += n
        return self.semc[name]

    def sbuf(self, name, shape, dtype):
        assert name not in self._buf_names, name
        self._buf_names.add(name)
        return self.st.enter_context(self.nc.sbuf_tensor(name, shape, dtype))

    def psum(self, name, shape, dtype):
        assert name not in self._buf_names, name
        self._buf_names.add(name)
        return self.st.enter_context(self.nc.psum_tensor(name, shape, dtype))

    def inp(self, name, shape, dtype=F32):
        return self.nc.dram_tensor(name, shape, dtype, kind="ExternalInput").ap()

    def outp(self, name, shape, dtype=F32):
        return self.nc.dram_tensor(name, shape, dtype, kind="ExternalOutput").ap()

    def dram(self, name, shape, dtype):
        return self.nc.dram_tensor(name, shape, dtype, kind="Internal")

    def on(self, eng, fn):
        self.q[eng].append(fn)

    def wait(self, eng, sname, val):
        if val <= 0:
            return
        if self._waited[eng].get(sname, -1) >= val:
            return
        self._waited[eng][sname] = val
        h = self.sem(sname)
        self.on(eng, lambda e, h=h, v=val: e.wait_ge(h, v))

    def _own(self, sname, eng):
        if sname in self.sem_owner:
            assert self.sem_owner[sname] == eng, \
                f"sem {sname} inc from {eng} and {self.sem_owner[sname]}"
        else:
            self.sem_owner[sname] = eng

    def dma(self, eng, dst, src, sname="dma"):
        self._own(sname, eng)
        h = self.sem(sname)
        c = self.bump(sname, 16)
        self.on(eng, lambda e, d=dst, s=src, h=h: e.dma_start(out=d, in_=s)
                .then_inc(h, 16))
        return c

    def ins(self, eng, fn, sname=None, n=1):
        if sname is not None:
            self._own(sname, eng)
        drain = eng in ("vector", "scalar")
        if sname is None:
            self.on(eng, fn)
            if drain:
                self.on(eng, lambda e: e.drain())
            return None
        h = self.sem(sname)
        c = self.bump(sname, n)
        if drain:
            self.on(eng, fn)
            self.on(eng, lambda e, h=h, n=n: e.drain().then_inc(h, n))
        else:
            self.on(eng, lambda e, f=fn, h=h, n=n: f(e).then_inc(h, n))
        return c

    def banks_req(self, bank_ids):
        need = {}
        for b in bank_ids:
            s, c = self.bank_dep[b]
            if c > need.get(s, 0):
                need[s] = c
        for s, c in need.items():
            self.wait("tensor", s, c)

    def banks_done(self, bank_ids, sname, cntv):
        for b in bank_ids:
            self.bank_dep[b] = (sname, cntv)

    def barrier(self):
        for eng in ENGINES:
            if not self.q[eng]:
                continue
            for s in list(self.semc):
                self.wait(eng, s, self.semc[s])

    def finalize(self):
        nc = self.nc
        with nc.Block() as block:
            for eng in ENGINES:
                if not self.q[eng]:
                    continue
                dec = getattr(block, eng)

                def runner(e, _q=tuple(self.q[eng])):
                    for f in _q:
                        f(e)
                dec(runner)
        self.st.close()
        return nc


class Arena:

    def __init__(self, p, name, cols, dtype):
        self.t = p.sbuf(name, [128, cols], dtype)
        self.cols = cols
        self.off = 0
        self.marks = []

    def take(self, cols):
        assert self.off + cols <= self.cols, \
            f"arena overflow: {self.off}+{cols} > {self.cols}"
        o = self.off
        self.off += cols
        t = self.t

        class Slice:
            def __getitem__(_, key):
                if isinstance(key, tuple) and len(key) == 2:
                    rows, csl = key
                    if isinstance(csl, slice):
                        a = csl.start or 0
                        b = csl.stop if csl.stop is not None else cols
                        step = csl.step
                        if step is None:
                            return t[rows, o + a:o + b]
                        return t[rows, o + a:o + b:step]
                raise KeyError(key)
        return Slice()

    def mark(self):
        self.marks.append(self.off)

    def release(self):
        self.off = self.marks.pop()




def emit_aux(p):
    aux = {}
    for nm, shape, dt in (
        ("eye_h", [128, 128], BF16),
        ("eye_f", [128, 128], F32),
        ("onesr_f", [1, 128], F32),
        ("onesr_h", [1, 128], BF16),
        ("onesc_f", [128, 1], F32),
        ("onesc_h", [128, 1], BF16),
    ):
        ap = p.inp("aux_" + nm, shape, dt)
        aux[nm] = p.sbuf("aux_" + nm + "_s", shape, dt)
        p.dma("sync", aux[nm][:], ap, "ld")
    return aux


def host_sel16():
    s = np.zeros((16, 16 * 128), np.float32)
    for e in range(16):
        s[e, e * 128:(e + 1) * 128] = 1.0
    return s


def host_aux():
    return {
        "aux_eye_h": np.eye(128, dtype=BF),
        "aux_eye_f": np.eye(128, dtype=np.float32),
        "aux_onesr_f": np.ones((1, 128), np.float32),
        "aux_onesr_h": np.ones((1, 128), BF),
        "aux_onesc_f": np.ones((128, 1), np.float32),
        "aux_onesc_h": np.ones((128, 1), BF),
    }




def emit_ln(p, aux, xap, xn_ap, nt, pb, scr, tag, g_col=None, b_col=None):
    scr.mark()
    sq = scr.take(nt)
    bA = scr.take(nt)
    bN = scr.take(nt)
    RW = scr.take(nt)
    RW2 = scr.take(nt)
    r_mu = RW[0:1, 0:nt]
    r_ms = RW[32:33, 0:nt]
    r_m2 = RW2[32:33, 0:nt]
    r_t = RW[64:65, 0:nt]
    r_rstd = RW2[0:1, 0:nt]
    r_nmr = r_mu

    ntile = nt // 512
    assert nt % 512 == 0 and ntile <= 4

    p.banks_req(range(8))
    if p.semc.get("ev", 0) > 0:
        p.wait("scalar", "ev", p.cnt("ev"))
    for c in range(NCH):
        ac = p.ins("scalar", lambda e, c=c: e.activation(
            sq[:, 0:nt], xap(c, 0, nt), AF.Square), "ac")
        for j in range(ntile):
            t0 = j * 512
            p.ins("tensor", lambda e, c=c, t0=t0: e.matmul(
                pb[0:1, t0:t0 + 512], aux["onesc_f"][:],
                xap(c, t0, t0 + 512), start=(c == 0), stop=(c == NCH - 1)))
        p.wait("tensor", "ac", ac)
        for j in range(ntile):
            t0 = j * 512
            mmc = p.ins("tensor", lambda e, c=c, t0=t0: e.matmul(
                pb[0:1, 2048 + t0:2048 + t0 + 512], aux["onesc_f"][:],
                sq[:, t0:t0 + 512], start=(c == 0), stop=(c == NCH - 1)), "mm")
        p.wait("scalar", "mm", mmc)
    p.wait("vector", "mm", p.cnt("mm"))
    p.ins("vector", lambda e: e.tensor_scalar_mul(
        r_mu, pb[0:1, 0:nt], 1.0 / D))
    p.ins("vector", lambda e: e.tensor_scalar_mul(
        r_ms, pb[0:1, 2048:2048 + nt], 1.0 / D))
    p.ins("vector", lambda e: e.tensor_mul(r_m2, r_mu, r_mu))
    p.ins("vector", lambda e: e.tensor_sub(r_t, r_ms, r_m2))
    ev = p.ins("vector", lambda e: e.tensor_scalar_add(r_t, r_t, 1e-5), "ev")
    p.wait("scalar", "ev", ev)
    ac = p.ins("scalar", lambda e: e.activation(r_t, r_t, AF.Sqrt), "ac")
    p.wait("vector", "ac", ac)
    p.ins("vector", lambda e: e.reciprocal(r_rstd, r_t))
    ev = p.ins("vector", lambda e: e.scalar_tensor_tensor(
        r_nmr, r_mu, -1.0, r_rstd, ALU.mult, ALU.mult), "ev")
    p.wait("tensor", "ev", ev)
    for j in range(ntile):
        t0 = j * 512
        p.ins("tensor", lambda e, t0=t0: e.matmul(
            pb[:, t0:t0 + 512], aux["onesr_f"][:],
            RW2[0:1, t0:t0 + 512], start=True, stop=True))
        mmc = p.ins("tensor", lambda e, t0=t0: e.matmul(
            pb[:, 2048 + t0:2048 + t0 + 512], aux["onesr_f"][:],
            RW[0:1, t0:t0 + 512], start=True, stop=True), "mm")
    p.wait("vector", "mm", mmc)
    p.ins("vector", lambda e: e.tensor_copy(bA[:, 0:nt], pb[:, 0:nt]))
    ev = p.ins("vector", lambda e: e.tensor_copy(
        bN[:, 0:nt], pb[:, 2048:2048 + nt]), "ev")
    p.banks_done(range(8), "ev", ev)
    for c in range(NCH):
        if g_col is None:
            p.ins("vector", lambda e, c=c: e.tensor_mul(
                sq[:, 0:nt], xap(c, 0, nt), bA[:, 0:nt]))
            ev = p.ins("vector", lambda e, c=c: e.tensor_add(
                xn_ap(c), sq[:, 0:nt], bN[:, 0:nt]), "ev")
        else:
            p.ins("vector", lambda e, c=c: e.scalar_tensor_tensor(
                sq[:, 0:nt], xap(c, 0, nt), g_col(c), bA[:, 0:nt],
                ALU.mult, ALU.mult))
            p.ins("vector", lambda e, c=c: e.scalar_tensor_tensor(
                sq[:, 0:nt], bN[:, 0:nt], g_col(c), sq[:, 0:nt],
                ALU.mult, ALU.add))
            ev = p.ins("vector", lambda e, c=c: e.tensor_scalar_add(
                xn_ap(c), sq[:, 0:nt], b_col(c)), "ev")
    scr.release()
    return ev




def wload_chunks(p, dst, kchunks, ncols, src_ap, wait_sem=None, wait_val=0,
                 sname="wld"):
    if wait_sem is not None and wait_val > 0:
        p.wait("gpsimd", wait_sem, wait_val)
    c = None
    for k in range(kchunks):
        c = p.dma("gpsimd", dst[:, k * ncols:(k + 1) * ncols],
                  src_ap[k * 128:(k + 1) * 128, :], sname)
    return c




def _trunc_finish(p, yout):
    p.barrier()
    c = p.dma("sync", yout, p.sbuf("truncz", [128, 16], F32)[:, 0:16], "tst")
    p.wait("sync", "tst", c)
    return p


def build_l1_ffn(debug=False, skip_shared_z=False, serial_p2=False,
                 stop_after=None):
    p = Prog("l1ffn")
    E = 16

    xin = p.inp("xfm", [128, NCH * T])
    wg = p.inp("wg", [D, E])
    ws1 = p.inp("ws1", [D, 2048])
    ws2 = p.inp("ws2", [1024, D])
    w1 = p.inp("w1", [E, D, 1024])
    w2 = p.inp("w2", [E, 512, D])
    yout = p.outp("yfm", [128, NCH * T])

    dbg = {}
    aux = emit_aux(p)
    sel16 = p.inp("sel16", [16, 16 * 128], F32)
    sel16s = p.sbuf("sel16s", [16, 16 * 128], F32)
    p.dma("sync", sel16s[:], sel16, "ld")
    X = p.sbuf("X", [128, NCH * T], F32)
    XN = p.sbuf("XN", [128, NCH * T], BF16)
    pb = p.psum("pb", [128, 4096], F32)
    ah = Arena(p, "AH", 25216, BF16)
    af = Arena(p, "AF", 10240, F32)
    ldc = p.dma("sync", X[:], xin, "ld")
    for eng in ("tensor", "scalar", "vector"):
        p.wait(eng, "ld", p.cnt("ld"))

    def xap(c, t0, t1):
        return X[:, c * T + t0:c * T + t1]

    def xnap(c, t0, t1):
        return XN[:, c * T + t0:c * T + t1]

    emit_ln(p, aux, xap, lambda c: XN[:, c * T:(c + 1) * T], T, pb, af, "o")

    ah.mark()
    af.mark()
    wgs = ah.take(NCH * E)
    wc = wload_chunks(p, wgs, NCH, E, wg)
    LGS = af.take(T)
    p.wait("tensor", "wld", wc)
    p.wait("tensor", "ev", p.cnt("ev"))
    p.banks_req(range(4))
    for c in range(NCH):
        for j in range(4):
            mmc = p.ins("tensor", lambda e, c=c, j=j: e.matmul(
                pb[0:16, j * 512:(j + 1) * 512],
                wgs[:, c * E:(c + 1) * E], xnap(c, j * 512, (j + 1) * 512),
                start=(c == 0), stop=(c == NCH - 1)),
                "mm" if (c == NCH - 1) else None)
    p.wait("vector", "mm", mmc)
    ev = p.ins("vector", lambda e: e.tensor_copy(LGS[0:16, 0:T], pb[0:16, 0:T]), "ev")
    p.banks_done(range(4), "ev", ev)

    CWFS = af.take(T)
    PRT = p.sbuf("PRT", [128, 16], F32)
    PM = p.sbuf("PM", [128, 16], F32)
    I1 = p.sbuf("I1", [128, 16], F32)
    I2 = p.sbuf("I2", [128, 16], F32)
    CWT = p.sbuf("CWT", [128, 16], F32)
    PT0 = p.sbuf("PT0", [128, 16], F32)
    if debug:
        CAP = p.sbuf("CAP", [128, 64], F32)
        CAP2 = p.sbuf("CAP2", [128, T], F32)
    mxc = p.sbuf("mxc", [128, 1], F32)
    smc = p.sbuf("smc", [128, 1], F32)
    p.wait("tensor", "ev", ev)
    for t in range(16):
        bank = 2048 + (t % 2) * 512
        bid = 4 + (t % 2)
        p.banks_req([bid])
        mmc = p.ins("tensor", lambda e, t=t, bank=bank: e.transpose(
            pb[:, bank:bank + 16], LGS[0:16, t * 128:(t + 1) * 128],
            aux["eye_f"][0:16, 0:16]), "mm")
        p.wait("vector", "mm", mmc)
        ev0 = p.ins("vector", lambda e, bank=bank: e.tensor_copy(
            PT0[:], pb[:, bank:bank + 16]), "ev")
        p.banks_done([bid], "ev", ev0)
        p.ins("vector", lambda e: e.tensor_reduce(
            mxc[:], PT0[:], AX.X, ALU.max))
        evn = p.ins("vector", lambda e: e.tensor_scalar_mul(
            mxc[:], mxc[:], -1.0), "ev")
        p.wait("scalar", "ev", evn)
        ac = p.ins("scalar", lambda e: e.activation(
            PRT[:], PT0[:], AF.Exp, bias=mxc[:, 0:1]), "ac")
        p.wait("vector", "ac", ac)
        p.ins("vector", lambda e: e.tensor_reduce(
            smc[:], PRT[:], AX.X, ALU.add))
        p.ins("vector", lambda e: e.reciprocal(smc[:], smc[:]))
        p.ins("vector", lambda e: e.tensor_scalar_mul(PRT[:], PRT[:],
                                                      smc[:, 0:1]))
        p.ins("vector", lambda e: e.tensor_reduce(
            mxc[:], PRT[:], AX.X, ALU.max))
        p.ins("vector", lambda e: e.tensor_scalar_mul(mxc[:], mxc[:], -1.0))
        p.ins("vector", lambda e: e.tensor_scalar(
            I1[:], PRT[:], mxc[:, 0:1], 0.0, ALU.add, ALU.is_ge))
        p.ins("vector", lambda e: e.tensor_scalar(
            PM[:], I1[:], -1.0, 1.0, ALU.mult, ALU.add))
        p.ins("vector", lambda e: e.tensor_mul(PM[:], PRT[:], PM[:]))
        p.ins("vector", lambda e: e.tensor_reduce(
            mxc[:], PM[:], AX.X, ALU.max))
        p.ins("vector", lambda e: e.tensor_scalar_mul(mxc[:], mxc[:], -1.0))
        p.ins("vector", lambda e: e.tensor_scalar(
            I2[:], PM[:], mxc[:, 0:1], 0.0, ALU.add, ALU.is_ge))
        p.ins("vector", lambda e: e.tensor_add(I1[:], I1[:], I2[:]))
        ev = p.ins("vector", lambda e: e.tensor_mul(CWT[:], PRT[:], I1[:]),
                   "ev")
        if debug and t == 0:
            p.ins("vector", lambda e: e.tensor_copy(CAP[:, 0:16], PT0[:]))
            p.ins("vector", lambda e: e.tensor_copy(CAP[:, 16:32], PRT[:]))
            p.ins("vector", lambda e: e.tensor_copy(CAP[:, 32:48], I1[:]))
            ev = p.ins("vector", lambda e: e.tensor_copy(CAP[:, 48:64],
                                                         CWT[:]), "ev")
        p.wait("tensor", "ev", ev)
        bank2 = 2048 + (t % 2) * 512 + 128
        bid2 = 4 + (t % 2)
        p.banks_req([bid2])
        mmc = p.ins("tensor", lambda e, bank2=bank2: e.transpose(
            pb[0:16, bank2:bank2 + 128], CWT[:], aux["eye_f"][:]), "mm")
        p.wait("vector", "mm", mmc)
        ev = p.ins("vector", lambda e, t=t, bank2=bank2: e.tensor_copy(
            CWFS[0:16, t * 128:(t + 1) * 128], pb[0:16, bank2:bank2 + 128]), "ev")
        p.banks_done([bid2], "ev", ev)

    if stop_after == "router":
        return _trunc_finish(p, yout[:, 0:16])
    hkind = "ExternalOutput" if debug else "Internal"
    h_scr = p.nc.dram_tensor("h_scr", [E, 128, 4 * T], BF16, kind=hkind)
    hsh_scr = p.nc.dram_tensor("hsh_scr", [128, 8 * T], BF16, kind=hkind)
    WB = [ah.take(NCH * 1024), ah.take(NCH * 1024)]
    HEb = [ah.take(4 * 512), ah.take(4 * 512)]
    SIL = ah.take(512)
    CWB = af.take(T)
    TF = af.take(512)
    SILF = af.take(512)

    mm_after = {}
    hst_after = {}
    for e_idx in range(E):
        buf = e_idx % 2
        wc = wload_chunks(p, WB[buf], NCH, 1024, w1[e_idx],
                          "mm", mm_after.get(e_idx - 2, 0), f"wld{buf}")
        p.banks_req(range(4))
        for j in range(4):
            mmc = p.ins("tensor", lambda e, e_idx=e_idx, j=j: e.matmul(
                pb[:, j * 512:(j + 1) * 512],
                sel16s[:, e_idx * 128:(e_idx + 1) * 128],
                CWFS[0:16, j * 512:(j + 1) * 512],
                start=True, stop=True), "mm" if j == 3 else None)
        p.wait("vector", "mm", mmc)
        cwev = p.ins("vector", lambda e: e.tensor_copy(CWB[:, 0:T],
                                                       pb[:, 0:T]), "ev")
        p.banks_done(range(4), "ev", cwev)
        if debug and e_idx == 0:
            p.ins("vector", lambda e: e.tensor_copy(CAP2[:, 0:T],
                                                    CWB[:, 0:T]))
        p.wait("tensor", f"wld{buf}", wc)
        for tg in range(4):
            t0 = tg * 512
            hbuf = HEb[tg % 2]
            p.wait("vector", "hst", hst_after.get((e_idx, tg - 2),
                                                  hst_after.get((e_idx - 1, tg + 2), 0)))
            for rnd in range(2):
                bank_ids = [rnd * 4 + i for i in range(4)]
                p.banks_req(bank_ids)
                for half in range(2):
                    for mi in range(2):
                        m = rnd * 2 + mi
                        och = half * 4 + m
                        bank = (rnd * 4 + half * 2 + mi) * 512
                        for c in range(NCH):
                            mmc = p.ins("tensor", lambda e, och=och, c=c,
                                        bank=bank, t0=t0, buf=buf: e.matmul(
                                pb[:, bank:bank + 512],
                                WB[buf][:, c * 1024 + och * 128:
                                        c * 1024 + och * 128 + 128],
                                xnap(c, t0, t0 + 512),
                                start=(c == 0), stop=(c == NCH - 1)),
                                "mm" if c == NCH - 1 else None)
                mmdone = mmc
                if debug and e_idx == 0 and tg == 0 and rnd == 0:
                    p.wait("vector", "mm", mmdone)
                    p.ins("vector", lambda e: e.tensor_copy(
                        CAP2[:, 512:1024], pb[:, 0:512]))
                    p.ins("vector", lambda e: e.tensor_copy(
                        CAP2[:, 1024:1536], pb[:, 2 * 512:3 * 512]))
                    p.wait("vector", "wld", p.cnt("wld"))
                    p.ins("vector", lambda e: e.tensor_copy(
                        CAP2[:, 1536:1664], WB[0][:, 0:128]))
                    p.ins("vector", lambda e: e.tensor_copy(
                        CAP2[:, 1664:1792], XN[:, 0:128]))
                for mi in range(2):
                    m = rnd * 2 + mi
                    ubank = (rnd * 4 + mi) * 512
                    gbank = (rnd * 4 + 2 + mi) * 512
                    p.wait("scalar", "mm", mmdone)
                    p.wait("scalar", "sil", p.cnt("sil"))
                    ac = p.ins("scalar", lambda e, gbank=gbank: e.activation(
                        SILF[:, 0:512], pb[:, gbank:gbank + 512], AF.Exp,
                        scale=-1.0), "ac")
                    p.wait("vector", "ac", ac)
                    p.wait("vector", "mm", mmdone)
                    p.ins("vector", lambda e: e.tensor_scalar_add(
                        SILF[:, 0:512], SILF[:, 0:512], 1.0))
                    p.ins("vector", lambda e: e.reciprocal(
                        SILF[:, 0:512], SILF[:, 0:512]))
                    p.ins("vector", lambda e, gbank=gbank: e.tensor_mul(
                        SILF[:, 0:512], SILF[:, 0:512],
                        pb[:, gbank:gbank + 512]))
                    p.ins("vector", lambda e, ubank=ubank:
                          e.scalar_tensor_tensor(
                        TF[:, 0:512], pb[:, ubank:ubank + 512], 1.0,
                        SILF[:, 0:512], ALU.mult, ALU.mult), "sil")
                    ev = p.ins("vector", lambda e, m=m, hbuf=hbuf, t0=t0:
                               e.scalar_tensor_tensor(
                        hbuf[:, m * 512:(m + 1) * 512], TF[:, 0:512],
                        1.0, CWB[:, t0:t0 + 512], ALU.mult, ALU.mult), "ev")
                p.banks_done(bank_ids, "ev", ev)
            if debug and e_idx == 0 and tg == 0:
                p.ins("vector", lambda e, hbuf=hbuf: e.tensor_copy(
                    CAP2[:, 0:512], hbuf[:, 0:512]))
            p.wait("sync", "ev", p.cnt("ev"))
            for m in range(4):
                hst_after[(e_idx, tg)] = p.dma(
                    "sync", h_scr.ap()[e_idx, :, m * T + t0:m * T + t0 + 512],
                    hbuf[:, m * 512:(m + 1) * 512], "hst")
        mm_after[e_idx] = p.cnt("mm")
        if stop_after == ("expert", e_idx):
            return _trunc_finish(p, yout[:, 0:16])

    if stop_after == "pass1":
        return _trunc_finish(p, yout[:, 0:16])
    wsh = [WB[0], WB[1]]

    def wsh_ap(c, lo, n):
        b = c // 4
        o = (c % 4) * 2048
        return wsh[b][:, o + lo:o + lo + n]
    clast = None
    p.wait("gpsimd", "mm", mm_after[E - 1])
    for k in range(NCH):
        clast = p.dma("gpsimd", wsh_ap(k, 0, 2048),
                      ws1[k * 128:(k + 1) * 128, :], "wld")
    p.wait("tensor", "wld", clast)
    for tg in range(4):
        t0 = tg * 512
        for m in range(8):
            ub = (m % 4) * 512
            gb = (4 + m % 4) * 512
            p.banks_req([m % 4, 4 + m % 4])
            for c in range(NCH):
                p.ins("tensor", lambda e, m=m, c=c, ub=ub, t0=t0: e.matmul(
                    pb[:, ub:ub + 512], wsh_ap(c, m * 128, 128),
                    xnap(c, t0, t0 + 512),
                    start=(c == 0), stop=(c == NCH - 1)))
            for c in range(NCH):
                mmc = p.ins("tensor", lambda e, m=m, c=c, gb=gb, t0=t0:
                            e.matmul(
                    pb[:, gb:gb + 512], wsh_ap(c, (8 + m) * 128, 128),
                    xnap(c, t0, t0 + 512),
                    start=(c == 0), stop=(c == NCH - 1)),
                    "mm" if c == NCH - 1 else None)
            p.wait("scalar", "mm", mmc)
            p.wait("scalar", "sil", p.cnt("sil"))
            ac = p.ins("scalar", lambda e, gb=gb: e.activation(
                SILF[:, 0:512], pb[:, gb:gb + 512], AF.Exp, scale=-1.0), "ac")
            p.wait("vector", "ac", ac)
            p.wait("vector", "mm", mmc)
            p.wait("vector", "hst", p.cnt("hst"))
            p.ins("vector", lambda e: e.tensor_scalar_add(
                SILF[:, 0:512], SILF[:, 0:512], 1.0))
            p.ins("vector", lambda e: e.reciprocal(
                SILF[:, 0:512], SILF[:, 0:512]))
            p.ins("vector", lambda e, gb=gb: e.tensor_mul(
                SILF[:, 0:512], SILF[:, 0:512], pb[:, gb:gb + 512]))
            ev = p.ins("vector", lambda e, ub=ub: e.scalar_tensor_tensor(
                HEb[0][:, 0:512], pb[:, ub:ub + 512], 1.0, SILF[:, 0:512],
                ALU.mult, ALU.mult), "sil")
            p.banks_done([m % 4, 4 + m % 4], "sil", ev)
            p.wait("sync", "sil", ev)
            p.dma("sync", hsh_scr.ap()[:, m * T + t0:m * T + t0 + 512],
                  HEb[0][:, 0:512], "hst")

    if stop_after == "shared":
        return _trunc_finish(p, yout[:, 0:16])
    p.barrier()
    ah.release()
    af.release()
    ah.mark()
    W2B = [ah.take(4 * 1024), ah.take(4 * 1024)]
    WS2B = ah.take(NCH * 1024)
    HB = [ah.take(4 * 512), ah.take(4 * 512)]
    HSB = ah.take(8 * 512)
    ZSB = ah.take(512)
    if skip_shared_z:
        p.ins("vector", lambda e: e.memset(ZSB[:, 0:512], 0.0), "ev")
    wsc = wload_chunks(p, WS2B, NCH, 1024, ws2)
    guard = {}
    seq = 0
    shared_mm_guard = 0
    for tg in range(4):
        t0 = tg * 512
        p.banks_req(range(8))
        for e_idx in range(E):
            buf = 0 if serial_p2 else seq % 2
            gk = seq - 1 if serial_p2 else seq - 2
            wc = wload_chunks(p, W2B[buf], 4, 1024, w2[e_idx],
                              "mm", guard.get(gk, 0))
            p.wait("sync", "mm", guard.get(gk, 0))
            hc = None
            for m in range(4):
                hc = p.dma("sync", HB[buf][:, m * 512:(m + 1) * 512],
                           h_scr.ap()[e_idx, :, m * T + t0:m * T + t0 + 512],
                           "hld")
            p.wait("tensor", "wld", wc)
            p.wait("tensor", "hld", hc)
            for mo in range(8):
                for c in range(4):
                    guard[seq] = p.ins("tensor", lambda e, mo=mo, c=c, buf=buf,
                          first=(e_idx == 0 and c == 0): e.matmul(
                        pb[:, mo * 512:mo * 512 + 512],
                        W2B[buf][:, c * 1024 + mo * 128:
                                 c * 1024 + mo * 128 + 128],
                        HB[buf][:, c * 512:(c + 1) * 512],
                        start=first, stop=False),
                        "mm" if (mo == 7 and c == 3) else None)
            if debug and tg == 0 and e_idx == 0:
                p.wait("vector", "mm", guard[seq])
                p.ins("vector", lambda e: e.tensor_copy(CAP2[:, 0:512],
                                                        pb[:, 3072:3584]))
            if debug and tg == 0 and e_idx == E - 1:
                p.wait("vector", "mm", guard[seq])
                p.ins("vector", lambda e: e.tensor_copy(CAP2[:, 512:1024],
                                                        pb[:, 3072:3584]))
            seq += 1
        p.wait("sync", "mm", shared_mm_guard)
        hc = None
        for m in range(8):
            hc = p.dma("sync", HSB[:, m * 512:(m + 1) * 512],
                       hsh_scr.ap()[:, m * T + t0:m * T + t0 + 512], "hsld")
        p.wait("tensor", "wld", wsc)
        p.wait("tensor", "hsld", hc)
        nsh = 1 if skip_shared_z else NCH
        for mo in range(8):
            for c in range(nsh):
                mmc = p.ins("tensor", lambda e, mo=mo, c=c, z=skip_shared_z:
                            e.matmul(
                    pb[:, mo * 512:mo * 512 + 512],
                    WS2B[:, c * 1024 + mo * 128:c * 1024 + mo * 128 + 128],
                    HSB[:, c * 512:(c + 1) * 512] if not z else
                    ZSB[:, 0:512],
                    start=False, stop=(c == nsh - 1)),
                    "mm" if (c == nsh - 1) else None)
        shared_mm_guard = mmc
        p.wait("vector", "mm", mmc)
        if debug and tg == 0:
            p.ins("vector", lambda e: e.tensor_copy(CAP2[:, 1024:1536],
                                                    pb[:, 3072:3584]))
        for mo in range(NCH):
            ev = p.ins("vector", lambda e, mo=mo, t0=t0: e.tensor_add(
                X[:, mo * T + t0:mo * T + t0 + 512],
                pb[:, mo * 512:mo * 512 + 512],
                X[:, mo * T + t0:mo * T + t0 + 512]), "ev")
        p.banks_done(range(8), "ev", ev)

    p.wait("sync", "ev", p.cnt("ev"))
    if debug:
        d_xn = p.outp("d_xn", [128, NCH * T], BF16)
        p.dma("sync", d_xn, XN[:], "st")
        d_lg = p.outp("d_lg", [16, T], F32)
        p.dma("sync", d_lg, LGS[0:16, 0:T], "st")
        d_cwf = p.outp("d_cwf", [16, T], F32)
        p.dma("sync", d_cwf, CWFS[0:16, 0:T], "st")
        d_cap = p.outp("d_cap", [128, 64], F32)
        p.dma("sync", d_cap, CAP[:], "st")
        d_cap2 = p.outp("d_cap2", [128, T], F32)
        p.dma("sync", d_cap2, CAP2[:], "st")
    stc = p.dma("sync", yout, X[:], "st")
    p.wait("sync", "st", stc)
    return p




def l1ffn_inputs(inputs):
    d = {
        "wg": np.ascontiguousarray(np.asarray(inputs["L1_ffn_Wg"]).T),
        "ws1": np.ascontiguousarray(np.asarray(inputs["L1_ffn_Ws1"])),
        "ws2": np.ascontiguousarray(np.asarray(inputs["L1_ffn_Ws2"])),
        "w1": np.ascontiguousarray(np.asarray(inputs["L1_ffn_W1"])),
        "w2": np.ascontiguousarray(np.asarray(inputs["L1_ffn_W2"])),
    }
    d.update(host_aux())
    d["sel16"] = host_sel16()
    return d


_PROGS = {}


def get_prog(name, builder):
    if name not in _PROGS:
        p = builder()
        _PROGS[name] = p.finalize()
    return _PROGS[name]


def run_l1ffn(x, inputs):
    nc = get_prog("l1ffn", build_l1_ffn)
    shared = l1ffn_inputs(inputs)
    in_maps = []
    for c in range(NCORES):
        b, h = c // 2, c % 2
        m = dict(shared)
        m["xfm"] = fm_pack(x[b, h * T:(h + 1) * T])
        in_maps.append(m)
    res = run_spmd_cached(nc, in_maps, core_ids=list(range(NCORES)))
    out = np.empty_like(x)
    for c in range(NCORES):
        b, h = c // 2, c % 2
        out[b, h * T:(h + 1) * T] = fm_unpack(res.results[c]["yfm"], T)
    return out





def build_l1f_r():
    p = Prog("l1f_r")
    E = 16
    xin = p.inp("xfm", [128, NCH * T])
    wg = p.inp("wg", [D, E])
    xnout = p.outp("xn", [128, NCH * T], BF16)
    cwout = p.outp("cwfs", [16, T])
    aux = emit_aux(p)
    X = p.sbuf("X", [128, NCH * T], F32)
    XN = p.sbuf("XN", [128, NCH * T], BF16)
    pb = p.psum("pb", [128, 4096], F32)
    ah = Arena(p, "AH", 2048, BF16)
    af = Arena(p, "AF", 15000, F32)
    p.dma("sync", X[:], xin, "ld")
    for eng in ("tensor", "scalar", "vector"):
        p.wait(eng, "ld", p.cnt("ld"))

    def xap(c, t0, t1):
        return X[:, c * T + t0:c * T + t1]

    emit_ln(p, aux, xap, lambda c: XN[:, c * T:(c + 1) * T], T, pb, af, "o")

    wgs = ah.take(NCH * E)
    wc = wload_chunks(p, wgs, NCH, E, wg)
    LGS = af.take(T)
    p.wait("tensor", "wld", wc)
    p.wait("tensor", "ev", p.cnt("ev"))
    p.banks_req(range(4))
    for c in range(NCH):
        for j in range(4):
            mmc = p.ins("tensor", lambda e, c=c, j=j: e.matmul(
                pb[0:16, j * 512:(j + 1) * 512],
                wgs[:, c * E:(c + 1) * E],
                XN[:, c * T + j * 512:c * T + (j + 1) * 512],
                start=(c == 0), stop=(c == NCH - 1)),
                "mm" if (c == NCH - 1) else None)
    p.wait("vector", "mm", mmc)
    ev = p.ins("vector", lambda e: e.tensor_copy(LGS[0:16, 0:T],
                                                 pb[0:16, 0:T]), "ev")
    p.banks_done(range(4), "ev", ev)

    CWFS = af.take(T)
    PRT = p.sbuf("PRT", [128, 16], F32)
    PM = p.sbuf("PM", [128, 16], F32)
    I1 = p.sbuf("I1", [128, 16], F32)
    I2 = p.sbuf("I2", [128, 16], F32)
    CWT = p.sbuf("CWT", [128, 16], F32)
    PT0 = p.sbuf("PT0", [128, 16], F32)
    mxc = p.sbuf("mxc", [128, 1], F32)
    smc = p.sbuf("smc", [128, 1], F32)
    p.wait("tensor", "ev", ev)
    for t in range(16):
        bank = 2048 + (t % 2) * 512
        bid = 4 + (t % 2)
        p.banks_req([bid])
        mmc = p.ins("tensor", lambda e, t=t, bank=bank: e.transpose(
            pb[:, bank:bank + 16], LGS[0:16, t * 128:(t + 1) * 128],
            aux["eye_f"][0:16, 0:16]), "mm")
        p.wait("vector", "mm", mmc)
        ev0 = p.ins("vector", lambda e, bank=bank: e.tensor_copy(
            PT0[:], pb[:, bank:bank + 16]), "ev")
        p.banks_done([bid], "ev", ev0)
        p.ins("vector", lambda e: e.tensor_reduce(
            mxc[:], PT0[:], AX.X, ALU.max))
        evn = p.ins("vector", lambda e: e.tensor_scalar_mul(
            mxc[:], mxc[:], -1.0), "ev")
        p.wait("scalar", "ev", evn)
        ac = p.ins("scalar", lambda e: e.activation(
            PRT[:], PT0[:], AF.Exp, bias=mxc[:, 0:1]), "ac")
        p.wait("vector", "ac", ac)
        p.ins("vector", lambda e: e.tensor_reduce(
            smc[:], PRT[:], AX.X, ALU.add))
        p.ins("vector", lambda e: e.reciprocal(smc[:], smc[:]))
        p.ins("vector", lambda e: e.tensor_scalar_mul(PRT[:], PRT[:],
                                                      smc[:, 0:1]))
        p.ins("vector", lambda e: e.tensor_reduce(
            mxc[:], PRT[:], AX.X, ALU.max))
        p.ins("vector", lambda e: e.tensor_scalar_mul(mxc[:], mxc[:], -1.0))
        p.ins("vector", lambda e: e.tensor_scalar(
            I1[:], PRT[:], mxc[:, 0:1], 0.0, ALU.add, ALU.is_ge))
        p.ins("vector", lambda e: e.tensor_scalar(
            PM[:], I1[:], -1.0, 1.0, ALU.mult, ALU.add))
        p.ins("vector", lambda e: e.tensor_mul(PM[:], PRT[:], PM[:]))
        p.ins("vector", lambda e: e.tensor_reduce(
            mxc[:], PM[:], AX.X, ALU.max))
        p.ins("vector", lambda e: e.tensor_scalar_mul(mxc[:], mxc[:], -1.0))
        p.ins("vector", lambda e: e.tensor_scalar(
            I2[:], PM[:], mxc[:, 0:1], 0.0, ALU.add, ALU.is_ge))
        p.ins("vector", lambda e: e.tensor_add(I1[:], I1[:], I2[:]))
        ev = p.ins("vector", lambda e: e.tensor_mul(CWT[:], PRT[:], I1[:]),
                   "ev")
        p.wait("tensor", "ev", ev)
        bank2 = 2048 + (t % 2) * 512 + 128
        bid2 = 4 + (t % 2)
        p.banks_req([bid2])
        mmc = p.ins("tensor", lambda e, bank2=bank2: e.transpose(
            pb[0:16, bank2:bank2 + 128], CWT[:], aux["eye_f"][:]), "mm")
        p.wait("vector", "mm", mmc)
        ev = p.ins("vector", lambda e, t=t, bank2=bank2: e.tensor_copy(
            CWFS[0:16, t * 128:(t + 1) * 128],
            pb[0:16, bank2:bank2 + 128]), "ev")
        p.banks_done([bid2], "ev", ev)
    p.wait("sync", "ev", p.cnt("ev"))
    p.dma("sync", xnout, XN[:], "st")
    p.dma("sync", cwout, CWFS[0:16, 0:T], "st")
    p.wait("sync", "st", p.cnt("st"))
    return p


def build_l1f_he(npe=2):
    p = Prog("l1f_he")
    xn_in = p.inp("xn", [128, NCH * T], BF16)
    cw2 = p.inp("cw2", [1, npe * T])
    w1s = p.inp("w1s", [npe, D, 1024])
    he = p.outp("he", [npe, 128, 4 * T], BF16)
    aux = emit_aux(p)
    XN = p.sbuf("XN", [128, NCH * T], BF16)
    pb = p.psum("pb", [128, 4096], F32)
    ah = Arena(p, "AH", 2 * NCH * 1024 + 4 * 512 * 2, BF16)
    af = Arena(p, "AF", T + 3 * 512 + npe * T, F32)
    WB = [ah.take(NCH * 1024), ah.take(NCH * 1024)]
    HEb = [ah.take(4 * 512), ah.take(4 * 512)]
    CWR = af.take(npe * T)
    CWB = af.take(T)
    TF = af.take(512)
    SILF = af.take(512)
    p.dma("sync", XN[:], xn_in, "ld")
    p.dma("sync", CWR[0:1, 0:npe * T], cw2, "ld")
    for eng in ("tensor", "scalar", "vector"):
        p.wait(eng, "ld", p.cnt("ld"))

    def xnap(c, t0, t1):
        return XN[:, c * T + t0:c * T + t1]

    mm_after = {}
    hst_after = {}
    for e_idx in range(npe):
        buf = e_idx % 2
        wc = wload_chunks(p, WB[buf], NCH, 1024, w1s[e_idx],
                          "mm", mm_after.get(e_idx - 2, 0), f"wld{buf}")
        p.banks_req(range(4))
        for j in range(4):
            mmc = p.ins("tensor", lambda e, e_idx=e_idx, j=j: e.matmul(
                pb[:, j * 512:(j + 1) * 512], aux["onesr_f"][0:1, 0:128],
                CWR[0:1, e_idx * T + j * 512:e_idx * T + (j + 1) * 512],
                start=True, stop=True), "mm" if j == 3 else None)
        p.wait("vector", "mm", mmc)
        cwev = p.ins("vector", lambda e: e.tensor_copy(CWB[:, 0:T],
                                                       pb[:, 0:T]), "ev")
        p.banks_done(range(4), "ev", cwev)
        p.wait("tensor", f"wld{buf}", wc)
        for tg in range(4):
            t0 = tg * 512
            hbuf = HEb[tg % 2]
            p.wait("vector", "hst", hst_after.get(
                (e_idx, tg - 2), hst_after.get((e_idx - 1, tg + 2), 0)))
            for rnd in range(2):
                bank_ids = [rnd * 4 + i for i in range(4)]
                p.banks_req(bank_ids)
                for half in range(2):
                    for mi in range(2):
                        m = rnd * 2 + mi
                        och = half * 4 + m
                        bank = (rnd * 4 + half * 2 + mi) * 512
                        for c in range(NCH):
                            mmc = p.ins("tensor", lambda e, och=och, c=c,
                                        bank=bank, t0=t0, buf=buf: e.matmul(
                                pb[:, bank:bank + 512],
                                WB[buf][:, c * 1024 + och * 128:
                                        c * 1024 + och * 128 + 128],
                                xnap(c, t0, t0 + 512),
                                start=(c == 0), stop=(c == NCH - 1)),
                                "mm" if c == NCH - 1 else None)
                mmdone = mmc
                for mi in range(2):
                    m = rnd * 2 + mi
                    ubank = (rnd * 4 + mi) * 512
                    gbank = (rnd * 4 + 2 + mi) * 512
                    p.wait("scalar", "mm", mmdone)
                    p.wait("scalar", "sil", p.cnt("sil"))
                    ac = p.ins("scalar", lambda e, gbank=gbank: e.activation(
                        SILF[:, 0:512], pb[:, gbank:gbank + 512], AF.Exp,
                        scale=-1.0), "ac")
                    p.wait("vector", "ac", ac)
                    p.wait("vector", "mm", mmdone)
                    p.ins("vector", lambda e: e.tensor_scalar_add(
                        SILF[:, 0:512], SILF[:, 0:512], 1.0))
                    p.ins("vector", lambda e: e.reciprocal(
                        SILF[:, 0:512], SILF[:, 0:512]))
                    p.ins("vector", lambda e, gbank=gbank: e.tensor_mul(
                        SILF[:, 0:512], SILF[:, 0:512],
                        pb[:, gbank:gbank + 512]))
                    p.ins("vector", lambda e, ubank=ubank:
                          e.scalar_tensor_tensor(
                        TF[:, 0:512], pb[:, ubank:ubank + 512], 1.0,
                        SILF[:, 0:512], ALU.mult, ALU.mult), "sil")
                    ev = p.ins("vector", lambda e, m=m, hbuf=hbuf, t0=t0:
                               e.scalar_tensor_tensor(
                        hbuf[:, m * 512:(m + 1) * 512], TF[:, 0:512],
                        1.0, CWB[:, t0:t0 + 512], ALU.mult, ALU.mult), "ev")
                p.banks_done(bank_ids, "ev", ev)
            p.wait("sync", "ev", p.cnt("ev"))
            for m in range(4):
                hst_after[(e_idx, tg)] = p.dma(
                    "sync", he[e_idx, :, m * T + t0:m * T + t0 + 512],
                    hbuf[:, m * 512:(m + 1) * 512], "hst")
        mm_after[e_idx] = p.cnt("mm")
    p.wait("sync", "hst", p.cnt("hst"))
    return p


def build_l1f_s():
    p = Prog("l1f_s")
    xn_in = p.inp("xn", [128, NCH * T], BF16)
    ws1 = p.inp("ws1", [D, 2048])
    hsh = p.outp("hsh", [128, 8 * T], BF16)
    aux = emit_aux(p)
    XN = p.sbuf("XN", [128, NCH * T], BF16)
    pb = p.psum("pb", [128, 4096], F32)
    ah = Arena(p, "AH", 2 * NCH * 1024 + 512, BF16)
    af = Arena(p, "AF", 2 * 512, F32)
    WB = [ah.take(NCH * 1024), ah.take(NCH * 1024)]
    HE0 = ah.take(512)
    TF = af.take(512)
    SILF = af.take(512)
    p.dma("sync", XN[:], xn_in, "ld")
    for eng in ("tensor", "scalar", "vector"):
        p.wait(eng, "ld", p.cnt("ld"))

    def xnap(c, t0, t1):
        return XN[:, c * T + t0:c * T + t1]

    wsh = [WB[0], WB[1]]

    def wsh_ap(c, lo, n):
        b = c // 4
        o = (c % 4) * 2048
        return wsh[b][:, o + lo:o + lo + n]
    clast = None
    for k in range(NCH):
        clast = p.dma("gpsimd", wsh_ap(k, 0, 2048),
                      ws1[k * 128:(k + 1) * 128, :], "wld")
    p.wait("tensor", "wld", clast)
    for tg in range(4):
        t0 = tg * 512
        for m in range(8):
            ub = (m % 4) * 512
            gb = (4 + m % 4) * 512
            p.banks_req([m % 4, 4 + m % 4])
            for c in range(NCH):
                p.ins("tensor", lambda e, m=m, c=c, ub=ub, t0=t0: e.matmul(
                    pb[:, ub:ub + 512], wsh_ap(c, m * 128, 128),
                    xnap(c, t0, t0 + 512),
                    start=(c == 0), stop=(c == NCH - 1)))
            for c in range(NCH):
                mmc = p.ins("tensor", lambda e, m=m, c=c, gb=gb, t0=t0:
                            e.matmul(
                    pb[:, gb:gb + 512], wsh_ap(c, (8 + m) * 128, 128),
                    xnap(c, t0, t0 + 512),
                    start=(c == 0), stop=(c == NCH - 1)),
                    "mm" if c == NCH - 1 else None)
            p.wait("scalar", "mm", mmc)
            p.wait("scalar", "sil", p.cnt("sil"))
            ac = p.ins("scalar", lambda e, gb=gb: e.activation(
                SILF[:, 0:512], pb[:, gb:gb + 512], AF.Exp, scale=-1.0), "ac")
            p.wait("vector", "ac", ac)
            p.wait("vector", "mm", mmc)
            p.wait("vector", "hst", p.cnt("hst"))
            p.ins("vector", lambda e: e.tensor_scalar_add(
                SILF[:, 0:512], SILF[:, 0:512], 1.0))
            p.ins("vector", lambda e: e.reciprocal(
                SILF[:, 0:512], SILF[:, 0:512]))
            p.ins("vector", lambda e, gb=gb: e.tensor_mul(
                SILF[:, 0:512], SILF[:, 0:512], pb[:, gb:gb + 512]))
            ev = p.ins("vector", lambda e, ub=ub: e.scalar_tensor_tensor(
                HE0[:, 0:512], pb[:, ub:ub + 512], 1.0, SILF[:, 0:512],
                ALU.mult, ALU.mult), "sil")
            p.banks_done([m % 4, 4 + m % 4], "sil", ev)
            p.wait("sync", "sil", ev)
            p.dma("sync", hsh[:, m * T + t0:m * T + t0 + 512],
                  HE0[:, 0:512], "hst")
    p.wait("sync", "hst", p.cnt("hst"))
    return p


def build_l1f_p2(ne=4, final=False):
    p = Prog("l1f_p2f" if final else "l1f_p2")
    he_in = p.inp("he", [ne, 128, 4 * T], BF16)
    w2s = p.inp("w2s", [ne, 512, D])
    part = p.inp("part", [128, NCH * T])
    if final:
        hsh_in = p.inp("hsh", [128, 8 * T], BF16)
        ws2 = p.inp("ws2", [1024, D])
    pout = p.outp("pout", [128, NCH * T])
    aux = emit_aux(p)
    PX = p.sbuf("PX", [128, NCH * T], F32)
    pb = p.psum("pb", [128, 4096], F32)
    ah = Arena(p, "AH", 2 * 4 * 1024 + 2 * 4 * 512 + NCH * 1024 + 8 * 512,
               BF16)
    W2B = [ah.take(4 * 1024), ah.take(4 * 1024)]
    HB = [ah.take(4 * 512), ah.take(4 * 512)]
    WS2B = ah.take(NCH * 1024)
    HSB = ah.take(8 * 512)
    p.dma("sync", PX[:], part, "ld")
    for eng in ("tensor", "vector"):
        p.wait(eng, "ld", p.cnt("ld"))
    wsc = None
    if final:
        wsc = wload_chunks(p, WS2B, NCH, 1024, ws2)
    guard = {}
    seq = 0
    for tg in range(4):
        t0 = tg * 512
        p.banks_req(range(8))
        for e_idx in range(ne):
            buf = seq % 2
            gk = seq - 2
            wc = wload_chunks(p, W2B[buf], 4, 1024, w2s[e_idx],
                              "mm", guard.get(gk, 0), f"wld{buf}")
            p.wait("sync", "mm", guard.get(gk, 0))
            hc = None
            for m in range(4):
                hc = p.dma("sync", HB[buf][:, m * 512:(m + 1) * 512],
                           he_in[e_idx, :, m * T + t0:m * T + t0 + 512],
                           "hld")
            p.wait("tensor", f"wld{buf}", wc)
            p.wait("tensor", "hld", hc)
            for mo in range(8):
                for c in range(4):
                    guard[seq] = p.ins("tensor", lambda e, mo=mo, c=c,
                          buf=buf, first=(e_idx == 0 and c == 0): e.matmul(
                        pb[:, mo * 512:mo * 512 + 512],
                        W2B[buf][:, c * 1024 + mo * 128:
                                 c * 1024 + mo * 128 + 128],
                        HB[buf][:, c * 512:(c + 1) * 512],
                        start=first, stop=(not final and e_idx == ne - 1
                                           and c == 3)),
                        "mm" if (mo == 7 and c == 3) else None)
            seq += 1
        if final:
            p.wait("sync", "mm", guard[seq - 1] if tg == 0 else shg)
            hc = None
            for m in range(8):
                hc = p.dma("sync", HSB[:, m * 512:(m + 1) * 512],
                           hsh_in[:, m * T + t0:m * T + t0 + 512], "hsld")
            p.wait("tensor", "wld", wsc)
            p.wait("tensor", "hsld", hc)
            for mo in range(8):
                for c in range(NCH):
                    mmc = p.ins("tensor", lambda e, mo=mo, c=c: e.matmul(
                        pb[:, mo * 512:mo * 512 + 512],
                        WS2B[:, c * 1024 + mo * 128:c * 1024 + mo * 128 + 128],
                        HSB[:, c * 512:(c + 1) * 512],
                        start=False, stop=(c == NCH - 1)),
                        "mm" if (c == NCH - 1) else None)
            shg = mmc
        p.wait("vector", "mm", p.cnt("mm"))
        for mo in range(NCH):
            ev = p.ins("vector", lambda e, mo=mo, t0=t0: e.tensor_add(
                PX[:, mo * T + t0:mo * T + t0 + 512],
                pb[:, mo * 512:mo * 512 + 512],
                PX[:, mo * T + t0:mo * T + t0 + 512]), "ev")
        p.banks_done(range(8), "ev", ev)
    p.wait("sync", "ev", p.cnt("ev"))
    stc = p.dma("sync", pout, PX[:], "st")
    p.wait("sync", "st", stc)
    return p


def run_l1ffn_split(x, inputs):
    wg = np.ascontiguousarray(np.asarray(inputs["L1_ffn_Wg"],
                                         np.float32).T)
    W1 = np.asarray(inputs["L1_ffn_W1"], np.float32)
    W2 = np.asarray(inputs["L1_ffn_W2"], np.float32)
    Ws1 = np.asarray(inputs["L1_ffn_Ws1"], np.float32)
    Ws2 = np.asarray(inputs["L1_ffn_Ws2"], np.float32)
    ncr = get_prog("l1f_r", build_l1f_r)
    nche = get_prog("l1f_he", build_l1f_he)
    ncs = get_prog("l1f_s", build_l1f_s)
    ncp = get_prog("l1f_p2", lambda: build_l1f_p2(4, False))
    ncpf = get_prog("l1f_p2f", lambda: build_l1f_p2(4, True))
    xpk = [fm_pack(x[c // 2, (c % 2) * T:(c % 2 + 1) * T])
           for c in range(NCORES)]
    res = run_spmd_cached(ncr, [
        {"xfm": xpk[c], "wg": wg, **host_aux()} for c in range(NCORES)],
        core_ids=list(range(NCORES)))
    xn = [res.results[c]["xn"] for c in range(NCORES)]
    cwfs = [res.results[c]["cwfs"] for c in range(NCORES)]
    he = [[] for _ in range(NCORES)]
    for pair in range(8):
        e0 = pair * 2
        res = run_spmd_cached(nche, [
            {"xn": xn[c],
             "cw2": np.ascontiguousarray(cwfs[c][e0:e0 + 2]).reshape(1, -1),
             "w1s": np.ascontiguousarray(W1[e0:e0 + 2]), **host_aux()}
            for c in range(NCORES)],
            core_ids=list(range(NCORES)))
        for c in range(NCORES):
            he[c].append(res.results[c]["he"])
    res = run_spmd_cached(ncs, [
        {"xn": xn[c], "ws1": Ws1, **host_aux()} for c in range(NCORES)],
        core_ids=list(range(NCORES)))
    hsh = [res.results[c]["hsh"] for c in range(NCORES)]
    part = xpk
    for grp in range(4):
        e0 = grp * 4
        hegrp = [np.concatenate([he[c][grp * 2], he[c][grp * 2 + 1]], axis=0)
                 for c in range(NCORES)]
        nc = ncpf if grp == 3 else ncp
        maps = []
        for c in range(NCORES):
            m = {"he": hegrp[c],
                 "w2s": np.ascontiguousarray(W2[e0:e0 + 4]),
                 "part": part[c], **host_aux()}
            if grp == 3:
                m["hsh"] = hsh[c]
                m["ws2"] = Ws2
            maps.append(m)
        res = run_spmd_cached(nc, maps, core_ids=list(range(NCORES)))
        part = [res.results[c]["pout"] for c in range(NCORES)]
    out = np.empty_like(x)
    for c in range(NCORES):
        b, h = c // 2, c % 2
        out[b, h * T:(h + 1) * T] = fm_unpack(part[c], T)
    return out




def build_attn_a_A():
    p = Prog("attn_a_A")
    TF_ = 4096

    xfull = p.inp("xfull", [128, NCH * TF_])
    wq = p.inp("wq", [NCH, 128, NCH * 128])
    bq = p.inp("bq", [128, NCH])
    wkv = p.inp("wkv", [D, 128])
    bkvk = p.inp("bkvk", [64, 1])
    bkvv = p.inp("bkvv", [1, 64])
    hsel = p.inp("hsel", [128, NCH * 16], BF16)
    aout = p.outp("aout", [128, 4 * TF_], BF16)

    aux = emit_aux(p)
    XN = p.sbuf("XN", [128, NCH * TF_], BF16)
    pb = p.psum("pb", [128, 4096], F32)
    ah = Arena(p, "AH", 44800, BF16)
    af = Arena(p, "AF", 13312, F32)

    af.mark()
    XS = af.take(NCH * 1024)
    for q4 in range(4):
        lc = None
        for c in range(NCH):
            lc = p.dma("sync", XS[:, c * 1024:(c + 1) * 1024],
                       xfull[:, c * TF_ + q4 * 1024:c * TF_ + (q4 + 1) * 1024],
                       "ld")
        for eng in ("tensor", "scalar", "vector"):
            p.wait(eng, "ld", lc)
        emit_ln(p, aux, lambda c, t0, t1, q4=q4: XS[:, c * 1024 + t0:c * 1024 + t1],
                lambda c, q4=q4: XN[:, c * TF_ + q4 * 1024:c * TF_ + (q4 + 1) * 1024],
                1024, pb, af, f"s{q4}")
        p.wait("sync", "ev", p.cnt("ev"))
    af.release()

    KH = ah.take(TF_)
    VA = ah.take(32 * 65)
    Q = ah.take(NCH * T)
    AO = ah.take(4 * TF_)
    WQB = ah.take(2 * 1024)
    WKVB = ah.take(NCH * 128)
    PT = ah.take(2 * 256)
    SQH = ah.take(2048)
    af.mark()
    QSS = af.take(2048)
    RR = af.take(2048)
    NEGMC = p.sbuf("negmc", [128, 1], F32)
    bq8 = p.sbuf("bq8", [128, NCH], F32)
    bkvks = p.sbuf("bkvks", [64, 1], F32)

    wc2 = wload_chunks(p, WKVB, NCH, 128, wkv)
    lc = p.dma("sync", bq8[:], bq, "ld2")
    lc = p.dma("sync", bkvks[:], bkvk, "ld2")
    bkvvs = p.sbuf("bkvvs", [1, 64], BF16)
    lc = p.dma("gpsimd", bkvvs[:], bkvv, "wld")
    hsels = p.sbuf("hsels", [128, NCH * 16], BF16)
    lc = p.dma("sync", hsels[:], hsel, "ld2")
    p.wait("vector", "ld2", p.cnt("ld2"))
    ev = p.ins("vector", lambda e: e.tensor_scalar_mul(bq8[:], bq8[:], 0.125),
               "ev")

    ev = p.ins("vector", lambda e: e.memset(VA[:, 64:32 * 65:65], 1.0), "ev")

    p.wait("tensor", "wld", p.cnt("wld"))
    p.wait("tensor", "ld", p.cnt("ld"))
    p.wait("tensor", "ld2", p.cnt("ld2"))
    p.wait("tensor", "ev", p.cnt("ev"))

    for nt in range(8):
        bk = nt % 2
        p.banks_req([bk])
        for c in range(NCH):
            mmc = p.ins("tensor", lambda e, nt=nt, c=c, bk=bk: e.matmul(
                pb[0:64, bk * 512:bk * 512 + 512],
                WKVB[:, c * 128:c * 128 + 64],
                XN[:, c * TF_ + nt * 512:c * TF_ + (nt + 1) * 512],
                start=(c == 0), stop=(c == NCH - 1)),
                "mm" if c == NCH - 1 else None)
        p.wait("vector", "mm", mmc)
        ac = p.ins("vector", lambda e, nt=nt, bk=bk: e.tensor_scalar_add(
            KH[0:64, nt * 512:(nt + 1) * 512],
            pb[0:64, bk * 512:bk * 512 + 512], bkvks[:, 0:1]), "ev")
        ac = p.ins("vector", lambda e, nt=nt, bk=bk: e.tensor_scalar_add(
            KH[64:128, nt * 512:(nt + 1) * 512],
            pb[0:64, bk * 512:bk * 512 + 512], bkvks[:, 0:1]), "ev")
        p.banks_done([bk], "ev", ac)

    for tt in range(32):
        bk = 2 + tt % 2
        p.banks_req([bk])
        for c in range(NCH):
            p.ins("tensor", lambda e, tt=tt, c=c, bk=bk: e.matmul(
                pb[:, bk * 512:bk * 512 + 64],
                XN[:, c * TF_ + tt * 128:c * TF_ + (tt + 1) * 128],
                WKVB[:, c * 128 + 64:c * 128 + 128],
                start=(c == 0), stop=False))
        mmc = p.ins("tensor", lambda e, tt=tt, bk=bk: e.matmul(
            pb[:, bk * 512:bk * 512 + 64], aux["onesr_h"][:],
            bkvvs[:], start=False, stop=True), "mm")
        p.wait("vector", "mm", mmc)
        ev = p.ins("vector", lambda e, tt=tt, bk=bk: e.tensor_copy(
            VA[:, tt * 65:tt * 65 + 64], pb[:, bk * 512:bk * 512 + 64]), "ev")
        p.banks_done([bk], "ev", ev)

    def xnown(c, nt):
        return XN[:, c * TF_ + 2048 + nt * 512:c * TF_ + 2048 + (nt + 1) * 512]

    qmm = {}
    for m in range(NCH):
        wbuf = m % 2
        if m >= 2:
            p.wait("gpsimd", "mm", qmm[m - 2])
        wc = p.dma("gpsimd", WQB[:, wbuf * 1024:(wbuf + 1) * 1024],
                   wq[m], f"wq{wbuf}")
        p.wait("tensor", f"wq{wbuf}", wc)
        for nt in range(4):
            bk = 4 + (m * 4 + nt) % 2
            p.banks_req([bk])
            for c in range(NCH):
                mmc = p.ins("tensor", lambda e, m=m, nt=nt, c=c, bk=bk,
                            wbuf=wbuf: e.matmul(
                    pb[:, bk * 512:bk * 512 + 512],
                    WQB[:, wbuf * 1024 + c * 128:wbuf * 1024 + c * 128 + 128],
                    xnown(c, nt),
                    start=(c == 0), stop=(c == NCH - 1)),
                    "mm" if c == NCH - 1 else None)
            p.wait("vector", "mm", mmc)
            ac = p.ins("vector", lambda e, m=m, nt=nt, bk=bk: e.tensor_scalar(
                Q[:, m * T + nt * 512:m * T + (nt + 1) * 512],
                pb[:, bk * 512:bk * 512 + 512],
                0.125, bq8[:, m:m + 1], ALU.mult, ALU.add), "ev")
            p.banks_done([bk], "ev", ac)
        qmm[m] = p.cnt("mm")

    p.wait("scalar", "ev", p.cnt("ev"))
    p.banks_req([4, 5, 6, 7])
    for c in range(NCH):
        ac = p.ins("scalar", lambda e, c=c: e.activation(
            SQH[:, 0:2048], Q[:, c * T:(c + 1) * T], AF.Square), "ac")
        p.wait("tensor", "ac", ac)
        for nt in range(4):
            mmc = p.ins("tensor", lambda e, c=c, nt=nt: e.matmul(
                pb[0:16, 2048 + nt * 512:2048 + nt * 512 + 512],
                hsels[:, c * 16:(c + 1) * 16],
                SQH[:, nt * 512:(nt + 1) * 512],
                start=(c == 0), stop=(c == NCH - 1)),
                "mm" if nt == 3 else None)
        p.wait("scalar", "mm", mmc)
    p.wait("vector", "mm", p.cnt("mm"))
    ev = p.ins("vector", lambda e: e.tensor_copy(
        QSS[0:16, 0:2048], pb[0:16, 2048:4096]), "ev")
    p.banks_done([4, 5, 6, 7], "ev", ev)
    p.ins("vector", lambda e: e.tensor_reduce(
        RR[0:16, 0:1], QSS[0:16, 0:2048], AX.X, ALU.max))
    ev = p.ins("vector", lambda e: e.engine_nop(), "ev")
    p.wait("tensor", "ev", ev)
    p.banks_req([6])
    mmc = p.ins("tensor", lambda e: e.transpose(
        pb[0:1, 3072:3072 + 16], RR[0:16, 0:1], aux["eye_f"][0:16, 0:16]),
        "mm")
    p.wait("vector", "mm", mmc)
    ev = p.ins("vector", lambda e: e.tensor_copy(RR[0:1, 16:32],
                                                 pb[0:1, 3072:3088]), "ev")
    p.banks_done([6], "ev", ev)
    p.ins("vector", lambda e: e.tensor_reduce(
        RR[0:1, 32:33], RR[0:1, 16:32], AX.X, ALU.max))
    for hh in range(2):
        p.wait("scalar", "ev", p.cnt("ev"))
        ac = p.ins("scalar", lambda e, hh=hh: e.activation(
            SQH[0:64, 0:2048], KH[0:64, hh * 2048:(hh + 1) * 2048],
            AF.Square), "ac")
        p.wait("tensor", "ac", ac)
        p.banks_req([4, 5, 6, 7])
        for nt in range(4):
            mmc = p.ins("tensor", lambda e, nt=nt: e.matmul(
                pb[0:1, 2048 + nt * 512:2048 + nt * 512 + 512],
                aux["onesc_h"][0:64, 0:1],
                SQH[0:64, nt * 512:(nt + 1) * 512],
                start=True, stop=True), "mm")
        p.wait("vector", "mm", mmc)
        ev = p.ins("vector", lambda e: e.tensor_copy(
            RR[32:33, 0:2048], pb[0:1, 2048:4096]), "ev")
        p.banks_done([4, 5, 6, 7], "ev", ev)
        p.ins("vector", lambda e, hh=hh: e.tensor_reduce(
            RR[0:1, 40 + hh:41 + hh], RR[32:33, 0:2048], AX.X, ALU.max))
        ev = p.ins("vector", lambda e: e.engine_nop(), "ev")
    p.ins("vector", lambda e: e.tensor_max(
        RR[0:1, 33:34], RR[0:1, 40:41], RR[0:1, 41:42]))
    p.ins("vector", lambda e: e.tensor_mul(
        RR[0:1, 34:35], RR[0:1, 32:33], RR[0:1, 33:34]))
    ev = p.ins("vector", lambda e: e.engine_nop(), "ev")
    p.wait("scalar", "ev", ev)
    ac = p.ins("scalar", lambda e: e.activation(
        RR[0:1, 34:35], RR[0:1, 34:35], AF.Sqrt), "ac")
    p.wait("vector", "ac", ac)
    ev = p.ins("vector", lambda e: e.tensor_scalar_mul(
        RR[0:1, 34:35], RR[0:1, 34:35], -1.0), "ev")
    p.wait("tensor", "ev", ev)
    p.banks_req([6])
    mmc = p.ins("tensor", lambda e: e.matmul(
        pb[:, 3072:3073], aux["onesr_f"][:], RR[0:1, 34:35],
        start=True, stop=True), "mm")
    p.wait("vector", "mm", mmc)
    ev = p.ins("vector", lambda e: e.tensor_copy(NEGMC[:], pb[:, 3072:3073]),
               "ev")
    p.banks_done([6], "ev", ev)

    p.wait("scalar", "ev", p.cnt("ev"))
    p.wait("tensor", "ev", p.cnt("ev"))
    TMPA = QSS
    for j in range(8):
        for s in range(16):
            base = (s % 2) * 64
            qap = Q[base:base + 64,
                    (s // 2) * T + j * 256:(s // 2) * T + (j + 1) * 256]
            pv0 = p.cnt("pv")
            for kc in range(32):
                slot = kc % 4
                bk = 4 + slot // 2
                col = bk * 512 + (slot % 2) * 256
                p.banks_req([bk])
                mmc = p.ins("tensor", lambda e, kc=kc, col=col, base=base,
                            qap=qap: e.matmul(
                    pb[:, col:col + 256],
                    KH[base:base + 64, kc * 128:(kc + 1) * 128],
                    qap, start=True, stop=True), "mm")
                p.wait("scalar", "mm", mmc)
                if kc >= 2:
                    p.wait("scalar", "pv", pv0 + kc - 1)
                ac = p.ins("scalar", lambda e, kc=kc, col=col: e.activation(
                    PT[:, (kc % 2) * 256:(kc % 2) * 256 + 256],
                    pb[:, col:col + 256], AF.Exp, bias=NEGMC[:, 0:1]), "ac")
                p.banks_done([bk], "ac", ac)
                p.wait("tensor", "ac", ac)
                if kc == 0:
                    p.banks_req([6])
                p.ins("tensor", lambda e, kc=kc: e.matmul(
                    pb[0:65, 3072:3072 + 256],
                    VA[:, kc * 65:(kc + 1) * 65],
                    PT[:, (kc % 2) * 256:(kc % 2) * 256 + 256],
                    start=(kc == 0), stop=(kc == 31)), "pv")
            mmpv = p.cnt("pv")
            p.wait("vector", "pv", mmpv)
            p.ins("vector", lambda e: e.tensor_copy(
                RR[0:1, 64:320], pb[64:65, 3072:3072 + 256]))
            p.ins("vector", lambda e: e.reciprocal(
                RR[0:1, 64:320], RR[0:1, 64:320]))
            ev = p.ins("vector", lambda e: e.tensor_copy(
                TMPA[64:128, 0:256], pb[0:64, 3072:3072 + 256]), "ev")
            p.banks_done([6], "ev", ev)
            p.wait("tensor", "ev", ev)
            p.banks_req([7])
            mmc = p.ins("tensor", lambda e: e.matmul(
                pb[0:64, 3584:3584 + 256], aux["onesr_f"][0:1, 0:64],
                RR[0:1, 64:320], start=True, stop=True), "mm")
            p.wait("vector", "mm", mmc)
            och = j // 2
            obase = (j % 2) * 64
            ev = p.ins("vector", lambda e, och=och, obase=obase, s=s:
                       e.tensor_mul(
                AO[obase:obase + 64, och * TF_ + s:och * TF_ + TF_:16],
                TMPA[64:128, 0:256], pb[0:64, 3584:3584 + 256]), "ev")
            p.banks_done([7], "ev", ev)

    p.wait("sync", "ev", p.cnt("ev"))
    stc = p.dma("sync", aout, AO[:, 0:4 * TF_], "st")
    p.wait("sync", "st", stc)
    return p




def build_attn_a_B():
    p = Prog("attn_a_B")
    xin = p.inp("xfm", [128, NCH * T])
    attn = p.inp("attn", [128, NCH * T], BF16)
    wo = p.inp("wo", [NCH, 128, NCH * 128])
    bo = p.inp("bo", [128, NCH])
    yout = p.outp("yfm", [128, NCH * T])
    aux = emit_aux(p)
    X = p.sbuf("X", [128, NCH * T], F32)
    A = p.sbuf("A", [128, NCH * T], BF16)
    WB = p.sbuf("WB", [128, 2 * 1024], BF16)
    bos = p.sbuf("bos", [128, NCH], F32)
    pb = p.psum("pb", [128, 4096], F32)
    p.dma("sync", X[:], xin, "ld")
    p.dma("sync", A[:], attn, "ld")
    lc = p.dma("sync", bos[:], bo, "ld")
    for eng in ("tensor", "vector"):
        p.wait(eng, "ld", lc)
    qmm = {}
    for m in range(NCH):
        wbuf = m % 2
        if m >= 2:
            p.wait("gpsimd", "mm", qmm[m - 2])
        wc = p.dma("gpsimd", WB[:, wbuf * 1024:(wbuf + 1) * 1024], wo[m],
                   f"wld{wbuf}")
        p.wait("tensor", f"wld{wbuf}", wc)
        for nt in range(4):
            bk = (m * 4 + nt) % 4
            p.banks_req([bk])
            for c in range(NCH):
                mmc = p.ins("tensor", lambda e, m=m, nt=nt, c=c, bk=bk,
                            wbuf=wbuf: e.matmul(
                    pb[:, bk * 512:bk * 512 + 512],
                    WB[:, wbuf * 1024 + c * 128:wbuf * 1024 + c * 128 + 128],
                    A[:, c * T + nt * 512:c * T + (nt + 1) * 512],
                    start=(c == 0), stop=(c == NCH - 1)),
                    "mm" if c == NCH - 1 else None)
            p.wait("vector", "mm", mmc)
            p.ins("vector", lambda e, m=m, nt=nt, bk=bk: e.tensor_scalar_add(
                pb[:, bk * 512:bk * 512 + 512],
                pb[:, bk * 512:bk * 512 + 512], bos[:, m:m + 1]))
            ev = p.ins("vector", lambda e, m=m, nt=nt, bk=bk: e.tensor_add(
                X[:, m * T + nt * 512:m * T + (nt + 1) * 512],
                pb[:, bk * 512:bk * 512 + 512],
                X[:, m * T + nt * 512:m * T + (nt + 1) * 512]), "ev")
            p.banks_done([bk], "ev", ev)
        qmm[m] = p.cnt("mm")
    p.wait("sync", "ev", p.cnt("ev"))
    stc = p.dma("sync", yout, X[:], "st")
    p.wait("sync", "st", stc)
    return p


def attn_a_inputs(inputs):
    wqT = np.ascontiguousarray(np.asarray(inputs["L0_attn_a_Wq"]).T)
    wkvT = np.ascontiguousarray(np.asarray(inputs["L0_attn_a_Wkv"]).T)
    woT = np.ascontiguousarray(np.asarray(inputs["L0_attn_a_Wo"]).T)
    bq = np.asarray(inputs["L0_attn_a_bq"])
    bkv = np.asarray(inputs["L0_attn_a_bkv"])
    bo = np.asarray(inputs["L0_attn_a_bo"])
    hsel = np.zeros((128, NCH * 16), BF)
    for c in range(NCH):
        hsel[0:64, c * 16 + 2 * c] = 1
        hsel[64:128, c * 16 + 2 * c + 1] = 1
    dA = {
        "wq": pack_wio(wqT),
        "wkv": wkvT,
        "bq": col_pack(bq),
        "bkvk": np.ascontiguousarray(bkv[:64].reshape(64, 1)),
        "bkvv": np.ascontiguousarray(bkv[64:].reshape(1, 64)),
        "hsel": hsel,
    }
    dA.update(host_aux())
    dB = {
        "wo": pack_wio(woT),
        "bo": col_pack(bo),
    }
    dB.update(host_aux())
    return dA, dB


def run_attn_a(x, inputs):
    ncA = get_prog("attn_a_A", build_attn_a_A)
    ncB = get_prog("attn_a_B", build_attn_a_B)
    dA, dB = attn_a_inputs(inputs)
    in_maps = []
    for c in range(NCORES):
        b, h = c // 2, c % 2
        m = dict(dA)
        xrot = np.concatenate([x[b, (1 - h) * T:(2 - h) * T],
                               x[b, h * T:(h + 1) * T]], axis=0)
        m["xfull"] = fm_pack(xrot)
        in_maps.append(m)
    res = run_spmd_cached(ncA, in_maps, core_ids=list(range(NCORES)))
    in_maps = []
    for c in range(NCORES):
        b, h = c // 2, c % 2
        a0 = np.asarray(res.results[2 * b]["aout"]).reshape(128, 4, 4096)
        a1 = np.asarray(res.results[2 * b + 1]["aout"]).reshape(128, 4, 4096)
        attn = np.concatenate([a0[:, :, h * T:(h + 1) * T],
                               a1[:, :, h * T:(h + 1) * T]], axis=1)
        m = dict(dB)
        m["attn"] = np.ascontiguousarray(attn.reshape(128, NCH * T))
        m["xfm"] = fm_pack(x[b, h * T:(h + 1) * T])
        in_maps.append(m)
    res = run_spmd_cached(ncB, in_maps, core_ids=list(range(NCORES)))
    out = np.empty_like(x)
    for c in range(NCORES):
        b, h = c // 2, c % 2
        out[b, h * T:(h + 1) * T] = fm_unpack(res.results[c]["yfm"], T)
    return out





def build_attn_b(nheads=16):
    p = Prog("attn_b")
    TF_ = 4096
    NKC = 32

    xfull = p.inp("xfull", [128, NCH * TF_])
    xown = p.inp("xown", [128, NCH * T])
    wq = p.inp("wq", [NCH, 128, NCH * 128])
    wk = p.inp("wk", [16, 128, NCH * 64])
    wv = p.inp("wv", [16, 128, NCH * 64])
    wproj = p.inp("wproj", [NCH, 128, NCH * 128])
    hsel = p.inp("hsel", [128, NCH * 16], BF16)
    invtab = p.inp("invtab", [128, 128])
    strip = p.inp("strip", [128, 4 * 512], BF16)
    yout = p.outp("yfm", [128, NCH * T])

    aux = emit_aux(p)
    XN = p.sbuf("XN", [128, NCH * TF_], BF16)
    pb = p.psum("pb", [128, 4096], F32)
    ah = Arena(p, "AH", 49184, BF16)
    af = Arena(p, "AF", 8704, F32)

    af.mark()
    XS = af.take(NCH * 512)
    for q8 in range(8):
        lc = None
        for c in range(NCH):
            lc = p.dma("sync", XS[:, c * 512:(c + 1) * 512],
                       xfull[:, c * TF_ + q8 * 512:c * TF_ + (q8 + 1) * 512],
                       "ld")
        for eng in ("tensor", "scalar", "vector"):
            p.wait(eng, "ld", lc)
        emit_ln(p, aux, lambda c, t0, t1, q8=q8: XS[:, c * 512 + t0:c * 512 + t1],
                lambda c, q8=q8: XN[:, c * TF_ + q8 * 512:c * TF_ + (q8 + 1) * 512],
                512, pb, af, f"s{q8}")
        p.wait("sync", "ev", p.cnt("ev"))
    af.release()

    KH = ah.take(TF_)
    VA = ah.take(32 * 65)
    Q = ah.take(NCH * T)
    AO = ah.take(NCH * T)
    WB = ah.take(2 * 1024)
    WKVB = ah.take(2 * 512)
    PT = ah.take(2 * 512)
    SQH = ah.take(2048)
    STR = ah.take(4 * 512)
    af.mark()
    QSS = af.take(2048)
    RR = af.take(2048)
    X2 = af.take(0)
    NEGB = p.sbuf("negb", [128, 128], F32)
    NEGMC = p.sbuf("negmc", [128, 1], F32)
    invs = p.sbuf("invs", [128, 128], F32)
    hsels = p.sbuf("hsels", [128, NCH * 16], BF16)

    lc = p.dma("sync", hsels[:], hsel, "ld2")
    lc = p.dma("sync", invs[:], invtab, "ld2")
    lc = p.dma("gpsimd", STR[:, 0:4 * 512], strip, "wld")
    ev = p.ins("vector", lambda e: e.memset(VA[:, 64:32 * 65:65], 1.0), "ev")
    p.wait("tensor", "ld2", p.cnt("ld2"))
    p.wait("tensor", "ev", p.cnt("ev"))
    p.wait("tensor", "wld", p.cnt("wld"))
    p.wait("scalar", "ld2", p.cnt("ld2"))
    p.wait("vector", "ld2", p.cnt("ld2"))

    qmm = {}
    for m in range(NCH):
        wbuf = m % 2
        if m >= 2:
            p.wait("gpsimd", "mm", qmm[m - 2])
        wc = p.dma("gpsimd", WB[:, wbuf * 1024:(wbuf + 1) * 1024], wq[m],
                   f"wq{wbuf}")
        p.wait("tensor", f"wq{wbuf}", wc)
        for nt in range(4):
            bk = (m * 4 + nt) % 4
            p.banks_req([bk])
            for c in range(NCH):
                mmc = p.ins("tensor", lambda e, m=m, nt=nt, c=c, bk=bk,
                            wbuf=wbuf: e.matmul(
                    pb[:, bk * 512:bk * 512 + 512],
                    WB[:, wbuf * 1024 + c * 128:wbuf * 1024 + c * 128 + 128],
                    XN[:, c * TF_ + 2048 + nt * 512:
                       c * TF_ + 2048 + (nt + 1) * 512],
                    start=(c == 0), stop=(c == NCH - 1)),
                    "mm" if c == NCH - 1 else None)
            p.wait("vector", "mm", mmc)
            ev = p.ins("vector", lambda e, m=m, nt=nt, bk=bk:
                       e.tensor_scalar_mul(
                Q[:, m * T + nt * 512:m * T + (nt + 1) * 512],
                pb[:, bk * 512:bk * 512 + 512], 0.125), "ev")
            p.banks_done([bk], "ev", ev)
        qmm[m] = p.cnt("mm")

    p.wait("scalar", "ev", p.cnt("ev"))
    p.banks_req([4, 5, 6, 7])
    for c in range(NCH):
        ac = p.ins("scalar", lambda e, c=c: e.activation(
            SQH[:, 0:2048], Q[:, c * T:(c + 1) * T], AF.Square), "ac")
        p.wait("tensor", "ac", ac)
        for nt in range(4):
            mmc = p.ins("tensor", lambda e, c=c, nt=nt: e.matmul(
                pb[0:16, 2048 + nt * 512:2048 + nt * 512 + 512],
                hsels[:, c * 16:(c + 1) * 16],
                SQH[:, nt * 512:(nt + 1) * 512],
                start=(c == 0), stop=(c == NCH - 1)),
                "mm" if nt == 3 else None)
        p.wait("scalar", "mm", mmc)
    p.wait("vector", "mm", p.cnt("mm"))
    ev = p.ins("vector", lambda e: e.tensor_copy(
        QSS[0:16, 0:2048], pb[0:16, 2048:4096]), "ev")
    p.banks_done([4, 5, 6, 7], "ev", ev)
    p.ins("vector", lambda e: e.tensor_reduce(
        RR[0:16, 0:1], QSS[0:16, 0:2048], AX.X, ALU.max))
    ev = p.ins("vector", lambda e: e.engine_nop(), "ev")
    p.wait("tensor", "ev", ev)
    p.banks_req([6])
    mmc = p.ins("tensor", lambda e: e.transpose(
        pb[0:1, 3072:3072 + 16], RR[0:16, 0:1], aux["eye_f"][0:16, 0:16]),
        "mm")
    p.wait("vector", "mm", mmc)
    ev = p.ins("vector", lambda e: e.tensor_copy(RR[0:1, 0:16],
                                                 pb[0:1, 3072:3088]), "ev")
    p.banks_done([6], "ev", ev)

    if nheads < 16:
        p.ins("vector", lambda e: e.memset(AO[:, 0:NCH * T], 0.0), "ev")
    for h in range(nheads):
        base = (h % 2) * 64
        qch = h // 2
        kbuf = h % 2
        if h >= 2:
            p.wait("gpsimd", "mm", p.cnt("mm") - 8)
        wc = p.dma("gpsimd", WKVB[:, kbuf * 512:kbuf * 512 + 512], wk[h],
                   f"wkv{kbuf}")
        p.wait("tensor", f"wkv{kbuf}", wc)
        for nt in range(8):
            bk = nt % 2
            p.banks_req([bk])
            for c in range(NCH):
                mmc = p.ins("tensor", lambda e, nt=nt, c=c, bk=bk, kbuf=kbuf:
                            e.matmul(
                    pb[0:64, bk * 512:bk * 512 + 512],
                    WKVB[:, kbuf * 512 + c * 64:kbuf * 512 + (c + 1) * 64],
                    XN[:, c * TF_ + nt * 512:c * TF_ + (nt + 1) * 512],
                    start=(c == 0), stop=(c == NCH - 1)),
                    "mm" if c == NCH - 1 else None)
            p.wait("vector", "mm", mmc)
            p.ins("vector", lambda e, nt=nt, bk=bk: e.tensor_copy(
                KH[0:64, nt * 512:(nt + 1) * 512],
                pb[0:64, bk * 512:bk * 512 + 512]))
            ev = p.ins("vector", lambda e, nt=nt, bk=bk: e.tensor_copy(
                KH[64:128, nt * 512:(nt + 1) * 512],
                pb[0:64, bk * 512:bk * 512 + 512]), "ev")
            p.banks_done([bk], "ev", ev)
        kmm = p.cnt("mm")
        p.wait("gpsimd", "mm", kmm)
        wcv = p.dma("gpsimd", WKVB[:, kbuf * 512:kbuf * 512 + 512], wv[h],
                    f"wkv{kbuf}")
        p.wait("tensor", f"wkv{kbuf}", wcv)
        for tt in range(32):
            bk = 2 + tt % 2
            p.banks_req([bk])
            for c in range(NCH):
                mmc = p.ins("tensor", lambda e, tt=tt, c=c, bk=bk, kbuf=kbuf:
                            e.matmul(
                    pb[:, bk * 512:bk * 512 + 64],
                    XN[:, c * TF_ + tt * 128:c * TF_ + (tt + 1) * 128],
                    WKVB[:, kbuf * 512 + c * 64:kbuf * 512 + (c + 1) * 64],
                    start=(c == 0), stop=(c == NCH - 1)),
                    "mm" if c == NCH - 1 else None)
            p.wait("vector", "mm", mmc)
            ev = p.ins("vector", lambda e, tt=tt, bk=bk: e.tensor_copy(
                VA[:, tt * 65:tt * 65 + 64], pb[:, bk * 512:bk * 512 + 64]),
                "ev")
            p.banks_done([bk], "ev", ev)
        p.wait("scalar", "ev", p.cnt("ev"))
        for hh in range(2):
            ac = p.ins("scalar", lambda e, hh=hh: e.activation(
                SQH[0:64, 0:2048], KH[0:64, hh * 2048:(hh + 1) * 2048],
                AF.Square), "ac")
            p.wait("tensor", "ac", ac)
            p.banks_req([4, 5, 6, 7])
            for nt in range(4):
                mmc = p.ins("tensor", lambda e, nt=nt: e.matmul(
                    pb[0:1, 2048 + nt * 512:2048 + nt * 512 + 512],
                    aux["onesc_h"][0:64, 0:1],
                    SQH[0:64, nt * 512:(nt + 1) * 512],
                    start=True, stop=True), "mm")
            p.wait("vector", "mm", mmc)
            ev = p.ins("vector", lambda e: e.tensor_copy(
                RR[32:33, 0:2048], pb[0:1, 2048:4096]), "ev")
            p.banks_done([4, 5, 6, 7], "ev", ev)
            p.ins("vector", lambda e, hh=hh: e.tensor_reduce(
                RR[0:1, 40 + hh:41 + hh], RR[32:33, 0:2048], AX.X, ALU.max))
            ev = p.ins("vector", lambda e: e.engine_nop(), "ev")
            p.wait("scalar", "ev", ev)
        p.ins("vector", lambda e: e.tensor_max(
            RR[0:1, 42:43], RR[0:1, 40:41], RR[0:1, 41:42]))
        p.ins("vector", lambda e, h=h: e.tensor_mul(
            RR[0:1, 43:44], RR[0:1, h:h + 1], RR[0:1, 42:43]))
        ev = p.ins("vector", lambda e: e.engine_nop(), "ev")
        p.wait("scalar", "ev", ev)
        ac = p.ins("scalar", lambda e: e.activation(
            RR[0:1, 43:44], RR[0:1, 43:44], AF.Sqrt), "ac")
        p.wait("vector", "ac", ac)
        ev = p.ins("vector", lambda e: e.tensor_scalar_mul(
            RR[0:1, 43:44], RR[0:1, 43:44], -1.0), "ev")
        p.wait("tensor", "ev", ev)
        p.banks_req([6])
        mmc = p.ins("tensor", lambda e: e.matmul(
            pb[:, 3072:3073], aux["onesr_f"][:], RR[0:1, 43:44],
            start=True, stop=True), "mm")
        p.wait("vector", "mm", mmc)
        ev = p.ins("vector", lambda e: e.tensor_copy(NEGMC[:],
                                                     pb[:, 3072:3073]), "ev")
        p.banks_done([6], "ev", ev)
        ev = p.ins("vector", lambda e: e.tensor_scalar(
            NEGB[:], invs[:], -1.0e30, NEGMC[:, 0:1], ALU.mult, ALU.add),
            "ev")
        p.wait("scalar", "ev", ev)
        p.wait("tensor", "ev", ev)

        for qg in range(4):
            qap = Q[base:base + 64, qch * T + qg * 512:qch * T + (qg + 1) * 512]
            nkc = 20 + 4 * qg
            pv0 = p.cnt("pv")
            for kc in range(nkc):
                slot = kc % 2
                bk = 4 + slot
                col = bk * 512
                p.banks_req([bk])
                mmc = p.ins("tensor", lambda e, kc=kc, col=col, base=base,
                            qap=qap: e.matmul(
                    pb[:, col:col + 512],
                    KH[base:base + 64, kc * 128:(kc + 1) * 128],
                    qap, start=True, stop=True), "mm")
                p.wait("scalar", "mm", mmc)
                if kc >= 2:
                    p.wait("scalar", "pv", pv0 + kc - 1)
                ac = p.ins("scalar", lambda e, kc=kc, col=col, qg=qg:
                           e.activation(
                    PT[:, (kc % 2) * 512:(kc % 2) * 512 + 512],
                    pb[:, col:col + 512], AF.Exp,
                    bias=NEGB[:, qg * 32 + kc:qg * 32 + kc + 1]), "ac")
                p.banks_done([bk], "ac", ac)
                dj = -1
                if 16 + 4 * qg <= kc < 16 + 4 * qg + 4:
                    dj = kc - 16 - 4 * qg
                if dj >= 0:
                    p.wait("vector", "ac", ac)
                    sv = p.ins("vector", lambda e, kc=kc, dj=dj: e.tensor_mul(
                        PT[:, (kc % 2) * 512:(kc % 2) * 512 + 512],
                        PT[:, (kc % 2) * 512:(kc % 2) * 512 + 512],
                        STR[:, dj * 512:(dj + 1) * 512]), "sv")
                    p.wait("tensor", "sv", sv)
                p.wait("tensor", "ac", ac)
                if kc == 0:
                    p.banks_req([6])
                p.ins("tensor", lambda e, kc=kc, nkc=nkc: e.matmul(
                    pb[0:65, 3072:3072 + 512],
                    VA[:, kc * 65:(kc + 1) * 65],
                    PT[:, (kc % 2) * 512:(kc % 2) * 512 + 512],
                    start=(kc == 0), stop=(kc == nkc - 1)), "pv")
            p.wait("vector", "pv", p.cnt("pv"))
            p.ins("vector", lambda e: e.tensor_copy(
                RR[0:1, 64:576], pb[64:65, 3072:3072 + 512]))
            p.ins("vector", lambda e: e.reciprocal(
                RR[0:1, 64:576], RR[0:1, 64:576]))
            ev = p.ins("vector", lambda e: e.tensor_copy(
                QSS[64:128, 0:512], pb[0:64, 3072:3072 + 512]), "ev")
            p.banks_done([6], "ev", ev)
            p.wait("tensor", "ev", ev)
            p.banks_req([7])
            mmc = p.ins("tensor", lambda e: e.matmul(
                pb[0:64, 3584:3584 + 512], aux["onesr_f"][0:1, 0:64],
                RR[0:1, 64:576], start=True, stop=True), "mm")
            p.wait("vector", "mm", mmc)
            ev = p.ins("vector", lambda e, base=base, qch=qch, qg=qg:
                       e.tensor_mul(
                AO[base:base + 64, qch * T + qg * 512:qch * T + (qg + 1) * 512],
                QSS[64:128, 0:512], pb[0:64, 3584:3584 + 512]), "ev")
            p.banks_done([7], "ev", ev)

    XB = af.take(2 * 512)
    YB = af.take(2 * 512)
    p.wait("tensor", "ev", p.cnt("ev"))
    qmm = {}
    stbuf = {0: 0, 1: 0}
    k_it = 0
    for m in range(NCH):
        wbuf = m % 2
        if m >= 2:
            p.wait("gpsimd", "mm", qmm[m - 2])
        wc = p.dma("gpsimd", WB[:, wbuf * 1024:(wbuf + 1) * 1024], wproj[m],
                   f"wq{wbuf}")
        p.wait("tensor", f"wq{wbuf}", wc)
        for nt in range(4):
            xb = k_it % 2
            p.wait("sync", f"st{xb}", stbuf[xb])
            lc = p.dma("sync", XB[:, xb * 512:(xb + 1) * 512],
                       xown[:, m * T + nt * 512:m * T + (nt + 1) * 512],
                       f"xld{xb}")
            bk = (m * 4 + nt) % 4
            p.banks_req([bk])
            for c in range(NCH):
                mmc = p.ins("tensor", lambda e, m=m, nt=nt, c=c, bk=bk,
                            wbuf=wbuf: e.matmul(
                    pb[:, bk * 512:bk * 512 + 512],
                    WB[:, wbuf * 1024 + c * 128:wbuf * 1024 + c * 128 + 128],
                    AO[:, c * T + nt * 512:c * T + (nt + 1) * 512],
                    start=(c == 0), stop=(c == NCH - 1)),
                    "mm" if c == NCH - 1 else None)
            p.wait("vector", "mm", mmc)
            p.wait("vector", f"xld{xb}", lc)
            p.wait("vector", f"st{xb}", stbuf[xb])
            ev = p.ins("vector", lambda e, bk=bk, xb=xb: e.tensor_add(
                YB[:, xb * 512:(xb + 1) * 512],
                pb[:, bk * 512:bk * 512 + 512],
                XB[:, xb * 512:(xb + 1) * 512]), "ev")
            p.banks_done([bk], "ev", ev)
            p.wait("sync", "ev", ev)
            stbuf[xb] = p.dma("sync",
                              yout[:, m * T + nt * 512:m * T + (nt + 1) * 512],
                              YB[:, xb * 512:(xb + 1) * 512], f"st{xb}")
            k_it += 1
        qmm[m] = p.cnt("mm")
    p.wait("sync", "st0", p.cnt("st0"))
    p.wait("sync", "st1", p.cnt("st1"))
    return p


def attn_b_inputs(inputs):
    W = np.asarray(inputs["L0_attn_b_W_attn"])
    Wp = np.asarray(inputs["L0_attn_b_W_proj"])
    hsel = np.zeros((128, NCH * 16), BF)
    for c in range(NCH):
        hsel[0:64, c * 16 + 2 * c] = 1
        hsel[64:128, c * 16 + 2 * c + 1] = 1
    wk = np.stack([np.ascontiguousarray(
        W[:, 1024 + h * 64:1024 + (h + 1) * 64]
        .reshape(NCH, 128, 64).transpose(1, 0, 2).reshape(128, NCH * 64))
        for h in range(16)])
    wv = np.stack([np.ascontiguousarray(
        W[:, 2048 + h * 64:2048 + (h + 1) * 64]
        .reshape(NCH, 128, 64).transpose(1, 0, 2).reshape(128, NCH * 64))
        for h in range(16)])
    d = {
        "wq": pack_wio(np.ascontiguousarray(W[:, :1024])),
        "wk": wk,
        "wv": wv,
        "wproj": pack_wio(Wp),
        "hsel": hsel,
    }
    d.update(host_aux())
    percore = []
    p_ = np.arange(128)[:, None]
    f_ = np.arange(512)[None, :]
    st = np.zeros((128, 4 * 512), BF)
    for j in range(4):
        st[:, j * 512:(j + 1) * 512] = ((j * 128 + p_) <= f_)
    for h in range(2):
        inv = np.zeros((128, 128), np.float32)
        for qg in range(4):
            for kc in range(32):
                if kc < 16:
                    inv[:, qg * 32 + kc] = 1.0 if h == 0 else 0.0
                else:
                    keys = (kc - 16) * 128 + np.arange(128)
                    inv[:, qg * 32 + kc] = (keys > qg * 512 + 511)
            for j in range(4):
                inv[:, qg * 32 + 16 + 4 * qg + j] = 0
        percore.append({"invtab": inv, "strip": st})
    return d, percore


def run_attn_b(x, inputs):
    nc = get_prog("attn_b", build_attn_b)
    d, percore = attn_b_inputs(inputs)
    in_maps = []
    for c in range(NCORES):
        b, h = c // 2, c % 2
        m = dict(d)
        m.update(percore[h])
        xrot = np.concatenate([x[b, (1 - h) * T:(2 - h) * T],
                               x[b, h * T:(h + 1) * T]], axis=0)
        m["xfull"] = fm_pack(xrot)
        m["xown"] = fm_pack(x[b, h * T:(h + 1) * T])
        in_maps.append(m)
    res = run_spmd_cached(nc, in_maps, core_ids=list(range(NCORES)))
    out = np.empty_like(x)
    for c in range(NCORES):
        b, h = c // 2, c % 2
        out[b, h * T:(h + 1) * T] = fm_unpack(res.results[c]["yfm"], T)
    return out





def build_l1_attn_b(dbg_qss=False):
    p = Prog("l1_attn_b")
    TF_ = 4096

    xfull = p.inp("xfull", [128, NCH * TF_])
    xown = p.inp("xown", [128, NCH * T])
    wq = p.inp("wq", [NCH, 128, NCH * 128])
    wk = p.inp("wk", [NCH, 128, NCH * 128])
    wv = p.inp("wv", [D, D])
    wo = p.inp("wo", [NCH, 128, NCH * 128])
    ob = p.inp("ob", [128, NCH])
    yout = p.outp("yfm", [128, NCH * T])

    aux = emit_aux(p)
    pb = p.psum("pb", [128, 4096], F32)
    ah = Arena(p, "AH", 82000, BF16)
    af = Arena(p, "AF", 9728, F32)
    v_scr = p.dram("v_scr", [32, 128, 1024], BF16)

    KF = ah.take(NCH * TF_)
    Q = ah.take(NCH * T)
    WVB = ah.take(NCH * 1024)
    WQB = ah.take(2 * 1024)
    VT = ah.take(2 * 1024)
    SQB = ah.take(4096)
    ah.mark()
    XNH = ah.take(NCH * T)
    obs = p.sbuf("obs", [128, NCH], F32)
    NEGMC = p.sbuf("negmc", [128, 1], F32)
    lc = p.dma("sync", obs[:], ob, "ld2")

    gwq = {}
    gq = [0]
    vstb = {0: 0, 1: 0}

    def wq_guard():
        p.wait("gpsimd", "mm", gwq.get(gq[0] - 2, 0))

    def wq_done():
        gwq[gq[0]] = p.cnt("mm")
        gq[0] += 1

    for half in range(2):
        af.mark()
        XS = af.take(NCH * 512)
        for q8 in range(4):
            lc = None
            for c in range(NCH):
                lc = p.dma("sync", XS[:, c * 512:(c + 1) * 512],
                           xfull[:, c * TF_ + half * T + q8 * 512:
                                 c * TF_ + half * T + (q8 + 1) * 512], "ld")
            for eng in ("tensor", "scalar", "vector"):
                p.wait(eng, "ld", lc)
            emit_ln(p, aux,
                    lambda c, t0, t1, q8=q8: XS[:, c * 512 + t0:c * 512 + t1],
                    lambda c, q8=q8: XNH[:, c * T + q8 * 512:
                                         c * T + (q8 + 1) * 512],
                    512, pb, af, f"h{half}s{q8}")
            p.wait("sync", "ev", p.cnt("ev"))
        af.release()
        p.wait("tensor", "ev", p.cnt("ev"))

        for m in range(NCH):
            wbuf = gq[0] % 2
            wq_guard()
            wc = p.dma("gpsimd", WQB[:, wbuf * 1024:(wbuf + 1) * 1024],
                       wk[m], f"wq{wbuf}")
            p.wait("tensor", f"wq{wbuf}", wc)
            for nt in range(4):
                bk = (m * 4 + nt) % 4
                p.banks_req([bk])
                for c in range(NCH):
                    mmc = p.ins("tensor", lambda e, m=m, nt=nt, c=c, bk=bk,
                                wbuf=wbuf: e.matmul(
                        pb[:, bk * 512:bk * 512 + 512],
                        WQB[:, wbuf * 1024 + c * 128:
                            wbuf * 1024 + c * 128 + 128],
                        XNH[:, c * T + nt * 512:c * T + (nt + 1) * 512],
                        start=(c == 0), stop=(c == NCH - 1)),
                        "mm" if c == NCH - 1 else None)
                p.wait("vector", "mm", mmc)
                ev = p.ins("vector", lambda e, m=m, nt=nt, bk=bk, half=half:
                           e.tensor_copy(
                    KF[:, m * TF_ + half * T + nt * 512:
                       m * TF_ + half * T + (nt + 1) * 512],
                    pb[:, bk * 512:bk * 512 + 512]), "ev")
                p.banks_done([bk], "ev", ev)
            wq_done()

        wcv = wload_chunks(p, WVB, NCH, 1024, wv, "mm", gwq[gq[0] - 2])
        p.wait("tensor", "wld", wcv)
        for tt in range(16):
            vb = tt % 2
            for n2 in range(2):
                bk = 4 + (tt * 2 + n2) % 2
                p.banks_req([bk])
                for c in range(NCH):
                    mmc = p.ins("tensor", lambda e, tt=tt, n2=n2, c=c, bk=bk:
                                e.matmul(
                        pb[:, bk * 512:bk * 512 + 512],
                        XNH[:, c * T + tt * 128:c * T + (tt + 1) * 128],
                        WVB[:, c * 1024 + n2 * 512:c * 1024 + (n2 + 1) * 512],
                        start=(c == 0), stop=(c == NCH - 1)),
                        "mm" if c == NCH - 1 else None)
                p.wait("vector", "mm", mmc)
                p.wait("vector", f"vst{vb}", vstb[vb])
                ev = p.ins("vector", lambda e, vb=vb, n2=n2, bk=bk:
                           e.tensor_copy(
                    VT[:, vb * 1024 + n2 * 512:vb * 1024 + (n2 + 1) * 512],
                    pb[:, bk * 512:bk * 512 + 512]), "ev")
                p.banks_done([bk], "ev", ev)
            p.wait("sync", "ev", p.cnt("ev"))
            vstb[vb] = p.dma("sync", v_scr.ap()[half * 16 + tt],
                             VT[:, vb * 1024:(vb + 1) * 1024], f"vst{vb}")

        if half == 1:
            for m in range(NCH):
                wbuf = gq[0] % 2
                wq_guard()
                wc = p.dma("gpsimd", WQB[:, wbuf * 1024:(wbuf + 1) * 1024],
                           wq[m], f"wq{wbuf}")
                p.wait("tensor", f"wq{wbuf}", wc)
                for nt in range(4):
                    bk = (m * 4 + nt) % 4
                    p.banks_req([bk])
                    for c in range(NCH):
                        mmc = p.ins("tensor", lambda e, m=m, nt=nt, c=c,
                                    bk=bk, wbuf=wbuf: e.matmul(
                            pb[:, bk * 512:bk * 512 + 512],
                            WQB[:, wbuf * 1024 + c * 128:
                                wbuf * 1024 + c * 128 + 128],
                            XNH[:, c * T + nt * 512:c * T + (nt + 1) * 512],
                            start=(c == 0), stop=(c == NCH - 1)),
                            "mm" if c == NCH - 1 else None)
                    p.wait("vector", "mm", mmc)
                    ev = p.ins("vector", lambda e, m=m, nt=nt, bk=bk:
                               e.tensor_scalar_mul(
                        Q[:, m * T + nt * 512:m * T + (nt + 1) * 512],
                        pb[:, bk * 512:bk * 512 + 512], 1.0 / 32.0), "ev")
                    p.banks_done([bk], "ev", ev)
                wq_done()
    ah.release()
    PROBS = ah.take(32 * 512)
    TMPF = af.take(512)
    RR = af.take(2048)

    p.wait("scalar", "ev", p.cnt("ev"))
    p.banks_req([0, 1, 2, 3])
    for c in range(NCH):
        ac = p.ins("scalar", lambda e, c=c: e.activation(
            SQB[:, 0:2048], Q[:, c * T:(c + 1) * T], AF.Square), "ac")
        p.wait("tensor", "ac", ac)
        for nt in range(4):
            mmc = p.ins("tensor", lambda e, c=c, nt=nt: e.matmul(
                pb[0:1, nt * 512:nt * 512 + 512],
                aux["onesc_h"][:],
                SQB[:, nt * 512:(nt + 1) * 512],
                start=(c == 0), stop=(c == NCH - 1)),
                "mm" if nt == 3 else None)
        p.wait("scalar", "mm", mmc)
    p.wait("vector", "mm", p.cnt("mm"))
    ev = p.ins("vector", lambda e: e.tensor_copy(RR[0:1, 0:2048],
                                                 pb[0:1, 0:2048]), "ev")
    p.banks_done([0, 1, 2, 3], "ev", ev)
    if dbg_qss:
        dq = p.outp("d_qss", [1, 2048], F32)
        p.wait("sync", "ev", p.cnt("ev"))
        p.dma("sync", dq, RR[0:1, 0:2048], "st0")
        dq2 = p.outp("d_sqb", [128, 2048], BF16)
        p.dma("sync", dq2, SQB[:, 0:2048], "st0")
    p.ins("vector", lambda e: e.tensor_reduce(
        TMPF[0:1, 0:1], RR[0:1, 0:2048], AX.X, ALU.max))
    ev = p.ins("vector", lambda e: e.engine_nop(), "ev")
    for kk in range(2):
        p.wait("scalar", "ev", p.cnt("ev"))
        p.banks_req([0, 1, 2, 3])
        for c in range(NCH):
            ac = p.ins("scalar", lambda e, c=c, kk=kk: e.activation(
                SQB[:, 0:2048], KF[:, c * TF_ + kk * T:c * TF_ + (kk + 1) * T],
                AF.Square), "ac")
            p.wait("tensor", "ac", ac)
            for nt in range(4):
                mmc = p.ins("tensor", lambda e, c=c, nt=nt: e.matmul(
                    pb[0:1, nt * 512:nt * 512 + 512],
                    aux["onesc_h"][:],
                    SQB[:, nt * 512:(nt + 1) * 512],
                    start=(c == 0), stop=(c == NCH - 1)),
                    "mm" if nt == 3 else None)
            p.wait("scalar", "mm", mmc)
        p.wait("vector", "mm", p.cnt("mm"))
        ev = p.ins("vector", lambda e: e.tensor_copy(RR[0:1, 0:2048],
                                                     pb[0:1, 0:2048]), "ev")
        p.banks_done([0, 1, 2, 3], "ev", ev)
        p.ins("vector", lambda e, kk=kk: e.tensor_reduce(
            TMPF[0:1, 1 + kk:2 + kk], RR[0:1, 0:2048], AX.X, ALU.max))
        ev = p.ins("vector", lambda e: e.engine_nop(), "ev")
    p.ins("vector", lambda e: e.tensor_max(
        TMPF[0:1, 3:4], TMPF[0:1, 1:2], TMPF[0:1, 2:3]))
    p.ins("vector", lambda e: e.tensor_mul(
        TMPF[0:1, 4:5], TMPF[0:1, 0:1], TMPF[0:1, 3:4]))
    ev = p.ins("vector", lambda e: e.engine_nop(), "ev")
    p.wait("scalar", "ev", ev)
    ac = p.ins("scalar", lambda e: e.activation(
        TMPF[0:1, 4:5], TMPF[0:1, 4:5], AF.Sqrt), "ac")
    p.wait("vector", "ac", ac)
    ev = p.ins("vector", lambda e: e.tensor_scalar_mul(
        TMPF[0:1, 4:5], TMPF[0:1, 4:5], -1.0), "ev")
    p.wait("tensor", "ev", ev)
    p.banks_req([6])
    mmc = p.ins("tensor", lambda e: e.matmul(
        pb[:, 3072:3073], aux["onesr_f"][:], TMPF[0:1, 4:5],
        start=True, stop=True), "mm")
    p.wait("vector", "mm", mmc)
    ev = p.ins("vector", lambda e: e.tensor_copy(NEGMC[:], pb[:, 3072:3073]),
               "ev")
    p.banks_done([6], "ev", ev)
    p.wait("scalar", "ev", p.cnt("ev"))
    p.wait("tensor", "ev", p.cnt("ev"))

    stbuf = {0: 0, 1: 0}
    vld = {0: 0, 1: 0}
    YF = SQB
    XB2 = af.take(2 * 512)
    YB2 = af.take(2 * 512)
    k_it = 0
    p.wait("sync", "vst0", p.cnt("vst0"))
    p.wait("sync", "vst1", p.cnt("vst1"))
    for qg in range(4):
        for kc in range(32):
            bk = kc % 2
            p.banks_req([bk])
            for c in range(NCH):
                mmc = p.ins("tensor", lambda e, kc=kc, c=c, bk=bk, qg=qg:
                            e.matmul(
                    pb[:, bk * 512:bk * 512 + 512],
                    KF[:, c * TF_ + kc * 128:c * TF_ + (kc + 1) * 128],
                    Q[:, c * T + qg * 512:c * T + (qg + 1) * 512],
                    start=(c == 0), stop=(c == NCH - 1)),
                    "mm" if c == NCH - 1 else None)
            p.wait("scalar", "mm", mmc)
            ac = p.ins("scalar", lambda e, kc=kc, bk=bk: e.activation(
                PROBS[:, kc * 512:(kc + 1) * 512],
                pb[:, bk * 512:bk * 512 + 512], AF.Exp,
                bias=NEGMC[:, 0:1]), "ac")
            p.banks_done([bk], "ac", ac)
        p.wait("tensor", "ac", p.cnt("ac"))
        p.banks_req(range(8))
        for kc in range(32):
            vb = kc % 2
            p.wait("sync", "pv", vld[vb])
            lcv = p.dma("sync", VT[:, vb * 1024:(vb + 1) * 1024],
                        v_scr.ap()[kc], f"vld{vb}")
            p.wait("tensor", f"vld{vb}", lcv)
            for dc in range(7):
                p.ins("tensor", lambda e, kc=kc, dc=dc, vb=vb: e.matmul(
                    pb[:, dc * 512:dc * 512 + 512],
                    VT[:, vb * 1024 + dc * 128:vb * 1024 + dc * 128 + 128],
                    PROBS[:, kc * 512:(kc + 1) * 512],
                    start=(kc == 0), stop=(kc == 31)))
            vld[vb] = p.ins("tensor", lambda e, kc=kc: e.matmul(
                pb[0:1, 3584:3584 + 512], aux["onesc_h"][:],
                PROBS[:, kc * 512:(kc + 1) * 512],
                start=(kc == 0), stop=(kc == 31)), "pv")
        p.wait("vector", "pv", p.cnt("pv"))
        p.ins("vector", lambda e: e.tensor_copy(RR[0:1, 0:512],
                                                pb[0:1, 3584:3584 + 512]))
        ev = p.ins("vector", lambda e: e.reciprocal(RR[0:1, 0:512],
                                                    RR[0:1, 0:512]), "ev")
        p.wait("tensor", "ev", ev)
        mmc = p.ins("tensor", lambda e: e.matmul(
            pb[:, 3584:3584 + 512], aux["onesr_f"][:], RR[0:1, 0:512],
            start=True, stop=True), "mm")
        p.wait("vector", "mm", mmc)
        for dc in range(7):
            p.ins("vector", lambda e, dc=dc: e.tensor_copy(
                TMPF[:, 0:512], pb[:, dc * 512:dc * 512 + 512]))
            ev = p.ins("vector", lambda e, dc=dc: e.tensor_mul(
                YF[:, dc * 512:(dc + 1) * 512], TMPF[:, 0:512],
                pb[:, 3584:3584 + 512]), "ev")
            p.banks_done([dc], "ev", ev)
        p.banks_req([0])
        for kc in range(32):
            vb = kc % 2
            p.wait("sync", "pv", vld[vb])
            lcv = p.dma("sync", VT[:, vb * 1024:(vb + 1) * 1024],
                        v_scr.ap()[kc], f"vld{vb}")
            p.wait("tensor", f"vld{vb}", lcv)
            vld[vb] = p.ins("tensor", lambda e, kc=kc, vb=vb: e.matmul(
                pb[:, 0:512],
                VT[:, vb * 1024 + 7 * 128:vb * 1024 + 7 * 128 + 128],
                PROBS[:, kc * 512:(kc + 1) * 512],
                start=(kc == 0), stop=(kc == 31)), "pv")
        p.wait("vector", "pv", p.cnt("pv"))
        p.ins("vector", lambda e: e.tensor_copy(TMPF[:, 0:512], pb[:, 0:512]))
        ev = p.ins("vector", lambda e: e.tensor_mul(
            YF[:, 7 * 512:8 * 512], TMPF[:, 0:512],
            pb[:, 3584:3584 + 512]), "ev")
        p.banks_done([0], "ev", ev)
        p.banks_done([7], "ev", ev)

        p.wait("tensor", "ev", p.cnt("ev"))
        p.wait("vector", "ld2", p.cnt("ld2"))
        for m in range(NCH):
            wbuf = gq[0] % 2
            wq_guard()
            wc = p.dma("gpsimd", WQB[:, wbuf * 1024:(wbuf + 1) * 1024],
                       wo[m], f"wq{wbuf}")
            p.wait("tensor", f"wq{wbuf}", wc)
            xb = k_it % 2
            p.wait("sync", f"st{xb}", stbuf[xb])
            lc = p.dma("sync", XB2[:, xb * 512:(xb + 1) * 512],
                       xown[:, m * T + qg * 512:m * T + (qg + 1) * 512],
                       f"xld{xb}")
            bk = 1 + (m % 2)
            p.banks_req([bk])
            for c in range(NCH):
                mmc = p.ins("tensor", lambda e, m=m, c=c, bk=bk, wbuf=wbuf:
                            e.matmul(
                    pb[:, bk * 512:bk * 512 + 512],
                    WQB[:, wbuf * 1024 + c * 128:wbuf * 1024 + c * 128 + 128],
                    YF[:, c * 512:(c + 1) * 512],
                    start=(c == 0), stop=(c == NCH - 1)),
                    "mm" if c == NCH - 1 else None)
            p.wait("vector", "mm", mmc)
            p.wait("vector", f"xld{xb}", lc)
            p.wait("vector", f"st{xb}", stbuf[xb])
            p.ins("vector", lambda e, m=m, bk=bk: e.tensor_scalar_add(
                pb[:, bk * 512:bk * 512 + 512],
                pb[:, bk * 512:bk * 512 + 512], obs[:, m:m + 1]))
            ev = p.ins("vector", lambda e, bk=bk, xb=xb: e.tensor_add(
                YB2[:, xb * 512:(xb + 1) * 512],
                pb[:, bk * 512:bk * 512 + 512],
                XB2[:, xb * 512:(xb + 1) * 512]), "ev")
            p.banks_done([bk], "ev", ev)
            p.wait("sync", "ev", ev)
            stbuf[xb] = p.dma(
                "sync", yout[:, m * T + qg * 512:m * T + (qg + 1) * 512],
                YB2[:, xb * 512:(xb + 1) * 512], f"st{xb}")
            k_it += 1
            wq_done()
    p.wait("sync", "st0", p.cnt("st0"))
    p.wait("sync", "st1", p.cnt("st1"))
    return p


def l1_attn_b_inputs(inputs):
    W = np.asarray(inputs["L1_attn_b_weight_qkv"])
    wo = np.ascontiguousarray(np.asarray(inputs["L1_attn_b_out_w"]).T)
    d = {
        "wq": pack_wio(np.ascontiguousarray(W[:, 0:1024])),
        "wk": pack_wio(np.ascontiguousarray(W[:, 1024:2048])),
        "wv": np.ascontiguousarray(W[:, 2048:3072]),
        "wo": pack_wio(wo),
        "ob": col_pack(np.asarray(inputs["L1_attn_b_out_b"])),
    }
    d.update(host_aux())
    return d


def run_l1_attn_b(x, inputs):
    nc = get_prog("l1_attn_b", build_l1_attn_b)
    d = l1_attn_b_inputs(inputs)
    in_maps = []
    for c in range(NCORES):
        b, h = c // 2, c % 2
        m = dict(d)
        xrot = np.concatenate([x[b, (1 - h) * T:(2 - h) * T],
                               x[b, h * T:(h + 1) * T]], axis=0)
        m["xfull"] = fm_pack(xrot)
        m["xown"] = fm_pack(x[b, h * T:(h + 1) * T])
        in_maps.append(m)
    res = run_spmd_cached(nc, in_maps, core_ids=list(range(NCORES)))
    out = np.empty_like(x)
    for c in range(NCORES):
        b, h = c // 2, c % 2
        out[b, h * T:(h + 1) * T] = fm_unpack(res.results[c]["yfm"], T)
    return out





def build_l1_attn_a():
    p = Prog("l1_attn_a")
    T3 = 3072
    KR = 2560

    xbuf = p.inp("xbuf", [128, NCH * T3])
    xown = p.inp("xown", [128, NCH * T])
    wq = p.inp("wq", [NCH, 128, NCH * 128])
    wk = p.inp("wk", [NCH, 128, NCH * 128])
    wv = p.inp("wv", [NCH, 128, NCH * 128])
    wo = p.inp("wo", [NCH, 128, NCH * 128])
    bq = p.inp("bq", [128, NCH])
    bk = p.inp("bk", [128, NCH])
    bv = p.inp("bv", [128, NCH])
    bo = p.inp("bo", [128, NCH])
    cqw = p.inp("cqw", [128, NCH * 4])
    ckw = p.inp("ckw", [128, NCH * 4])
    cvw = p.inp("cvw", [128, NCH * 4])
    cqb = p.inp("cqb", [128, NCH])
    ckb = p.inp("ckb", [128, NCH])
    cvb = p.inp("cvb", [128, NCH])
    tcos = p.inp("tcos", [128, T3])
    tsin = p.inp("tsin", [128, T3])
    hsel2 = p.inp("hsel2", [128, 2])
    hsel2t = p.inp("hsel2t", [2, 128])
    biastab = p.inp("biastab", [128, 8])
    hflag = p.inp("hflag", [128, 1])
    stripd = p.inp("stripd", [128, 2 * 128], BF16)
    yout = p.outp("yfm", [128, NCH * T])

    aux = emit_aux(p)
    pb = p.psum("pb", [128, 4096], F32)
    ah = Arena(p, "AH", 26400, BF16)
    af = Arena(p, "AF", 27000, F32)

    XN3 = p.sbuf("XN3", [128, NCH * T3], BF16)
    smalls = {}
    for nm, ap_, sh in (("bq", bq, [128, NCH]), ("bk", bk, [128, NCH]),
                        ("bv", bv, [128, NCH]), ("bo", bo, [128, NCH]),
                        ("cqw", cqw, [128, NCH * 4]),
                        ("ckw", ckw, [128, NCH * 4]),
                        ("cvw", cvw, [128, NCH * 4]),
                        ("cqb", cqb, [128, NCH]), ("ckb", ckb, [128, NCH]),
                        ("cvb", cvb, [128, NCH]),
                        ("hsel2", hsel2, [128, 2]), ("hsel2t", hsel2t, [2, 128]),
                        ("biastab", biastab, [128, 8]),
                        ("hflag", hflag, [128, 1])):
        smalls[nm] = p.sbuf("s_" + nm, sh, F32)
        p.dma("sync", smalls[nm][:], ap_, "ld2")
    STRD = ah.take(2 * 128)
    p.dma("gpsimd", STRD[:, 0:256], stripd, "wld")
    TC = af.take(T3)
    TS = af.take(T3)
    p.dma("sync", TC[:, 0:T3], tcos, "ld2")
    p.dma("sync", TS[:, 0:T3], tsin, "ld2")

    af.mark()
    XS = af.take(NCH * 512)
    for q6 in range(6):
        lc = None
        for c in range(NCH):
            lc = p.dma("sync", XS[:, c * 512:(c + 1) * 512],
                       xbuf[:, c * T3 + q6 * 512:c * T3 + (q6 + 1) * 512],
                       "ld")
        for eng in ("tensor", "scalar", "vector"):
            p.wait(eng, "ld", lc)
        emit_ln(p, aux,
                lambda c, t0, t1, q6=q6: XS[:, c * 512 + t0:c * 512 + t1],
                lambda c, q6=q6: XN3[:, c * T3 + q6 * 512:
                                     c * T3 + (q6 + 1) * 512],
                512, pb, af, f"s{q6}")
        p.wait("sync", "ev", p.cnt("ev"))
    af.release()
    p.wait("tensor", "ev", p.cnt("ev"))
    p.wait("tensor", "ld2", p.cnt("ld2"))
    p.wait("vector", "ld2", p.cnt("ld2"))
    p.wait("scalar", "ld2", p.cnt("ld2"))
    p.wait("tensor", "wld", p.cnt("wld"))

    PROJ = af.take(T3)
    QC = af.take(T)
    KC = af.take(KR)
    VC = af.take(KR)
    SQ = af.take(T3)
    XR = SQ
    R2 = af.take(KR)
    WBUF = ah.take(2 * 1024)
    QH = ah.take(T)
    KH2 = ah.take(KR)
    VT = ah.take(2 * 20 * 65)
    AO = ah.take(NCH * T)
    PT = ah.take(2 * 128)
    NEGC = p.sbuf("negc", [128, 1], F32)
    ev = p.ins("vector", lambda e: e.memset(VT[:, 64:2 * 20 * 65:65], 1.0),
               "ev")

    gwq = {}
    gq = [0]

    def wq_guard():
        p.wait("gpsimd", "mm", gwq.get(gq[0] - 2, 0))

    def wq_done():
        gwq[gq[0]] = p.cnt("mm")
        gq[0] += 1

    def proj(dst, wsrc, hc, width, off3, bias_col):
        wbuf = gq[0] % 2
        wq_guard()
        wc = p.dma("gpsimd", WBUF[:, wbuf * 1024:(wbuf + 1) * 1024],
                   wsrc[hc], f"wq{wbuf}")
        p.wait("tensor", f"wq{wbuf}", wc)
        ntile = width // 512
        for nt in range(ntile):
            bkk = nt % 4
            p.banks_req([bkk])
            for c in range(NCH):
                mmc = p.ins("tensor", lambda e, nt=nt, c=c, bkk=bkk,
                            wbuf=wbuf, off3=off3: e.matmul(
                    pb[:, bkk * 512:bkk * 512 + 512],
                    WBUF[:, wbuf * 1024 + c * 128:wbuf * 1024 + c * 128 + 128],
                    XN3[:, c * T3 + off3 + nt * 512:
                        c * T3 + off3 + (nt + 1) * 512],
                    start=(c == 0), stop=(c == NCH - 1)),
                    "mm" if c == NCH - 1 else None)
            p.wait("vector", "mm", mmc)
            ev = p.ins("vector", lambda e, nt=nt, bkk=bkk, dst=dst,
                       bias_col=bias_col: e.tensor_scalar_add(
                dst[:, nt * 512:(nt + 1) * 512],
                pb[:, bkk * 512:bkk * 512 + 512], bias_col), "ev")
            p.banks_done([bkk], "ev", ev)
        wq_done()

    def conv(dst, srcp, wcol, bcol, dst_w, src_off):
        p.ins("vector", lambda e: e.tensor_scalar_mul(
            dst[:, 0:dst_w], srcp[:, src_off - 3:src_off - 3 + dst_w],
            wcol(0)))
        for j in range(1, 4):
            p.ins("vector", lambda e, j=j: e.scalar_tensor_tensor(
                dst[:, 0:dst_w], srcp[:, src_off - 3 + j:src_off - 3 + j + dst_w],
                wcol(j), dst[:, 0:dst_w], ALU.mult, ALU.add))
        ev = p.ins("vector", lambda e: e.tensor_scalar_add(
            dst[:, 0:dst_w], dst[:, 0:dst_w], bcol), "ev")
        return ev

    def rotary(buf, w, toff):
        for (d0, s0, sgn) in ((0, 32, -1.0), (32, 0, 1.0), (64, 96, -1.0),
                              (96, 64, 1.0)):
            p.ins("vector", lambda e, d0=d0, s0=s0, sgn=sgn:
                  e.tensor_scalar_mul(
                XR[d0:d0 + 32, 0:w], buf[s0:s0 + 32, 0:w], sgn))
        p.ins("vector", lambda e: e.tensor_mul(
            XR[:, 0:w], XR[:, 0:w], TS[:, toff:toff + w]))
        p.ins("vector", lambda e: e.tensor_mul(
            buf[:, 0:w], buf[:, 0:w], TC[:, toff:toff + w]))
        ev = p.ins("vector", lambda e: e.tensor_add(
            buf[:, 0:w], buf[:, 0:w], XR[:, 0:w]), "ev")
        return ev

    def l2norm_to(dst_bf, buf, w):
        p.wait("scalar", "ev", p.cnt("ev"))
        ac = p.ins("scalar", lambda e: e.activation(
            SQ[:, 0:w], buf[:, 0:w], AF.Square), "ac")
        p.wait("tensor", "ac", ac)
        for nt in range((w + 511) // 512):
            n = min(512, w - nt * 512)
            col = 2048 + (nt % 2) * 512
            p.banks_req([4 + nt % 2])
            mmc = p.ins("tensor", lambda e, nt=nt, n=n, col=col: e.matmul(
                pb[0:2, col:col + n],
                smalls["hsel2"][:], SQ[:, nt * 512:nt * 512 + n],
                start=True, stop=True), "mm")
            p.wait("vector", "mm", mmc)
            ev = p.ins("vector", lambda e, nt=nt, n=n, col=col: e.tensor_copy(
                R2[0:2, nt * 512:nt * 512 + n], pb[0:2, col:col + n]), "ev")
            p.banks_done([4 + nt % 2], "ev", ev)
        ev = p.ins("vector", lambda e: e.tensor_scalar_add(
            R2[0:2, 0:w], R2[0:2, 0:w], 1e-12), "ev")
        p.wait("scalar", "ev", ev)
        ac = p.ins("scalar", lambda e: e.activation(
            R2[0:2, 0:w], R2[0:2, 0:w], AF.Sqrt), "ac")
        p.wait("vector", "ac", ac)
        ev = p.ins("vector", lambda e: e.reciprocal(R2[0:2, 0:w],
                                                    R2[0:2, 0:w]), "ev")
        p.wait("tensor", "ev", ev)
        for nt in range((w + 511) // 512):
            n = min(512, w - nt * 512)
            col = 3072 + (nt % 2) * 512
            p.banks_req([6 + nt % 2])
            mmc = p.ins("tensor", lambda e, nt=nt, n=n, col=col: e.matmul(
                pb[:, col:col + n],
                smalls["hsel2t"][:], R2[0:2, nt * 512:nt * 512 + n],
                start=True, stop=True), "mm")
            p.wait("vector", "mm", mmc)
            ev = p.ins("vector", lambda e, nt=nt, n=n, col=col,
                       dst_bf=dst_bf: e.tensor_mul(
                dst_bf[:, nt * 512:nt * 512 + n],
                buf[:, nt * 512:nt * 512 + n], pb[:, col:col + n]), "ev")
            p.banks_done([6 + nt % 2], "ev", ev)
        return ev

    ev = p.ins("vector", lambda e: e.memset(NEGC[:], -0.125), "ev")
    p.wait("scalar", "ev", p.cnt("ev"))

    for hc in range(NCH):
        p.wait("vector", "ac", p.cnt("ac"))
        def zpad():
            p.ins("vector", lambda e: e.tensor_scalar_mul(
                PROJ[:, 1021:1024], PROJ[:, 1021:1024],
                smalls["hflag"][:, 0:1]))

        proj(PROJ, wq, hc, T3, 0, smalls["bq"][:, hc:hc + 1])
        zpad()
        conv(QC, PROJ, lambda j: smalls["cqw"][:, hc * 4 + j:hc * 4 + j + 1],
             smalls["cqb"][:, hc:hc + 1], T, 1024)
        proj(PROJ, wk, hc, T3, 0, smalls["bk"][:, hc:hc + 1])
        zpad()
        conv(KC, PROJ, lambda j: smalls["ckw"][:, hc * 4 + j:hc * 4 + j + 1],
             smalls["ckb"][:, hc:hc + 1], KR, 512)
        proj(PROJ, wv, hc, T3, 0, smalls["bv"][:, hc:hc + 1])
        zpad()
        ev = conv(VC, PROJ,
                  lambda j: smalls["cvw"][:, hc * 4 + j:hc * 4 + j + 1],
                  smalls["cvb"][:, hc:hc + 1], KR, 512)
        rotary(QC, T, 1024)
        rotary(KC, KR, 512)
        ev = l2norm_to(QH, QC, T)
        ev = l2norm_to(KH2, KC, KR)
        p.wait("tensor", "ev", p.cnt("ev"))
        for tt in range(20):
            bkk = 4 + tt % 2
            p.banks_req([bkk])
            mmc = p.ins("tensor", lambda e, tt=tt, bkk=bkk: e.transpose(
                pb[:, bkk * 512:bkk * 512 + 128],
                VC[:, tt * 128:(tt + 1) * 128], aux["eye_f"][:]), "mm")
            p.wait("vector", "mm", mmc)
            p.ins("vector", lambda e, tt=tt, bkk=bkk: e.tensor_copy(
                VT[:, tt * 65:tt * 65 + 64],
                pb[0:128, bkk * 512:bkk * 512 + 64]))
            ev = p.ins("vector", lambda e, tt=tt, bkk=bkk: e.tensor_copy(
                VT[:, (20 + tt) * 65:(20 + tt) * 65 + 64],
                pb[0:128, bkk * 512 + 64:bkk * 512 + 128]), "ev")
            p.banks_done([bkk], "ev", ev)
        p.wait("tensor", "ev", p.cnt("ev"))
        for hh in range(2):
            base = hh * 64
            for qt in range(16):
                pv0 = p.cnt("pv")
                for c in range(5):
                    s = qt + c
                    slot = c % 2
                    bkk = 4 + slot
                    p.banks_req([bkk])
                    mmc = p.ins("tensor", lambda e, s=s, qt=qt, base=base,
                                bkk=bkk: e.matmul(
                        pb[:, bkk * 512:bkk * 512 + 128],
                        KH2[base:base + 64, s * 128:(s + 1) * 128],
                        QH[base:base + 64, qt * 128:(qt + 1) * 128],
                        start=True, stop=True), "mm")
                    p.wait("scalar", "mm", mmc)
                    if c >= 2:
                        p.wait("scalar", "pv", pv0 + c - 1)
                    ac = p.ins("scalar", lambda e, s=s, qt=qt, bkk=bkk,
                               slot=slot: e.activation(
                        PT[:, slot * 128:slot * 128 + 128],
                        pb[:, bkk * 512:bkk * 512 + 128], AF.Exp,
                        scale=0.125,
                        bias=smalls["biastab"][:, min(s, 4) if False else
                                               (s if s < 4 else 4):
                                               (s if s < 4 else 4) + 1]), "ac")
                    p.banks_done([bkk], "ac", ac)
                    sv = None
                    if c == 0 or c == 4:
                        p.wait("vector", "ac", ac)
                        sv = p.ins("vector", lambda e, slot=slot,
                                   dj=(0 if c == 0 else 1): e.tensor_mul(
                            PT[:, slot * 128:slot * 128 + 128],
                            PT[:, slot * 128:slot * 128 + 128],
                            STRD[:, dj * 128:(dj + 1) * 128]), "sv")
                        p.wait("tensor", "sv", sv)
                    p.wait("tensor", "ac", ac)
                    if c == 0:
                        p.banks_req([6])
                    p.ins("tensor", lambda e, s=s, hh=hh, slot=slot, c=c:
                          e.matmul(
                        pb[0:65, 3072:3072 + 128],
                        VT[:, (hh * 20 + s) * 65:(hh * 20 + s) * 65 + 65],
                        PT[:, slot * 128:slot * 128 + 128],
                        start=(c == 0), stop=(c == 4)), "pv")
                p.wait("vector", "pv", p.cnt("pv"))
                p.ins("vector", lambda e: e.tensor_copy(
                    R2[0:1, 0:128], pb[64:65, 3072:3072 + 128]))
                p.ins("vector", lambda e: e.reciprocal(
                    R2[0:1, 0:128], R2[0:1, 0:128]))
                ev = p.ins("vector", lambda e: e.tensor_copy(
                    SQ[64:128, 0:128], pb[0:64, 3072:3072 + 128]), "ev")
                p.banks_done([6], "ev", ev)
                p.wait("tensor", "ev", ev)
                p.banks_req([7])
                mmc = p.ins("tensor", lambda e: e.matmul(
                    pb[0:64, 3584:3584 + 128], aux["onesr_f"][0:1, 0:64],
                    R2[0:1, 0:128], start=True, stop=True), "mm")
                p.wait("vector", "mm", mmc)
                ev = p.ins("vector", lambda e, hc=hc, base=base, qt=qt:
                           e.tensor_mul(
                    AO[base:base + 64, hc * T + qt * 128:
                       hc * T + (qt + 1) * 128],
                    SQ[64:128, 0:128], pb[0:64, 3584:3584 + 128]), "ev")
                p.banks_done([7], "ev", ev)

    stbuf = {0: 0, 1: 0}
    XB2 = af.take(2 * 512)
    YB2 = af.take(2 * 512)
    p.wait("tensor", "ev", p.cnt("ev"))
    k_it = 0
    for m in range(NCH):
        wbuf = gq[0] % 2
        wq_guard()
        wc = p.dma("gpsimd", WBUF[:, wbuf * 1024:(wbuf + 1) * 1024], wo[m],
                   f"wq{wbuf}")
        p.wait("tensor", f"wq{wbuf}", wc)
        for nt in range(4):
            xb = k_it % 2
            p.wait("sync", f"st{xb}", stbuf[xb])
            lc = p.dma("sync", XB2[:, xb * 512:(xb + 1) * 512],
                       xown[:, m * T + nt * 512:m * T + (nt + 1) * 512],
                       f"xld{xb}")
            bkk = (m * 4 + nt) % 4
            p.banks_req([bkk])
            for c in range(NCH):
                mmc = p.ins("tensor", lambda e, m=m, nt=nt, c=c, bkk=bkk,
                            wbuf=wbuf: e.matmul(
                    pb[:, bkk * 512:bkk * 512 + 512],
                    WBUF[:, wbuf * 1024 + c * 128:wbuf * 1024 + c * 128 + 128],
                    AO[:, c * T + nt * 512:c * T + (nt + 1) * 512],
                    start=(c == 0), stop=(c == NCH - 1)),
                    "mm" if c == NCH - 1 else None)
            p.wait("vector", "mm", mmc)
            p.wait("vector", f"xld{xb}", lc)
            p.wait("vector", f"st{xb}", stbuf[xb])
            p.ins("vector", lambda e, m=m, bkk=bkk: e.tensor_scalar_add(
                pb[:, bkk * 512:bkk * 512 + 512],
                pb[:, bkk * 512:bkk * 512 + 512], smalls["bo"][:, m:m + 1]))
            ev = p.ins("vector", lambda e, bkk=bkk, xb=xb: e.tensor_add(
                YB2[:, xb * 512:(xb + 1) * 512],
                pb[:, bkk * 512:bkk * 512 + 512],
                XB2[:, xb * 512:(xb + 1) * 512]), "ev")
            p.banks_done([bkk], "ev", ev)
            p.wait("sync", "ev", ev)
            stbuf[xb] = p.dma(
                "sync", yout[:, m * T + nt * 512:m * T + (nt + 1) * 512],
                YB2[:, xb * 512:(xb + 1) * 512], f"st{xb}")
            k_it += 1
        wq_done()
    p.wait("sync", "st0", p.cnt("st0"))
    p.wait("sync", "st1", p.cnt("st1"))
    return p




def l1_attn_a_inputs(inputs):
    g = lambda n: np.asarray(inputs["L1_attn_a_" + n])
    cw = lambda w: np.ascontiguousarray(
        w.reshape(NCH, 128, 4).transpose(1, 0, 2).reshape(128, NCH * 4))
    d = {
        "wq": pack_wio(np.ascontiguousarray(g("Wq").T)),
        "wk": pack_wio(np.ascontiguousarray(g("Wk").T)),
        "wv": pack_wio(np.ascontiguousarray(g("Wv").T)),
        "wo": pack_wio(np.ascontiguousarray(g("Wo").T)),
        "bq": col_pack(g("bq")), "bk": col_pack(g("bk")),
        "bv": col_pack(g("bv")), "bo": col_pack(g("bo")),
        "cqw": cw(g("cqw")), "ckw": cw(g("ckw")), "cvw": cw(g("cvw")),
        "cqb": col_pack(g("cqb")), "ckb": col_pack(g("ckb")),
        "cvb": col_pack(g("cvb")),
    }
    hsel2 = np.zeros((128, 2), np.float32)
    hsel2[0:64, 0] = 1; hsel2[64:128, 1] = 1
    hsel2t = np.zeros((2, 128), np.float32)
    hsel2t[0, 0:64] = 1; hsel2t[1, 64:128] = 1
    d["hsel2"] = hsel2; d["hsel2t"] = hsel2t
    p_ = np.arange(128)[:, None]
    f_ = np.arange(128)[None, :]
    stripd = np.zeros((128, 256), BF)
    stripd[:, 0:128] = (f_ < p_)
    stripd[:, 128:256] = (f_ >= p_)
    d["stripd"] = stripd
    d.update(host_aux())
    invf = 1.0 / 10000.0 ** (np.arange(0, 64, 2, dtype=np.float64) / 64)
    percore = []
    for h in range(2):
        bpos = np.arange(1024, 4096)
        gpos = np.where(bpos < 2048, (1 - h) * T + bpos,
                        h * T + (bpos - 2048)).astype(np.float64)
        fr = gpos[:, None] * invf[None, :]
        emb = np.concatenate([fr, fr], axis=1)
        cos = np.cos(emb).astype(np.float32)
        sin = np.sin(emb).astype(np.float32)
        tc = np.tile(cos.T, (2, 1))
        tsn = np.tile(sin.T, (2, 1))
        bt = np.full((128, 8), -0.125, np.float32)
        if h == 0:
            bt[:, 0:4] = -0.125 - 1.0e30
        percore.append({"tcos": np.ascontiguousarray(tc),
                        "tsin": np.ascontiguousarray(tsn),
                        "biastab": bt,
                        "hflag": np.full((128, 1), float(h), np.float32)})
    return d, percore


def run_l1_attn_a(x, inputs):
    nc = get_prog("l1_attn_a", build_l1_attn_a)
    d, percore = l1_attn_a_inputs(inputs)
    in_maps = []
    for c in range(NCORES):
        b, h = c // 2, c % 2
        m = dict(d)
        m.update(percore[h])
        xrot = np.concatenate([x[b, (1 - h) * T:(2 - h) * T],
                               x[b, h * T:(h + 1) * T]], axis=0)
        m["xbuf"] = fm_pack(xrot[1024:4096])
        m["xown"] = fm_pack(x[b, h * T:(h + 1) * T])
        in_maps.append(m)
    res = run_spmd_cached(nc, in_maps, core_ids=list(range(NCORES)))
    out = np.empty_like(x)
    for c in range(NCORES):
        b, h = c // 2, c % 2
        out[b, h * T:(h + 1) * T] = fm_unpack(res.results[c]["yfm"], T)
    return out




L0F_KT = 2560
L0F_QT0 = 384
L0F_NQT = 17
L0F_AOW = L0F_NQT * 128


def build_l0f_pre():
    p = Prog("l0f_pre")
    xi = p.inp("xi", [128, NCH * 2048])
    yo = p.outp("yo", [128, NCH * 2048])
    aux = emit_aux(p)
    pb = p.psum("pb", [128, 4096], F32)
    af = Arena(p, "AF", 22100, F32)
    XI = af.take(NCH * 2048)
    lc = p.dma("sync", XI[:, 0:NCH * 2048], xi, "ld")
    p.wait("scalar", "ld", lc)
    p.wait("tensor", "ld", lc)
    p.wait("vector", "ld", lc)
    for half in range(2):
        t0 = half * 1024
        ev = emit_ln(p, aux,
                     lambda c, a, b, t0=t0: XI[:, c * 2048 + t0 + a:
                                               c * 2048 + t0 + b],
                     lambda c, t0=t0: XI[:, c * 2048 + t0:c * 2048 + t0 + 1024],
                     1024, pb, af, "ln")
    p.wait("sync", "ev", p.cnt("ev"))
    p.dma("sync", yo, XI[:, 0:NCH * 2048], "st")
    p.wait("sync", "st", p.cnt("st"))
    return p


def build_l0f_attn():
    p = Prog("l0f_attn")
    KT = L0F_KT
    xo = p.inp("xo", [128, NCH * 2050])
    xh = p.inp("xh", [128, NCH * 512])
    wqkv = p.inp("wqkv", [NCH, 3, 128, NCH * 128])
    wo = p.inp("wo", [NCH, 128, NCH * 128])
    qs = p.inp("qs", [128, 1])
    ks = p.inp("ks", [128, 1])
    epsc = p.inp("epsc", [128, 1])
    tcq = p.inp("tcq", [128, KT])
    tsq = p.inp("tsq", [128, KT])
    tck = p.inp("tck", [128, KT])
    tsk = p.inp("tsk", [128, KT])
    perm = p.inp("perm", [128, 128])
    hs2 = p.inp("hs2", [128, 2])
    hs2t = p.inp("hs2t", [2, 128])
    maskA = p.inp("maskA", [128, 128])
    maskB = p.inp("maskB", [128, 128])
    bval = p.inp("bval", [128, 8])
    yout = p.outp("yfm", [128, NCH * 2049])

    aux = emit_aux(p)
    pb = p.psum("pb", [128, 4096], F32)
    ah = Arena(p, "AH", 51400, BF16)
    af = Arena(p, "AF", 14900, F32)

    XT = ah.take(NCH * KT)
    QKp = ah.take(2 * KT)
    K65 = ah.take(KT)
    Q65 = ah.take(2 * 128)
    PTL = ah.take(2 * 128)
    VT = ah.take(2 * 20 * 65)
    WB = ah.take(2 * 1024)
    AO = ah.take(NCH * L0F_AOW)

    TCQ = af.take(KT)
    TSQ = af.take(KT)
    TCK = af.take(KT)
    TSK = af.take(KT)
    PRS = af.take(512)
    SQ = af.take(512)
    XR = af.take(512)
    VSL = af.take(512)
    SC = af.take(2 * 384)
    NB = af.take(128)
    MM = af.take(144)
    XRS = af.take(2 * 512)
    MKA = af.take(128)
    MKB = af.take(128)
    PERMS = p.sbuf("perms", [128, 128], F32)
    qscol = p.sbuf("qscol", [128, 1], F32)
    kscol = p.sbuf("kscol", [128, 1], F32)
    epss = p.sbuf("epss", [128, 1], F32)
    bvals = p.sbuf("bvals", [128, 8], F32)
    hs2s = p.sbuf("hs2s", [128, 2], F32)
    hs2ts = p.sbuf("hs2ts", [2, 128], F32)

    for dst, srcap in ((TCQ, tcq), (TSQ, tsq), (TCK, tck), (TSK, tsk)):
        p.dma("sync", dst[:, 0:KT], srcap, "ld")
    p.dma("sync", MKA[:, 0:128], maskA, "ld")
    p.dma("sync", MKB[:, 0:128], maskB, "ld")
    p.dma("sync", PERMS[:], perm, "ld")
    p.dma("sync", qscol[:], qs, "ld")
    p.dma("sync", kscol[:], ks, "ld")
    p.dma("sync", epss[:], epsc, "ld")
    p.dma("sync", bvals[:], bval, "ld")
    p.dma("sync", hs2s[:], hs2, "ld")
    lc = p.dma("sync", hs2ts[:], hs2t, "ld")
    for c in range(NCH):
        s = 0 if c < 4 else 1
        p.dma("gpsimd", XT[:, c * KT:c * KT + 510 + s],
              xh[:, c * 512 + 2 - s:(c + 1) * 512], "xt")
        xc = p.dma("gpsimd", XT[:, c * KT + 510 + s:(c + 1) * KT],
                   xo[:, c * 2050:c * 2050 + 2050 - s], "xt")
    for eng in ("tensor", "vector", "scalar"):
        p.wait(eng, "ld", lc)
    p.wait("tensor", "xt", xc)
    for t in range(40):
        p.ins("vector", lambda e, t=t: e.memset(
            VT[:, t * 65 + 64:t * 65 + 65], 1.0))
    ev = p.ins("vector", lambda e: e.memset(K65[64:65, 0:KT], 1.0), "ev")

    gwq = {}
    gq = [0]

    def wq_guard():
        p.wait("gpsimd", "mm", gwq.get(gq[0] - 2, 0))

    def wq_done():
        gwq[gq[0]] = p.cnt("mm")
        gq[0] += 1

    for hc in range(NCH):
        for ti in (2, 0, 1):
            wbuf = gq[0] % 2
            wq_guard()
            wc = p.dma("gpsimd", WB[:, wbuf * 1024:(wbuf + 1) * 1024],
                       wqkv[hc, ti], f"wq{wbuf}")
            p.wait("tensor", f"wq{wbuf}", wc)
            for nt in range(5):
                pbk = nt % 2
                p.banks_req([pbk])
                for c in range(NCH):
                    mmc = p.ins("tensor", lambda e, nt=nt, c=c, pbk=pbk,
                                wbuf=wbuf: e.matmul(
                        pb[:, pbk * 512:pbk * 512 + 512],
                        WB[:, wbuf * 1024 + c * 128:
                           wbuf * 1024 + c * 128 + 128],
                        XT[:, c * KT + nt * 512:c * KT + (nt + 1) * 512],
                        start=(c == 0), stop=(c == NCH - 1)),
                        "mm" if c == NCH - 1 else None)
                p.wait("scalar", "mm", mmc)
                if ti == 2:
                    ac = p.ins("scalar", lambda e, pbk=pbk: e.activation(
                        VSL[:, 0:512], pb[:, pbk * 512:pbk * 512 + 512],
                        AF.Copy), "ac")
                    p.banks_done([pbk], "ac", ac)
                    p.wait("tensor", "ac", ac)
                    p.banks_req([2])
                    for t4 in range(4):
                        mmc = p.ins("tensor", lambda e, t4=t4: e.transpose(
                            pb[:, 1024 + t4 * 128:1024 + (t4 + 1) * 128],
                            VSL[:, t4 * 128:(t4 + 1) * 128],
                            aux["eye_f"][:]), "mm")
                    p.wait("vector", "mm", mmc)
                    for t4 in range(4):
                        tt = nt * 4 + t4
                        p.ins("vector", lambda e, t4=t4, tt=tt: e.tensor_copy(
                            VT[:, tt * 65:tt * 65 + 64],
                            pb[:, 1024 + t4 * 128:1024 + t4 * 128 + 64]))
                        ev = p.ins("vector", lambda e, t4=t4, tt=tt:
                                   e.tensor_copy(
                            VT[:, (20 + tt) * 65:(20 + tt) * 65 + 64],
                            pb[:, 1024 + t4 * 128 + 64:
                               1024 + (t4 + 1) * 128]), "ev")
                    p.banks_done([2], "ev", ev)
                else:
                    dsto = (0 if ti == 0 else KT) + nt * 512
                    scol = qscol if ti == 0 else kscol
                    TC = TCQ if ti == 0 else TCK
                    TS = TSQ if ti == 0 else TSK
                    b3 = 2 + 3 * (nt % 2)
                    p.wait("scalar", "ev", p.cnt("ev"))
                    ac = p.ins("scalar", lambda e, pbk=pbk: e.activation(
                        SQ[:, 0:512], pb[:, pbk * 512:pbk * 512 + 512],
                        AF.Square), "ac")
                    ac = p.ins("scalar", lambda e, pbk=pbk: e.activation(
                        PRS[:, 0:512], pb[:, pbk * 512:pbk * 512 + 512],
                        AF.Copy), "ac")
                    p.banks_done([pbk], "ac", ac)
                    p.wait("tensor", "ac", ac)
                    p.banks_req([b3])
                    mmc = p.ins("tensor", lambda e, b3=b3: e.matmul(
                        pb[0:2, b3 * 512:b3 * 512 + 512], hs2s[:],
                        SQ[:, 0:512], start=True, stop=True), "mm")
                    p.wait("scalar", "mm", mmc)
                    ac = p.ins("scalar", lambda e, b3=b3: e.activation(
                        SQ[0:2, 0:512], pb[0:2, b3 * 512:b3 * 512 + 512],
                        AF.Sqrt, bias=epss[0:2, 0:1]), "ac")
                    p.banks_done([b3], "ac", ac)
                    p.wait("vector", "ac", ac)
                    ev = p.ins("vector", lambda e: e.reciprocal(
                        SQ[0:2, 0:512], SQ[0:2, 0:512]), "ev")
                    p.wait("tensor", "ev", ev)
                    p.banks_req([b3 + 1])
                    mmc = p.ins("tensor", lambda e, b3=b3: e.matmul(
                        pb[:, (b3 + 1) * 512:(b3 + 1) * 512 + 512],
                        hs2ts[:], SQ[0:2, 0:512], start=True, stop=True),
                        "mm")
                    p.wait("vector", "mm", mmc)
                    p.ins("vector", lambda e, b3=b3, scol=scol:
                          e.scalar_tensor_tensor(
                        SQ[:, 0:512], PRS[:, 0:512], scol[:, 0:1],
                        pb[:, (b3 + 1) * 512:(b3 + 1) * 512 + 512],
                        ALU.mult, ALU.mult))
                    ev = p.ins("vector", lambda e, TS=TS, nt=nt:
                               e.tensor_mul(
                        XR[:, 0:512], SQ[:, 0:512],
                        TS[:, nt * 512:(nt + 1) * 512]), "ev")
                    p.banks_done([b3 + 1], "ev", ev)
                    p.wait("tensor", "ev", ev)
                    p.banks_req([b3 + 2])
                    mmc = p.ins("tensor", lambda e, b3=b3: e.matmul(
                        pb[:, (b3 + 2) * 512:(b3 + 2) * 512 + 512],
                        PERMS[:], XR[:, 0:512], start=True, stop=True), "mm")
                    p.ins("vector", lambda e, TC=TC, nt=nt: e.tensor_mul(
                        PRS[:, 0:512], SQ[:, 0:512],
                        TC[:, nt * 512:(nt + 1) * 512]))
                    p.wait("vector", "mm", mmc)
                    ev = p.ins("vector", lambda e, b3=b3, dsto=dsto:
                               e.tensor_add(
                        QKp[:, dsto:dsto + 512], PRS[:, 0:512],
                        pb[:, (b3 + 2) * 512:(b3 + 2) * 512 + 512]), "ev")
                    p.banks_done([b3 + 2], "ev", ev)
            wq_done()
        for hh in range(2):
            base = hh * 64
            p.wait("vector", "mm", p.cnt("mm"))
            ev = p.ins("vector", lambda e, base=base: e.tensor_copy(
                K65[0:64, 0:KT], QKp[base:base + 64, KT:2 * KT]), "ev")
            p.wait("tensor", "ev", ev)
            for qt in range(L0F_NQT):
                ks_ = 256 * ((qt + 1) // 2)
                ac0 = 1 if qt % 2 == 0 else 0
                sbk = qt % 2
                qb = qt % 2
                p.banks_req([sbk])
                mmc = p.ins("tensor", lambda e, qt=qt, ks_=ks_, ac0=ac0,
                            base=base, sbk=sbk: e.matmul(
                    pb[:, sbk * 512:sbk * 512 + 384],
                    QKp[base:base + 64,
                        L0F_QT0 + qt * 128:L0F_QT0 + (qt + 1) * 128],
                    QKp[base:base + 64,
                        KT + ks_ + ac0 * 128:KT + ks_ + ac0 * 128 + 384],
                    start=True, stop=True), "mm")
                p.wait("vector", "mm", mmc)
                p.ins("vector", lambda e, sbk=sbk: e.tensor_add(
                    pb[:, sbk * 512:sbk * 512 + 128],
                    pb[:, sbk * 512:sbk * 512 + 128], MKA[:, 0:128]))
                p.ins("vector", lambda e, sbk=sbk: e.tensor_add(
                    pb[:, sbk * 512 + 256:sbk * 512 + 384],
                    pb[:, sbk * 512 + 256:sbk * 512 + 384], MKB[:, 0:128]))
                p.ins("vector", lambda e, sbk=sbk: e.tensor_copy(
                    SC[:, sbk * 384:sbk * 384 + 384],
                    pb[:, sbk * 512:sbk * 512 + 384]))
                p.ins("vector", lambda e, sbk=sbk: e.tensor_reduce(
                    MM[:, sbk:sbk + 1], SC[:, sbk * 384:sbk * 384 + 384],
                    AX.X, ALU.max))
                ev = p.ins("vector", lambda e, sbk=sbk: e.tensor_scalar_mul(
                    MM[:, sbk:sbk + 1], MM[:, sbk:sbk + 1], -1.0), "ev")
                p.banks_done([sbk], "ev", ev)
                p.wait("tensor", "ev", ev)
                p.banks_req([2])
                mmc = p.ins("tensor", lambda e, sbk=sbk: e.transpose(
                    pb[0:1, 1024:1024 + 128], MM[:, sbk:sbk + 1],
                    aux["eye_f"][:]), "mm")
                p.wait("vector", "mm", mmc)
                p.ins("vector", lambda e, qb=qb: e.tensor_copy(
                    Q65[64:65, qb * 128:qb * 128 + 128],
                    pb[0:1, 1024:1024 + 128]))
                ev = p.ins("vector", lambda e, qb=qb, base=base, qt=qt:
                           e.tensor_copy(
                    Q65[0:64, qb * 128:qb * 128 + 128],
                    QKp[base:base + 64,
                        L0F_QT0 + qt * 128:L0F_QT0 + (qt + 1) * 128]), "ev")
                p.banks_done([2], "ev", ev)
                p.wait("tensor", "ev", ev)
                pv0 = p.cnt("pv")
                for c3 in range(3):
                    bkk = 4 + c3 % 2
                    sidx = min(ks_ // 128 + ac0 + c3, 7)
                    p.banks_req([bkk])
                    mmc = p.ins("tensor", lambda e, c3=c3, ks_=ks_, ac0=ac0,
                                bkk=bkk, qb=qb: e.matmul(
                        pb[:, bkk * 512:bkk * 512 + 128],
                        K65[0:65, ks_ + (ac0 + c3) * 128:
                            ks_ + (ac0 + c3) * 128 + 128],
                        Q65[0:65, qb * 128:qb * 128 + 128],
                        start=True, stop=True), "mm")
                    p.wait("scalar", "mm", mmc)
                    if c3 != 1:
                        p.wait("vector", "mm", mmc)
                        msk = MKB if c3 == 0 else MKA
                        sv = p.ins("vector", lambda e, bkk=bkk, msk=msk:
                                   e.tensor_add(
                            pb[:, bkk * 512:bkk * 512 + 128],
                            pb[:, bkk * 512:bkk * 512 + 128],
                            msk[:, 0:128]), "sv")
                        p.wait("scalar", "sv", sv)
                    p.wait("scalar", "pv", pv0 + 1 if c3 == 2 else pv0)
                    ac = p.ins("scalar", lambda e, c3=c3, bkk=bkk,
                               sidx=sidx: e.activation(
                        PTL[:, (c3 % 2) * 128:(c3 % 2) * 128 + 128],
                        pb[:, bkk * 512:bkk * 512 + 128], AF.Exp,
                        scale=8.0, bias=bvals[:, sidx:sidx + 1]), "ac")
                    p.banks_done([bkk], "ac", ac)
                    p.wait("tensor", "ac", ac)
                    if c3 == 0:
                        p.banks_req([6])
                    p.ins("tensor", lambda e, c3=c3, ks_=ks_, ac0=ac0,
                          hh=hh: e.matmul(
                        pb[0:65, 3072:3072 + 128],
                        VT[:, (hh * 20 + ks_ // 128 + ac0 + c3) * 65:
                           (hh * 20 + ks_ // 128 + ac0 + c3) * 65 + 65],
                        PTL[:, (c3 % 2) * 128:(c3 % 2) * 128 + 128],
                        start=(c3 == 0), stop=(c3 == 2)), "pv")
                p.wait("vector", "pv", p.cnt("pv"))
                p.ins("vector", lambda e: e.tensor_copy(
                    MM[0:1, 8:136], pb[64:65, 3072:3072 + 128]))
                p.ins("vector", lambda e: e.tensor_scalar_add(
                    MM[0:1, 8:136], MM[0:1, 8:136], 1e-30))
                p.ins("vector", lambda e: e.reciprocal(
                    MM[0:1, 8:136], MM[0:1, 8:136]))
                ev = p.ins("vector", lambda e: e.tensor_copy(
                    NB[64:128, 0:128], pb[0:64, 3072:3072 + 128]), "ev")
                p.banks_done([6], "ev", ev)
                p.wait("tensor", "ev", ev)
                p.banks_req([7])
                mmc = p.ins("tensor", lambda e: e.matmul(
                    pb[0:64, 3584:3584 + 128], aux["onesr_f"][0:1, 0:64],
                    MM[0:1, 8:136], start=True, stop=True), "mm")
                p.wait("vector", "mm", mmc)
                ev = p.ins("vector", lambda e, hc=hc, base=base, qt=qt:
                           e.tensor_mul(
                    AO[base:base + 64,
                       hc * L0F_AOW + qt * 128:hc * L0F_AOW + qt * 128 + 128],
                    NB[64:128, 0:128], pb[0:64, 3584:3584 + 128]), "ev")
                p.banks_done([7], "ev", ev)

    p.wait("tensor", "ev", p.cnt("ev"))
    stbuf = {0: 0, 1: 0}
    k_it = 0
    for m in range(NCH):
        wbuf = gq[0] % 2
        wq_guard()
        wc = p.dma("gpsimd", WB[:, wbuf * 1024:(wbuf + 1) * 1024], wo[m],
                   f"wq{wbuf}")
        p.wait("tensor", f"wq{wbuf}", wc)
        for nt in range(5):
            a0 = nt * 512 if nt < 4 else 2048
            nn = 512 if nt < 4 else 128
            lo = 127 if nt == 0 else 0
            xb = k_it % 2
            p.wait("sync", f"st{xb}", stbuf[xb])
            lc = p.dma("sync", XRS[:, xb * 512:xb * 512 + nn - lo],
                       xo[:, m * 2050 + a0 + lo - 126:
                          m * 2050 + a0 + nn - 126], f"xld{xb}")
            bkk = k_it % 4
            p.banks_req([bkk])
            for c in range(NCH):
                mmc = p.ins("tensor", lambda e, m=m, c=c, bkk=bkk, wbuf=wbuf,
                            a0=a0, nn=nn: e.matmul(
                    pb[:, bkk * 512:bkk * 512 + nn],
                    WB[:, wbuf * 1024 + c * 128:wbuf * 1024 + c * 128 + 128],
                    AO[:, c * L0F_AOW + a0:c * L0F_AOW + a0 + nn],
                    start=(c == 0), stop=(c == NCH - 1)),
                    "mm" if c == NCH - 1 else None)
            p.wait("vector", "mm", mmc)
            p.wait("vector", f"xld{xb}", lc)
            ev = p.ins("vector", lambda e, bkk=bkk, xb=xb, lo=lo, nn=nn:
                       e.tensor_add(
                XRS[:, xb * 512:xb * 512 + nn - lo],
                pb[:, bkk * 512 + lo:bkk * 512 + nn],
                XRS[:, xb * 512:xb * 512 + nn - lo]), "ev")
            p.banks_done([bkk], "ev", ev)
            p.wait("sync", "ev", ev)
            stbuf[xb] = p.dma(
                "sync",
                yout[:, m * 2049 + a0 + lo - 127:m * 2049 + a0 + nn - 127],
                XRS[:, xb * 512:xb * 512 + nn - lo], f"st{xb}")
            k_it += 1
        wq_done()
    p.wait("sync", "st0", p.cnt("st0"))
    p.wait("sync", "st1", p.cnt("st1"))
    return p


def build_l0f_ffn(addres=False):
    p = Prog("l0f_ffn" + ("_r" if addres else ""))
    NMH = 22
    xi = p.inp("xi", [128, NCH * 2049])
    w1a = p.inp("w1a", [NMH, 128, NCH * 128])
    w1g = p.inp("w1g", [NMH, 128, NCH * 128])
    w2 = p.inp("w2", [NCH, 128, NMH * 128])
    b1a = p.inp("b1a", [128, NMH])
    b1g = p.inp("b1g", [128, NMH])
    b2 = p.inp("b2", [128, NCH])
    lng = p.inp("lng", [128, NCH])
    lnb = p.inp("lnb", [128, NCH])
    if addres:
        xres = p.inp("xres", [128, NCH * 2048])
    yo = p.outp("yo", [128, NCH * 2048])

    aux = emit_aux(p)
    pb = p.psum("pb", [128, 4096], F32)
    ah = Arena(p, "AH", 54300, BF16)
    af = Arena(p, "AF", 21600, F32)

    XN = ah.take(NCH * 2048)
    H = ah.take(NMH * 512)
    W2R = ah.take(NCH * NMH * 128)
    WA = ah.take(2 * 1024)
    WG = ah.take(2 * 1024)
    XI = af.take(NCH * 2049)
    GELF = af.take(512)
    YS = af.take(2 * 512)
    XR2 = af.take(2 * 512)
    bcols = p.sbuf("bcols", [128, 3 * NMH + 3 * NCH], F32)
    B1A, B1G = 0, NMH
    B2C, LNG, LNB = 3 * NMH, 3 * NMH + NCH, 3 * NMH + 2 * NCH

    lc = p.dma("sync", XI[:, 0:NCH * 2049], xi, "ld")
    p.dma("sync", bcols[:, B1A:B1A + NMH], b1a, "ld")
    p.dma("sync", bcols[:, B1G:B1G + NMH], b1g, "ld")
    p.dma("sync", bcols[:, B2C:B2C + NCH], b2, "ld")
    p.dma("sync", bcols[:, LNG:LNG + NCH], lng, "ld")
    lc = p.dma("sync", bcols[:, LNB:LNB + NCH], lnb, "ld")
    for m in range(NCH):
        wl = p.dma("gpsimd", W2R[:, m * NMH * 128:(m + 1) * NMH * 128],
                   w2[m], "wld")
    for eng in ("tensor", "vector", "scalar"):
        p.wait(eng, "ld", lc)

    for tq in range(4):
        t0 = tq * 512
        ev = emit_ln(
            p, aux,
            lambda c, a, b, t0=t0: XI[:, c * 2049 + t0 + a +
                                      (1 if c < 4 else 0):
                                      c * 2049 + t0 + b +
                                      (1 if c < 4 else 0)],
            lambda c, t0=t0: XN[:, c * 2048 + t0:c * 2048 + t0 + 512],
            512, pb, af, "ln",
            g_col=lambda c: bcols[:, LNG + c:LNG + c + 1],
            b_col=lambda c: bcols[:, LNB + c:LNB + c + 1])

    gwq = {}
    gq = [0]

    def wq_guard():
        p.wait("gpsimd", "mm", gwq.get(gq[0] - 2, 0))

    def wq_done():
        gwq[gq[0]] = p.cnt("mm")
        gq[0] += 1

    p.wait("tensor", "wld", p.cnt("wld"))
    p.wait("tensor", "ev", p.cnt("ev"))
    stbuf = {0: 0, 1: 0}
    k_it = 0
    for tg in range(4):
        t0 = tg * 512
        for mh in range(NMH):
            wbuf = gq[0] % 2
            wq_guard()
            p.dma("gpsimd", WA[:, wbuf * 1024:(wbuf + 1) * 1024],
                  w1a[mh], f"wq{wbuf}")
            wc = p.dma("gpsimd", WG[:, wbuf * 1024:(wbuf + 1) * 1024],
                       w1g[mh], f"wq{wbuf}")
            p.wait("tensor", f"wq{wbuf}", wc)
            pa = (mh % 2) * 2
            pg = pa + 1
            p.banks_req([pa, pg])
            for c in range(NCH):
                p.ins("tensor", lambda e, c=c, pa=pa, wbuf=wbuf, t0=t0:
                      e.matmul(
                    pb[:, pa * 512:pa * 512 + 512],
                    WA[:, wbuf * 1024 + c * 128:wbuf * 1024 + c * 128 + 128],
                    XN[:, c * 2048 + t0:c * 2048 + t0 + 512],
                    start=(c == 0), stop=(c == NCH - 1)))
            for c in range(NCH):
                mmc = p.ins("tensor", lambda e, c=c, pg=pg, wbuf=wbuf,
                            t0=t0: e.matmul(
                    pb[:, pg * 512:pg * 512 + 512],
                    WG[:, wbuf * 1024 + c * 128:wbuf * 1024 + c * 128 + 128],
                    XN[:, c * 2048 + t0:c * 2048 + t0 + 512],
                    start=(c == 0), stop=(c == NCH - 1)),
                    "mm" if c == NCH - 1 else None)
            wq_done()
            p.wait("scalar", "mm", mmc)
            p.wait("scalar", "ev", p.cnt("ev"))
            ac = p.ins("scalar", lambda e, pg=pg, mh=mh: e.activation(
                GELF[:, 0:512], pb[:, pg * 512:pg * 512 + 512], AF.Gelu,
                bias=bcols[:, B1G + mh:B1G + mh + 1]), "ac")
            p.wait("vector", "ac", ac)
            ev = p.ins("vector", lambda e, pa=pa, mh=mh:
                       e.scalar_tensor_tensor(
                H[:, mh * 512:(mh + 1) * 512],
                pb[:, pa * 512:pa * 512 + 512],
                bcols[:, B1A + mh:B1A + mh + 1], GELF[:, 0:512],
                ALU.add, ALU.mult), "ev")
            p.banks_done([pa, pg], "ev", ev)
        p.wait("tensor", "ev", p.cnt("ev"))
        for m in range(NCH):
            ys = k_it % 2
            p.wait("sync", f"st{ys}", stbuf[ys])
            if addres:
                lc = p.dma("sync", XR2[:, ys * 512:(ys + 1) * 512],
                           xres[:, m * 2048 + t0:m * 2048 + t0 + 512],
                           f"xld{ys}")
            bkk = 4 + k_it % 4
            p.banks_req([bkk])
            for kc in range(NMH):
                mmc = p.ins("tensor", lambda e, m=m, kc=kc, bkk=bkk:
                            e.matmul(
                    pb[:, bkk * 512:bkk * 512 + 512],
                    W2R[:, m * NMH * 128 + kc * 128:
                        m * NMH * 128 + (kc + 1) * 128],
                    H[:, kc * 512:(kc + 1) * 512],
                    start=(kc == 0), stop=(kc == NMH - 1)),
                    "mm" if kc == NMH - 1 else None)
            p.wait("vector", "mm", mmc)
            p.wait("vector", f"st{ys}", stbuf[ys])
            sev = None if addres else "ev"
            ev = p.ins("vector", lambda e, m=m, bkk=bkk, t0=t0, ys=ys:
                       e.scalar_tensor_tensor(
                YS[:, ys * 512:(ys + 1) * 512],
                pb[:, bkk * 512:bkk * 512 + 512],
                bcols[:, B2C + m:B2C + m + 1],
                XI[:, m * 2049 + 1 + t0:m * 2049 + 1 + t0 + 512],
                ALU.add, ALU.add), sev)
            if addres:
                p.wait("vector", f"xld{ys}", lc)
                ev = p.ins("vector", lambda e, ys=ys: e.tensor_add(
                    YS[:, ys * 512:(ys + 1) * 512],
                    YS[:, ys * 512:(ys + 1) * 512],
                    XR2[:, ys * 512:(ys + 1) * 512]), "ev")
            p.banks_done([bkk], "ev", ev)
            p.wait("sync", "ev", ev)
            stbuf[ys] = p.dma(
                "sync", yo[:, m * 2048 + t0:m * 2048 + t0 + 512],
                YS[:, ys * 512:(ys + 1) * 512], f"st{ys}")
            k_it += 1
    p.wait("sync", "st0", p.cnt("st0"))
    p.wait("sync", "st1", p.cnt("st1"))
    return p




def l0f_weight_inputs(inputs):
    Wqkv = np.asarray(inputs["L0_ffn_Wqkv"], np.float32)
    qsc = np.asarray(inputs["L0_ffn_q_scale"], np.float32)
    ksc = np.asarray(inputs["L0_ffn_k_scale"], np.float32)
    Wo = np.asarray(inputs["L0_ffn_Wo"], np.float32)
    lng = np.asarray(inputs["L0_ffn_ln_g"], np.float32)
    lnb = np.asarray(inputs["L0_ffn_ln_b"], np.float32)
    W1 = np.asarray(inputs["L0_ffn_W1"], np.float32)
    b1 = np.asarray(inputs["L0_ffn_b1"], np.float32)
    W2 = np.asarray(inputs["L0_ffn_W2"], np.float32)
    b2 = np.asarray(inputs["L0_ffn_b2"], np.float32)
    layers = []
    for i in range(6):
        d = {}
        W = Wqkv[i]
        wq = W.reshape(8, 128, 3, 8, 128).transpose(3, 2, 1, 0, 4)
        d["wqkv"] = np.ascontiguousarray(
            wq.reshape(8, 3, 128, 1024)).astype(np.float32)
        d["wo"] = pack_wio(Wo[i])
        d["qs"] = np.tile(qsc[i], 2).reshape(128, 1).astype(np.float32)
        d["ks"] = np.tile(ksc[i], 2).reshape(128, 1).astype(np.float32)
        W1a = np.zeros((1024, 2816), np.float32)
        W1a[:, :2730] = W1[i][:, :2730]
        W1g = np.zeros((1024, 2816), np.float32)
        W1g[:, :2730] = W1[i][:, 2730:]
        W2p = np.zeros((2816, 1024), np.float32)
        W2p[:2730] = W2[i]
        d["w1a"] = pack_wio(W1a)
        d["w1g"] = pack_wio(W1g)
        d["w2"] = pack_wio(W2p)
        b1a = np.zeros(2816, np.float32)
        b1a[:2730] = b1[i][:2730]
        b1g = np.zeros(2816, np.float32)
        b1g[:2730] = b1[i][2730:]
        d["b1a"] = np.ascontiguousarray(b1a.reshape(22, 128).T)
        d["b1g"] = np.ascontiguousarray(b1g.reshape(22, 128).T)
        d["b2"] = col_pack(b2[i])
        d["lng"] = col_pack(lng[i])
        d["lnb"] = col_pack(lnb[i])
        layers.append(d)
    return layers


def l0f_attn_consts():
    i = np.arange(128)[:, None]
    j = np.arange(128)[None, :]
    shared = {
        "maskA": np.where(i > j, -1e30, 0).astype(np.float32),
        "maskB": np.where(i < j, -1e30, 0).astype(np.float32),
        "epsc": np.full((128, 1), 1e-24, np.float32),
    }
    perm = np.zeros((128, 128), np.float32)
    for d_ in range(128):
        if d_ % 64 < 32:
            perm[d_ + 32, d_] = -1.0
        else:
            perm[d_ - 32, d_] = 1.0
    shared["perm"] = perm
    hs2 = np.zeros((128, 2), np.float32)
    hs2[:64, 0] = 1.0
    hs2[64:, 1] = 1.0
    shared["hs2"] = hs2
    shared["hs2t"] = np.ascontiguousarray(hs2.T)
    half = np.arange(0, 64, 2, dtype=np.float32)
    invf = (1.0 / 10000.0 ** (half / 64)).astype(np.float32)
    basesc = ((half + 0.4 * 64) / (1.4 * 64)).astype(np.float32)
    per = {}
    for h in range(2):
        start = h * 2048
        t = np.arange(L0F_KT, dtype=np.float32) + np.float32(start - 512)
        fr = (t[:, None] * invf[None, :]).astype(np.float32)
        power = ((t - np.float32(2048.0)) / np.float32(128.0))
        sc = (basesc[None, :] ** power[:, None]).astype(np.float32)
        cos = np.cos(fr).astype(np.float32)
        sin = np.sin(fr).astype(np.float32)

        def expand(v):
            r = np.concatenate([v, v], axis=1)
            r = np.concatenate([r, r], axis=1)
            return np.ascontiguousarray(r.T.astype(np.float32))
        bv = np.zeros((128, 8), np.float32)
        if h == 0:
            bv[:, :4] = -1e30
        per[h] = {"tcq": expand(cos * sc), "tsq": expand(sin * sc),
                  "tck": expand(cos / sc), "tsk": expand(sin / sc),
                  "bval": bv}
    return shared, per


def run_l0f_pre(z):
    nc = get_prog("l0f_pre", build_l0f_pre)
    in_maps = []
    for c in range(NCORES):
        b, h = c // 2, c % 2
        in_maps.append({"xi": fm_pack(z[b, h * T:(h + 1) * T]),
                        **host_aux()})
    res = run_spmd_cached(nc, in_maps, core_ids=list(range(NCORES)))
    out = np.empty_like(z)
    for c in range(NCORES):
        b, h = c // 2, c % 2
        out[b, h * T:(h + 1) * T] = fm_unpack(res.results[c]["yo"], T)
    return out


def run_l0f_attn(z, wl, consts):
    nc = get_prog("l0f_attn", build_l0f_attn)
    shared, per = consts
    in_maps = []
    for c in range(NCORES):
        b, h = c // 2, c % 2
        start = h * T
        xo = np.zeros((4096, 1024), np.float32)[0:2050]
        lo = start - 2
        if lo >= 0:
            xo[:] = z[b, lo:start + 2048]
        else:
            xo[2:] = z[b, 0:2048]
        xh = np.zeros((512, 1024), np.float32)
        hlo = start - 514
        if hlo >= 0:
            xh[:] = z[b, hlo:start - 2]
        m = {"xo": fm_pack(xo), "xh": fm_pack(xh),
             "wqkv": wl["wqkv"], "wo": wl["wo"],
             "qs": wl["qs"], "ks": wl["ks"], **shared, **per[h],
             **host_aux()}
        in_maps.append(m)
    res = run_spmd_cached(nc, in_maps, core_ids=list(range(NCORES)))
    znew = np.empty_like(z)
    xi_list = []
    for c in range(NCORES):
        b, h = c // 2, c % 2
        y = res.results[c]["yfm"]
        xi_list.append(y)
        full = fm_unpack(np.ascontiguousarray(
            y.reshape(128, 8, 2049)[:, :, 1:].reshape(128, 8 * 2048)), T)
        znew[b, h * T:(h + 1) * T] = full
    return znew, xi_list


def run_l0f_ffn(xi_list, wl, zouter=None):
    addres = zouter is not None
    name = "l0f_ffn_r" if addres else "l0f_ffn"
    nc = get_prog(name, lambda: build_l0f_ffn(addres))
    in_maps = []
    for c in range(NCORES):
        b, h = c // 2, c % 2
        m = {"xi": xi_list[c], "w1a": wl["w1a"], "w1g": wl["w1g"],
             "w2": wl["w2"], "b1a": wl["b1a"], "b1g": wl["b1g"],
             "b2": wl["b2"], "lng": wl["lng"], "lnb": wl["lnb"],
             **host_aux()}
        if addres:
            m["xres"] = fm_pack(zouter[b, h * T:(h + 1) * T])
        in_maps.append(m)
    res = run_spmd_cached(nc, in_maps, core_ids=list(range(NCORES)))
    out = np.empty((4, 4096, 1024), np.float32)
    for c in range(NCORES):
        b, h = c // 2, c % 2
        out[b, h * T:(h + 1) * T] = fm_unpack(res.results[c]["yo"], T)
    return out


def run_l0f(x, inputs, layers=6, consts=None, wls=None):
    if wls is None:
        wls = l0f_weight_inputs(inputs)
    if consts is None:
        consts = l0f_attn_consts()
    z = run_l0f_pre(x)
    for i in range(layers):
        z, xi_list = run_l0f_attn(z, wls[i], consts)
        z = run_l0f_ffn(xi_list, wls[i], zouter=x if i == layers - 1 else None)
    return z


def kernel(**inputs):
    x = np.asarray(inputs["L0_attn_a_x"], dtype=np.float32)
    wls = l0f_weight_inputs(inputs)
    consts = l0f_attn_consts()
    x = run_attn_a(x, inputs)
    x = run_l0f(x, inputs, consts=consts, wls=wls)
    x = run_attn_b(x, inputs)
    x = run_l0f(x, inputs, consts=consts, wls=wls)
    x = run_l1_attn_a(x, inputs)
    x = run_l1ffn(x, inputs)
    x = run_l1_attn_b(x, inputs)
    x = run_l1ffn(x, inputs)
    return x


# version 05
# speedup vs baseline: 1.0132x; 1.0132x over previous
import contextlib
import numpy as np
import ml_dtypes
import concourse.bass as bass
import concourse.mybir as mybir
import concourse.bass_utils as _bass_utils
from concourse.bass_utils import run_bass_kernel_spmd



_JITS = {}


class _Res:
    def __init__(self, results):
        self.results = results


def run_spmd_cached(nc, in_maps, core_ids):
    return _bass_utils.run_bass_kernel_spmd(nc, in_maps, core_ids=core_ids)


def _run_spmd_cached_unused(nc, in_maps, core_ids):
    import jax
    from jax.sharding import Mesh, PartitionSpec
    from jax.experimental.shard_map import shard_map
    from concourse import bass2jax, mybir as _mb
    n_cores = len(core_ids)
    key = id(nc)
    if key not in _JITS:
        bass2jax.install_neuronx_cc_hook()
        assert nc.dbg_addr is None
        partition_name = (nc.partition_id_tensor.name
                          if nc.partition_id_tensor else None)
        in_names, out_names, out_avals, zero_shapes = [], [], [], []
        for alloc in nc.m.functions[0].allocations:
            if not isinstance(alloc, _mb.MemoryLocationSet):
                continue
            name = alloc.memorylocations[0].name
            if alloc.kind == "ExternalInput":
                if name != partition_name:
                    in_names.append(name)
            elif alloc.kind == "ExternalOutput":
                shape = tuple(alloc.tensor_shape)
                dtype = _mb.dt.np(alloc.dtype)
                out_avals.append(jax.core.ShapedArray(shape, dtype))
                out_names.append(name)
                zero_shapes.append((shape, dtype))
        n_params = len(in_names)
        n_outs = len(out_names)
        all_in = list(in_names) + list(out_names)
        if partition_name is not None:
            all_in.append(partition_name)

        def _body(*args):
            operands = list(args)
            if partition_name is not None:
                operands.append(bass2jax.partition_id_tensor())
            outs = bass2jax._bass_exec_p.bind(
                *operands,
                out_avals=tuple(out_avals),
                in_names=tuple(all_in),
                out_names=tuple(out_names),
                lowering_input_output_aliases=(),
                sim_require_finite=True,
                sim_require_nnan=True,
                nc=nc,
            )
            return tuple(outs)
        donate = tuple(range(n_params, n_params + n_outs))
        devices = jax.devices()[:n_cores]
        mesh = Mesh(np.asarray(devices), ("core",))
        sharded = jax.jit(
            shard_map(_body, mesh=mesh,
                      in_specs=(PartitionSpec("core"),) * (n_params + n_outs),
                      out_specs=(PartitionSpec("core"),) * n_outs,
                      check_rep=False),
            donate_argnums=donate, keep_unused=True)
        _JITS[key] = (sharded, in_names, out_names, out_avals, zero_shapes,
                      n_cores)
    sharded, in_names, out_names, out_avals, zero_shapes, nc_prev = _JITS[key]
    assert nc_prev == n_cores
    concat_in = [np.concatenate([np.asarray(in_maps[c][nm])
                                 for c in range(n_cores)], axis=0)
                 for nm in in_names]
    concat_zeros = [np.zeros((n_cores * s[0], *s[1:]), d)
                    for s, d in zero_shapes]
    out_arrs = sharded(*concat_in, *concat_zeros)
    results = [
        {nm: np.asarray(out_arrs[i]).reshape(
            n_cores, *out_avals[i].shape)[c]
         for i, nm in enumerate(out_names)}
        for c in range(n_cores)
    ]
    return _Res(results)

F32 = mybir.dt.float32
BF16 = mybir.dt.bfloat16
AF = mybir.ActivationFunctionType
ALU = mybir.AluOpType
AX = mybir.AxisListType
BF = ml_dtypes.bfloat16

NCORES = 8
T = 2048
D = 1024
NCH = 8
ENGINES = ("sync", "tensor", "vector", "scalar", "gpsimd")

NEG_BIG = -1.0e30



def fm_pack(x2d):
    t = x2d.shape[0]
    return np.ascontiguousarray(
        x2d.reshape(t, NCH, 128).transpose(2, 1, 0)).reshape(128, NCH * t)


def fm_unpack(xfm, t):
    return np.ascontiguousarray(
        xfm.reshape(128, NCH, t).transpose(2, 1, 0)).reshape(t, D)


def col_pack(v):
    return np.ascontiguousarray(v.reshape(NCH, 128).T)


def row_pack(v):
    return np.ascontiguousarray(v.reshape(NCH, 128))


def pack_wio(w, mcols=128):
    din, dout = w.shape
    kc = din // 128
    nm = dout // mcols
    return np.ascontiguousarray(
        w.reshape(kc, 128, nm, mcols).transpose(2, 1, 0, 3)
        .reshape(nm, 128, kc * mcols))




class Prog:
    def __init__(self, name="prog"):
        self.name = name
        self.nc = bass.Bass("TRN2", target_bir_lowering=False)
        self.st = contextlib.ExitStack()
        self.q = {e: [] for e in ENGINES}
        self.semh = {}
        self.semc = {}
        self._buf_names = set()
        self.bank_dep = [("ev", 0) for _ in range(8)]
        self._waited = {e: {} for e in ENGINES}
        self.sem_owner = {}

    def sem(self, name):
        if name not in self.semh:
            self.semh[name] = self.st.enter_context(self.nc.semaphore(name))
            self.semc[name] = 0
        return self.semh[name]

    def cnt(self, name):
        self.sem(name)
        return self.semc[name]

    def bump(self, name, n=1):
        self.sem(name)
        self.semc[name] += n
        return self.semc[name]

    def sbuf(self, name, shape, dtype):
        assert name not in self._buf_names, name
        self._buf_names.add(name)
        return self.st.enter_context(self.nc.sbuf_tensor(name, shape, dtype))

    def psum(self, name, shape, dtype):
        assert name not in self._buf_names, name
        self._buf_names.add(name)
        return self.st.enter_context(self.nc.psum_tensor(name, shape, dtype))

    def inp(self, name, shape, dtype=F32):
        return self.nc.dram_tensor(name, shape, dtype, kind="ExternalInput").ap()

    def outp(self, name, shape, dtype=F32):
        return self.nc.dram_tensor(name, shape, dtype, kind="ExternalOutput").ap()

    def dram(self, name, shape, dtype):
        return self.nc.dram_tensor(name, shape, dtype, kind="Internal")

    def on(self, eng, fn):
        self.q[eng].append(fn)

    def wait(self, eng, sname, val):
        if val <= 0:
            return
        if self._waited[eng].get(sname, -1) >= val:
            return
        self._waited[eng][sname] = val
        h = self.sem(sname)
        self.on(eng, lambda e, h=h, v=val: e.wait_ge(h, v))

    def _own(self, sname, eng):
        if sname in self.sem_owner:
            assert self.sem_owner[sname] == eng, \
                f"sem {sname} inc from {eng} and {self.sem_owner[sname]}"
        else:
            self.sem_owner[sname] = eng

    def dma(self, eng, dst, src, sname="dma"):
        self._own(sname, eng)
        h = self.sem(sname)
        c = self.bump(sname, 16)
        self.on(eng, lambda e, d=dst, s=src, h=h: e.dma_start(out=d, in_=s)
                .then_inc(h, 16))
        return c

    def ins(self, eng, fn, sname=None, n=1):
        if sname is not None:
            self._own(sname, eng)
        drain = eng in ("vector", "scalar")
        if sname is None:
            self.on(eng, fn)
            if drain:
                self.on(eng, lambda e: e.drain())
            return None
        h = self.sem(sname)
        c = self.bump(sname, n)
        if drain:
            self.on(eng, fn)
            self.on(eng, lambda e, h=h, n=n: e.drain().then_inc(h, n))
        else:
            self.on(eng, lambda e, f=fn, h=h, n=n: f(e).then_inc(h, n))
        return c

    def banks_req(self, bank_ids):
        need = {}
        for b in bank_ids:
            s, c = self.bank_dep[b]
            if c > need.get(s, 0):
                need[s] = c
        for s, c in need.items():
            self.wait("tensor", s, c)

    def banks_done(self, bank_ids, sname, cntv):
        for b in bank_ids:
            self.bank_dep[b] = (sname, cntv)

    def barrier(self):
        for eng in ENGINES:
            if not self.q[eng]:
                continue
            for s in list(self.semc):
                self.wait(eng, s, self.semc[s])

    def finalize(self):
        nc = self.nc
        with nc.Block() as block:
            for eng in ENGINES:
                if not self.q[eng]:
                    continue
                dec = getattr(block, eng)

                def runner(e, _q=tuple(self.q[eng])):
                    for f in _q:
                        f(e)
                dec(runner)
        self.st.close()
        return nc


class Arena:

    def __init__(self, p, name, cols, dtype):
        self.t = p.sbuf(name, [128, cols], dtype)
        self.cols = cols
        self.off = 0
        self.marks = []

    def take(self, cols):
        assert self.off + cols <= self.cols, \
            f"arena overflow: {self.off}+{cols} > {self.cols}"
        o = self.off
        self.off += cols
        t = self.t

        class Slice:
            def __getitem__(_, key):
                if isinstance(key, tuple) and len(key) == 2:
                    rows, csl = key
                    if isinstance(csl, slice):
                        a = csl.start or 0
                        b = csl.stop if csl.stop is not None else cols
                        step = csl.step
                        if step is None:
                            return t[rows, o + a:o + b]
                        return t[rows, o + a:o + b:step]
                raise KeyError(key)
        return Slice()

    def mark(self):
        self.marks.append(self.off)

    def release(self):
        self.off = self.marks.pop()




def emit_aux(p):
    aux = {}
    for nm, shape, dt in (
        ("eye_h", [128, 128], BF16),
        ("eye_f", [128, 128], F32),
        ("onesr_f", [1, 128], F32),
        ("onesr_h", [1, 128], BF16),
        ("onesc_f", [128, 1], F32),
        ("onesc_h", [128, 1], BF16),
    ):
        ap = p.inp("aux_" + nm, shape, dt)
        aux[nm] = p.sbuf("aux_" + nm + "_s", shape, dt)
        p.dma("sync", aux[nm][:], ap, "ld")
    return aux


def host_sel16():
    s = np.zeros((16, 16 * 128), np.float32)
    for e in range(16):
        s[e, e * 128:(e + 1) * 128] = 1.0
    return s


def host_aux():
    return {
        "aux_eye_h": np.eye(128, dtype=BF),
        "aux_eye_f": np.eye(128, dtype=np.float32),
        "aux_onesr_f": np.ones((1, 128), np.float32),
        "aux_onesr_h": np.ones((1, 128), BF),
        "aux_onesc_f": np.ones((128, 1), np.float32),
        "aux_onesc_h": np.ones((128, 1), BF),
    }




def emit_ln(p, aux, xap, xn_ap, nt, pb, scr, tag, g_col=None, b_col=None):
    scr.mark()
    sq = scr.take(nt)
    bA = scr.take(nt)
    bN = scr.take(nt)
    RW = scr.take(nt)
    RW2 = scr.take(nt)
    r_mu = RW[0:1, 0:nt]
    r_ms = RW[32:33, 0:nt]
    r_m2 = RW2[32:33, 0:nt]
    r_t = RW[64:65, 0:nt]
    r_rstd = RW2[0:1, 0:nt]
    r_nmr = r_mu

    ntile = nt // 512
    assert nt % 512 == 0 and ntile <= 4

    p.banks_req(range(8))
    if p.semc.get("ev", 0) > 0:
        p.wait("scalar", "ev", p.cnt("ev"))
    for c in range(NCH):
        ac = p.ins("scalar", lambda e, c=c: e.activation(
            sq[:, 0:nt], xap(c, 0, nt), AF.Square), "ac")
        for j in range(ntile):
            t0 = j * 512
            p.ins("tensor", lambda e, c=c, t0=t0: e.matmul(
                pb[0:1, t0:t0 + 512], aux["onesc_f"][:],
                xap(c, t0, t0 + 512), start=(c == 0), stop=(c == NCH - 1)))
        p.wait("tensor", "ac", ac)
        for j in range(ntile):
            t0 = j * 512
            mmc = p.ins("tensor", lambda e, c=c, t0=t0: e.matmul(
                pb[0:1, 2048 + t0:2048 + t0 + 512], aux["onesc_f"][:],
                sq[:, t0:t0 + 512], start=(c == 0), stop=(c == NCH - 1)), "mm")
        p.wait("scalar", "mm", mmc)
    p.wait("vector", "mm", p.cnt("mm"))
    p.ins("vector", lambda e: e.tensor_scalar_mul(
        r_mu, pb[0:1, 0:nt], 1.0 / D))
    p.ins("vector", lambda e: e.tensor_scalar_mul(
        r_ms, pb[0:1, 2048:2048 + nt], 1.0 / D))
    p.ins("vector", lambda e: e.tensor_mul(r_m2, r_mu, r_mu))
    p.ins("vector", lambda e: e.tensor_sub(r_t, r_ms, r_m2))
    ev = p.ins("vector", lambda e: e.tensor_scalar_add(r_t, r_t, 1e-5), "ev")
    p.wait("scalar", "ev", ev)
    ac = p.ins("scalar", lambda e: e.activation(r_t, r_t, AF.Sqrt), "ac")
    p.wait("vector", "ac", ac)
    p.ins("vector", lambda e: e.reciprocal(r_rstd, r_t))
    ev = p.ins("vector", lambda e: e.scalar_tensor_tensor(
        r_nmr, r_mu, -1.0, r_rstd, ALU.mult, ALU.mult), "ev")
    p.wait("tensor", "ev", ev)
    for j in range(ntile):
        t0 = j * 512
        p.ins("tensor", lambda e, t0=t0: e.matmul(
            pb[:, t0:t0 + 512], aux["onesr_f"][:],
            RW2[0:1, t0:t0 + 512], start=True, stop=True))
        mmc = p.ins("tensor", lambda e, t0=t0: e.matmul(
            pb[:, 2048 + t0:2048 + t0 + 512], aux["onesr_f"][:],
            RW[0:1, t0:t0 + 512], start=True, stop=True), "mm")
    p.wait("vector", "mm", mmc)
    p.ins("vector", lambda e: e.tensor_copy(bA[:, 0:nt], pb[:, 0:nt]))
    ev = p.ins("vector", lambda e: e.tensor_copy(
        bN[:, 0:nt], pb[:, 2048:2048 + nt]), "ev")
    p.banks_done(range(8), "ev", ev)
    for c in range(NCH):
        if g_col is None:
            p.ins("vector", lambda e, c=c: e.tensor_mul(
                sq[:, 0:nt], xap(c, 0, nt), bA[:, 0:nt]))
            ev = p.ins("vector", lambda e, c=c: e.tensor_add(
                xn_ap(c), sq[:, 0:nt], bN[:, 0:nt]), "ev")
        else:
            p.ins("vector", lambda e, c=c: e.scalar_tensor_tensor(
                sq[:, 0:nt], xap(c, 0, nt), g_col(c), bA[:, 0:nt],
                ALU.mult, ALU.mult))
            p.ins("vector", lambda e, c=c: e.scalar_tensor_tensor(
                sq[:, 0:nt], bN[:, 0:nt], g_col(c), sq[:, 0:nt],
                ALU.mult, ALU.add))
            ev = p.ins("vector", lambda e, c=c: e.tensor_scalar_add(
                xn_ap(c), sq[:, 0:nt], b_col(c)), "ev")
    scr.release()
    return ev




def wload_chunks(p, dst, kchunks, ncols, src_ap, wait_sem=None, wait_val=0,
                 sname="wld"):
    if wait_sem is not None and wait_val > 0:
        p.wait("gpsimd", wait_sem, wait_val)
    c = None
    for k in range(kchunks):
        c = p.dma("gpsimd", dst[:, k * ncols:(k + 1) * ncols],
                  src_ap[k * 128:(k + 1) * 128, :], sname)
    return c




def _trunc_finish(p, yout):
    p.barrier()
    c = p.dma("sync", yout, p.sbuf("truncz", [128, 16], F32)[:, 0:16], "tst")
    p.wait("sync", "tst", c)
    return p


def build_l1_ffn(debug=False, skip_shared_z=False, serial_p2=False,
                 stop_after=None):
    p = Prog("l1ffn")
    E = 16

    xin = p.inp("xfm", [128, NCH * T])
    wg = p.inp("wg", [D, E])
    ws1 = p.inp("ws1", [D, 2048])
    ws2 = p.inp("ws2", [1024, D])
    w1 = p.inp("w1", [E, D, 1024])
    w2 = p.inp("w2", [E, 512, D])
    yout = p.outp("yfm", [128, NCH * T])

    dbg = {}
    aux = emit_aux(p)
    sel16 = p.inp("sel16", [16, 16 * 128], F32)
    sel16s = p.sbuf("sel16s", [16, 16 * 128], F32)
    p.dma("sync", sel16s[:], sel16, "ld")
    X = p.sbuf("X", [128, NCH * T], F32)
    XN = p.sbuf("XN", [128, NCH * T], BF16)
    pb = p.psum("pb", [128, 4096], F32)
    ah = Arena(p, "AH", 25216, BF16)
    af = Arena(p, "AF", 10240, F32)
    ldc = p.dma("sync", X[:], xin, "ld")
    for eng in ("tensor", "scalar", "vector"):
        p.wait(eng, "ld", p.cnt("ld"))

    def xap(c, t0, t1):
        return X[:, c * T + t0:c * T + t1]

    def xnap(c, t0, t1):
        return XN[:, c * T + t0:c * T + t1]

    emit_ln(p, aux, xap, lambda c: XN[:, c * T:(c + 1) * T], T, pb, af, "o")

    ah.mark()
    af.mark()
    wgs = ah.take(NCH * E)
    wc = wload_chunks(p, wgs, NCH, E, wg)
    LGS = af.take(T)
    p.wait("tensor", "wld", wc)
    p.wait("tensor", "ev", p.cnt("ev"))
    p.banks_req(range(4))
    for c in range(NCH):
        for j in range(4):
            mmc = p.ins("tensor", lambda e, c=c, j=j: e.matmul(
                pb[0:16, j * 512:(j + 1) * 512],
                wgs[:, c * E:(c + 1) * E], xnap(c, j * 512, (j + 1) * 512),
                start=(c == 0), stop=(c == NCH - 1)),
                "mm" if (c == NCH - 1) else None)
    p.wait("vector", "mm", mmc)
    ev = p.ins("vector", lambda e: e.tensor_copy(LGS[0:16, 0:T], pb[0:16, 0:T]), "ev")
    p.banks_done(range(4), "ev", ev)

    CWFS = af.take(T)
    PRT = p.sbuf("PRT", [128, 16], F32)
    PM = p.sbuf("PM", [128, 16], F32)
    I1 = p.sbuf("I1", [128, 16], F32)
    I2 = p.sbuf("I2", [128, 16], F32)
    CWT = p.sbuf("CWT", [128, 16], F32)
    PT0 = p.sbuf("PT0", [128, 16], F32)
    if debug:
        CAP = p.sbuf("CAP", [128, 64], F32)
        CAP2 = p.sbuf("CAP2", [128, T], F32)
    mxc = p.sbuf("mxc", [128, 1], F32)
    smc = p.sbuf("smc", [128, 1], F32)
    p.wait("tensor", "ev", ev)
    for t in range(16):
        bank = 2048 + (t % 2) * 512
        bid = 4 + (t % 2)
        p.banks_req([bid])
        mmc = p.ins("tensor", lambda e, t=t, bank=bank: e.transpose(
            pb[:, bank:bank + 16], LGS[0:16, t * 128:(t + 1) * 128],
            aux["eye_f"][0:16, 0:16]), "mm")
        p.wait("vector", "mm", mmc)
        ev0 = p.ins("vector", lambda e, bank=bank: e.tensor_copy(
            PT0[:], pb[:, bank:bank + 16]), "ev")
        p.banks_done([bid], "ev", ev0)
        p.ins("vector", lambda e: e.tensor_reduce(
            mxc[:], PT0[:], AX.X, ALU.max))
        evn = p.ins("vector", lambda e: e.tensor_scalar_mul(
            mxc[:], mxc[:], -1.0), "ev")
        p.wait("scalar", "ev", evn)
        ac = p.ins("scalar", lambda e: e.activation(
            PRT[:], PT0[:], AF.Exp, bias=mxc[:, 0:1]), "ac")
        p.wait("vector", "ac", ac)
        p.ins("vector", lambda e: e.tensor_reduce(
            smc[:], PRT[:], AX.X, ALU.add))
        p.ins("vector", lambda e: e.reciprocal(smc[:], smc[:]))
        p.ins("vector", lambda e: e.tensor_scalar_mul(PRT[:], PRT[:],
                                                      smc[:, 0:1]))
        p.ins("vector", lambda e: e.tensor_reduce(
            mxc[:], PRT[:], AX.X, ALU.max))
        p.ins("vector", lambda e: e.tensor_scalar_mul(mxc[:], mxc[:], -1.0))
        p.ins("vector", lambda e: e.tensor_scalar(
            I1[:], PRT[:], mxc[:, 0:1], 0.0, ALU.add, ALU.is_ge))
        p.ins("vector", lambda e: e.tensor_scalar(
            PM[:], I1[:], -1.0, 1.0, ALU.mult, ALU.add))
        p.ins("vector", lambda e: e.tensor_mul(PM[:], PRT[:], PM[:]))
        p.ins("vector", lambda e: e.tensor_reduce(
            mxc[:], PM[:], AX.X, ALU.max))
        p.ins("vector", lambda e: e.tensor_scalar_mul(mxc[:], mxc[:], -1.0))
        p.ins("vector", lambda e: e.tensor_scalar(
            I2[:], PM[:], mxc[:, 0:1], 0.0, ALU.add, ALU.is_ge))
        p.ins("vector", lambda e: e.tensor_add(I1[:], I1[:], I2[:]))
        ev = p.ins("vector", lambda e: e.tensor_mul(CWT[:], PRT[:], I1[:]),
                   "ev")
        if debug and t == 0:
            p.ins("vector", lambda e: e.tensor_copy(CAP[:, 0:16], PT0[:]))
            p.ins("vector", lambda e: e.tensor_copy(CAP[:, 16:32], PRT[:]))
            p.ins("vector", lambda e: e.tensor_copy(CAP[:, 32:48], I1[:]))
            ev = p.ins("vector", lambda e: e.tensor_copy(CAP[:, 48:64],
                                                         CWT[:]), "ev")
        p.wait("tensor", "ev", ev)
        bank2 = 2048 + (t % 2) * 512 + 128
        bid2 = 4 + (t % 2)
        p.banks_req([bid2])
        mmc = p.ins("tensor", lambda e, bank2=bank2: e.transpose(
            pb[0:16, bank2:bank2 + 128], CWT[:], aux["eye_f"][:]), "mm")
        p.wait("vector", "mm", mmc)
        ev = p.ins("vector", lambda e, t=t, bank2=bank2: e.tensor_copy(
            CWFS[0:16, t * 128:(t + 1) * 128], pb[0:16, bank2:bank2 + 128]), "ev")
        p.banks_done([bid2], "ev", ev)

    if stop_after == "router":
        return _trunc_finish(p, yout[:, 0:16])
    hkind = "ExternalOutput" if debug else "Internal"
    h_scr = p.nc.dram_tensor("h_scr", [E, 128, 4 * T], BF16, kind=hkind)
    hsh_scr = p.nc.dram_tensor("hsh_scr", [128, 8 * T], BF16, kind=hkind)
    WB = [ah.take(NCH * 1024), ah.take(NCH * 1024)]
    HEb = [ah.take(4 * 512), ah.take(4 * 512)]
    SIL = ah.take(512)
    CWB = af.take(T)
    TF = af.take(512)
    SILF = af.take(512)

    mm_after = {}
    hst_after = {}
    for e_idx in range(E):
        buf = e_idx % 2
        wc = wload_chunks(p, WB[buf], NCH, 1024, w1[e_idx],
                          "mm", mm_after.get(e_idx - 2, 0), f"wld{buf}")
        p.banks_req(range(4))
        for j in range(4):
            mmc = p.ins("tensor", lambda e, e_idx=e_idx, j=j: e.matmul(
                pb[:, j * 512:(j + 1) * 512],
                sel16s[:, e_idx * 128:(e_idx + 1) * 128],
                CWFS[0:16, j * 512:(j + 1) * 512],
                start=True, stop=True), "mm" if j == 3 else None)
        p.wait("vector", "mm", mmc)
        cwev = p.ins("vector", lambda e: e.tensor_copy(CWB[:, 0:T],
                                                       pb[:, 0:T]), "ev")
        p.banks_done(range(4), "ev", cwev)
        if debug and e_idx == 0:
            p.ins("vector", lambda e: e.tensor_copy(CAP2[:, 0:T],
                                                    CWB[:, 0:T]))
        p.wait("tensor", f"wld{buf}", wc)
        for tg in range(4):
            t0 = tg * 512
            hbuf = HEb[tg % 2]
            p.wait("vector", "hst", hst_after.get((e_idx, tg - 2),
                                                  hst_after.get((e_idx - 1, tg + 2), 0)))
            for rnd in range(2):
                bank_ids = [rnd * 4 + i for i in range(4)]
                p.banks_req(bank_ids)
                for half in range(2):
                    for mi in range(2):
                        m = rnd * 2 + mi
                        och = half * 4 + m
                        bank = (rnd * 4 + half * 2 + mi) * 512
                        for c in range(NCH):
                            mmc = p.ins("tensor", lambda e, och=och, c=c,
                                        bank=bank, t0=t0, buf=buf: e.matmul(
                                pb[:, bank:bank + 512],
                                WB[buf][:, c * 1024 + och * 128:
                                        c * 1024 + och * 128 + 128],
                                xnap(c, t0, t0 + 512),
                                start=(c == 0), stop=(c == NCH - 1)),
                                "mm" if c == NCH - 1 else None)
                mmdone = mmc
                if debug and e_idx == 0 and tg == 0 and rnd == 0:
                    p.wait("vector", "mm", mmdone)
                    p.ins("vector", lambda e: e.tensor_copy(
                        CAP2[:, 512:1024], pb[:, 0:512]))
                    p.ins("vector", lambda e: e.tensor_copy(
                        CAP2[:, 1024:1536], pb[:, 2 * 512:3 * 512]))
                    p.wait("vector", "wld", p.cnt("wld"))
                    p.ins("vector", lambda e: e.tensor_copy(
                        CAP2[:, 1536:1664], WB[0][:, 0:128]))
                    p.ins("vector", lambda e: e.tensor_copy(
                        CAP2[:, 1664:1792], XN[:, 0:128]))
                for mi in range(2):
                    m = rnd * 2 + mi
                    ubank = (rnd * 4 + mi) * 512
                    gbank = (rnd * 4 + 2 + mi) * 512
                    p.wait("scalar", "mm", mmdone)
                    p.wait("scalar", "sil", p.cnt("sil"))
                    ac = p.ins("scalar", lambda e, gbank=gbank: e.activation(
                        SILF[:, 0:512], pb[:, gbank:gbank + 512], AF.Exp,
                        scale=-1.0), "ac")
                    p.wait("vector", "ac", ac)
                    p.wait("vector", "mm", mmdone)
                    p.ins("vector", lambda e: e.tensor_scalar_add(
                        SILF[:, 0:512], SILF[:, 0:512], 1.0))
                    p.ins("vector", lambda e: e.reciprocal(
                        SILF[:, 0:512], SILF[:, 0:512]))
                    p.ins("vector", lambda e, gbank=gbank: e.tensor_mul(
                        SILF[:, 0:512], SILF[:, 0:512],
                        pb[:, gbank:gbank + 512]))
                    p.ins("vector", lambda e, ubank=ubank:
                          e.scalar_tensor_tensor(
                        TF[:, 0:512], pb[:, ubank:ubank + 512], 1.0,
                        SILF[:, 0:512], ALU.mult, ALU.mult), "sil")
                    ev = p.ins("vector", lambda e, m=m, hbuf=hbuf, t0=t0:
                               e.scalar_tensor_tensor(
                        hbuf[:, m * 512:(m + 1) * 512], TF[:, 0:512],
                        1.0, CWB[:, t0:t0 + 512], ALU.mult, ALU.mult), "ev")
                p.banks_done(bank_ids, "ev", ev)
            if debug and e_idx == 0 and tg == 0:
                p.ins("vector", lambda e, hbuf=hbuf: e.tensor_copy(
                    CAP2[:, 0:512], hbuf[:, 0:512]))
            p.wait("sync", "ev", p.cnt("ev"))
            for m in range(4):
                hst_after[(e_idx, tg)] = p.dma(
                    "sync", h_scr.ap()[e_idx, :, m * T + t0:m * T + t0 + 512],
                    hbuf[:, m * 512:(m + 1) * 512], "hst")
        mm_after[e_idx] = p.cnt("mm")
        if stop_after == ("expert", e_idx):
            return _trunc_finish(p, yout[:, 0:16])

    if stop_after == "pass1":
        return _trunc_finish(p, yout[:, 0:16])
    wsh = [WB[0], WB[1]]

    def wsh_ap(c, lo, n):
        b = c // 4
        o = (c % 4) * 2048
        return wsh[b][:, o + lo:o + lo + n]
    clast = None
    p.wait("gpsimd", "mm", mm_after[E - 1])
    for k in range(NCH):
        clast = p.dma("gpsimd", wsh_ap(k, 0, 2048),
                      ws1[k * 128:(k + 1) * 128, :], "wld")
    p.wait("tensor", "wld", clast)
    for tg in range(4):
        t0 = tg * 512
        for m in range(8):
            ub = (m % 4) * 512
            gb = (4 + m % 4) * 512
            p.banks_req([m % 4, 4 + m % 4])
            for c in range(NCH):
                p.ins("tensor", lambda e, m=m, c=c, ub=ub, t0=t0: e.matmul(
                    pb[:, ub:ub + 512], wsh_ap(c, m * 128, 128),
                    xnap(c, t0, t0 + 512),
                    start=(c == 0), stop=(c == NCH - 1)))
            for c in range(NCH):
                mmc = p.ins("tensor", lambda e, m=m, c=c, gb=gb, t0=t0:
                            e.matmul(
                    pb[:, gb:gb + 512], wsh_ap(c, (8 + m) * 128, 128),
                    xnap(c, t0, t0 + 512),
                    start=(c == 0), stop=(c == NCH - 1)),
                    "mm" if c == NCH - 1 else None)
            p.wait("scalar", "mm", mmc)
            p.wait("scalar", "sil", p.cnt("sil"))
            ac = p.ins("scalar", lambda e, gb=gb: e.activation(
                SILF[:, 0:512], pb[:, gb:gb + 512], AF.Exp, scale=-1.0), "ac")
            p.wait("vector", "ac", ac)
            p.wait("vector", "mm", mmc)
            p.wait("vector", "hst", p.cnt("hst"))
            p.ins("vector", lambda e: e.tensor_scalar_add(
                SILF[:, 0:512], SILF[:, 0:512], 1.0))
            p.ins("vector", lambda e: e.reciprocal(
                SILF[:, 0:512], SILF[:, 0:512]))
            p.ins("vector", lambda e, gb=gb: e.tensor_mul(
                SILF[:, 0:512], SILF[:, 0:512], pb[:, gb:gb + 512]))
            ev = p.ins("vector", lambda e, ub=ub: e.scalar_tensor_tensor(
                HEb[0][:, 0:512], pb[:, ub:ub + 512], 1.0, SILF[:, 0:512],
                ALU.mult, ALU.mult), "sil")
            p.banks_done([m % 4, 4 + m % 4], "sil", ev)
            p.wait("sync", "sil", ev)
            p.dma("sync", hsh_scr.ap()[:, m * T + t0:m * T + t0 + 512],
                  HEb[0][:, 0:512], "hst")

    if stop_after == "shared":
        return _trunc_finish(p, yout[:, 0:16])
    p.barrier()
    ah.release()
    af.release()
    ah.mark()
    W2B = [ah.take(4 * 1024), ah.take(4 * 1024)]
    WS2B = ah.take(NCH * 1024)
    HB = [ah.take(4 * 512), ah.take(4 * 512)]
    HSB = ah.take(8 * 512)
    ZSB = ah.take(512)
    if skip_shared_z:
        p.ins("vector", lambda e: e.memset(ZSB[:, 0:512], 0.0), "ev")
    wsc = wload_chunks(p, WS2B, NCH, 1024, ws2)
    guard = {}
    seq = 0
    shared_mm_guard = 0
    for tg in range(4):
        t0 = tg * 512
        p.banks_req(range(8))
        for e_idx in range(E):
            buf = 0 if serial_p2 else seq % 2
            gk = seq - 1 if serial_p2 else seq - 2
            wc = wload_chunks(p, W2B[buf], 4, 1024, w2[e_idx],
                              "mm", guard.get(gk, 0))
            p.wait("sync", "mm", guard.get(gk, 0))
            hc = None
            for m in range(4):
                hc = p.dma("sync", HB[buf][:, m * 512:(m + 1) * 512],
                           h_scr.ap()[e_idx, :, m * T + t0:m * T + t0 + 512],
                           "hld")
            p.wait("tensor", "wld", wc)
            p.wait("tensor", "hld", hc)
            for mo in range(8):
                for c in range(4):
                    guard[seq] = p.ins("tensor", lambda e, mo=mo, c=c, buf=buf,
                          first=(e_idx == 0 and c == 0): e.matmul(
                        pb[:, mo * 512:mo * 512 + 512],
                        W2B[buf][:, c * 1024 + mo * 128:
                                 c * 1024 + mo * 128 + 128],
                        HB[buf][:, c * 512:(c + 1) * 512],
                        start=first, stop=False),
                        "mm" if (mo == 7 and c == 3) else None)
            if debug and tg == 0 and e_idx == 0:
                p.wait("vector", "mm", guard[seq])
                p.ins("vector", lambda e: e.tensor_copy(CAP2[:, 0:512],
                                                        pb[:, 3072:3584]))
            if debug and tg == 0 and e_idx == E - 1:
                p.wait("vector", "mm", guard[seq])
                p.ins("vector", lambda e: e.tensor_copy(CAP2[:, 512:1024],
                                                        pb[:, 3072:3584]))
            seq += 1
        p.wait("sync", "mm", shared_mm_guard)
        hc = None
        for m in range(8):
            hc = p.dma("sync", HSB[:, m * 512:(m + 1) * 512],
                       hsh_scr.ap()[:, m * T + t0:m * T + t0 + 512], "hsld")
        p.wait("tensor", "wld", wsc)
        p.wait("tensor", "hsld", hc)
        nsh = 1 if skip_shared_z else NCH
        for mo in range(8):
            for c in range(nsh):
                mmc = p.ins("tensor", lambda e, mo=mo, c=c, z=skip_shared_z:
                            e.matmul(
                    pb[:, mo * 512:mo * 512 + 512],
                    WS2B[:, c * 1024 + mo * 128:c * 1024 + mo * 128 + 128],
                    HSB[:, c * 512:(c + 1) * 512] if not z else
                    ZSB[:, 0:512],
                    start=False, stop=(c == nsh - 1)),
                    "mm" if (c == nsh - 1) else None)
        shared_mm_guard = mmc
        p.wait("vector", "mm", mmc)
        if debug and tg == 0:
            p.ins("vector", lambda e: e.tensor_copy(CAP2[:, 1024:1536],
                                                    pb[:, 3072:3584]))
        for mo in range(NCH):
            ev = p.ins("vector", lambda e, mo=mo, t0=t0: e.tensor_add(
                X[:, mo * T + t0:mo * T + t0 + 512],
                pb[:, mo * 512:mo * 512 + 512],
                X[:, mo * T + t0:mo * T + t0 + 512]), "ev")
        p.banks_done(range(8), "ev", ev)

    p.wait("sync", "ev", p.cnt("ev"))
    if debug:
        d_xn = p.outp("d_xn", [128, NCH * T], BF16)
        p.dma("sync", d_xn, XN[:], "st")
        d_lg = p.outp("d_lg", [16, T], F32)
        p.dma("sync", d_lg, LGS[0:16, 0:T], "st")
        d_cwf = p.outp("d_cwf", [16, T], F32)
        p.dma("sync", d_cwf, CWFS[0:16, 0:T], "st")
        d_cap = p.outp("d_cap", [128, 64], F32)
        p.dma("sync", d_cap, CAP[:], "st")
        d_cap2 = p.outp("d_cap2", [128, T], F32)
        p.dma("sync", d_cap2, CAP2[:], "st")
    stc = p.dma("sync", yout, X[:], "st")
    p.wait("sync", "st", stc)
    return p




def l1ffn_inputs(inputs):
    d = {
        "wg": np.ascontiguousarray(np.asarray(inputs["L1_ffn_Wg"]).T),
        "ws1": np.ascontiguousarray(np.asarray(inputs["L1_ffn_Ws1"])),
        "ws2": np.ascontiguousarray(np.asarray(inputs["L1_ffn_Ws2"])),
        "w1": np.ascontiguousarray(np.asarray(inputs["L1_ffn_W1"])),
        "w2": np.ascontiguousarray(np.asarray(inputs["L1_ffn_W2"])),
    }
    d.update(host_aux())
    d["sel16"] = host_sel16()
    return d


_PROGS = {}


def get_prog(name, builder):
    if name not in _PROGS:
        p = builder()
        _PROGS[name] = p.finalize()
    return _PROGS[name]


def run_l1ffn(x, inputs):
    nc = get_prog("l1ffn", build_l1_ffn)
    shared = l1ffn_inputs(inputs)
    in_maps = []
    for c in range(NCORES):
        b, h = c // 2, c % 2
        m = dict(shared)
        m["xfm"] = fm_pack(x[b, h * T:(h + 1) * T])
        in_maps.append(m)
    res = run_spmd_cached(nc, in_maps, core_ids=list(range(NCORES)))
    out = np.empty_like(x)
    for c in range(NCORES):
        b, h = c // 2, c % 2
        out[b, h * T:(h + 1) * T] = fm_unpack(res.results[c]["yfm"], T)
    return out





def build_l1f_r():
    p = Prog("l1f_r")
    E = 16
    xin = p.inp("xfm", [128, NCH * T])
    wg = p.inp("wg", [D, E])
    xnout = p.outp("xn", [128, NCH * T], BF16)
    cwout = p.outp("cwfs", [16, T])
    aux = emit_aux(p)
    X = p.sbuf("X", [128, NCH * T], F32)
    XN = p.sbuf("XN", [128, NCH * T], BF16)
    pb = p.psum("pb", [128, 4096], F32)
    ah = Arena(p, "AH", 2048, BF16)
    af = Arena(p, "AF", 15000, F32)
    p.dma("sync", X[:], xin, "ld")
    for eng in ("tensor", "scalar", "vector"):
        p.wait(eng, "ld", p.cnt("ld"))

    def xap(c, t0, t1):
        return X[:, c * T + t0:c * T + t1]

    emit_ln(p, aux, xap, lambda c: XN[:, c * T:(c + 1) * T], T, pb, af, "o")

    wgs = ah.take(NCH * E)
    wc = wload_chunks(p, wgs, NCH, E, wg)
    LGS = af.take(T)
    p.wait("tensor", "wld", wc)
    p.wait("tensor", "ev", p.cnt("ev"))
    p.banks_req(range(4))
    for c in range(NCH):
        for j in range(4):
            mmc = p.ins("tensor", lambda e, c=c, j=j: e.matmul(
                pb[0:16, j * 512:(j + 1) * 512],
                wgs[:, c * E:(c + 1) * E],
                XN[:, c * T + j * 512:c * T + (j + 1) * 512],
                start=(c == 0), stop=(c == NCH - 1)),
                "mm" if (c == NCH - 1) else None)
    p.wait("vector", "mm", mmc)
    ev = p.ins("vector", lambda e: e.tensor_copy(LGS[0:16, 0:T],
                                                 pb[0:16, 0:T]), "ev")
    p.banks_done(range(4), "ev", ev)

    CWFS = af.take(T)
    PRT = p.sbuf("PRT", [128, 16], F32)
    PM = p.sbuf("PM", [128, 16], F32)
    I1 = p.sbuf("I1", [128, 16], F32)
    I2 = p.sbuf("I2", [128, 16], F32)
    CWT = p.sbuf("CWT", [128, 16], F32)
    PT0 = p.sbuf("PT0", [128, 16], F32)
    mxc = p.sbuf("mxc", [128, 1], F32)
    smc = p.sbuf("smc", [128, 1], F32)
    p.wait("tensor", "ev", ev)
    for t in range(16):
        bank = 2048 + (t % 2) * 512
        bid = 4 + (t % 2)
        p.banks_req([bid])
        mmc = p.ins("tensor", lambda e, t=t, bank=bank: e.transpose(
            pb[:, bank:bank + 16], LGS[0:16, t * 128:(t + 1) * 128],
            aux["eye_f"][0:16, 0:16]), "mm")
        p.wait("vector", "mm", mmc)
        ev0 = p.ins("vector", lambda e, bank=bank: e.tensor_copy(
            PT0[:], pb[:, bank:bank + 16]), "ev")
        p.banks_done([bid], "ev", ev0)
        p.ins("vector", lambda e: e.tensor_reduce(
            mxc[:], PT0[:], AX.X, ALU.max))
        evn = p.ins("vector", lambda e: e.tensor_scalar_mul(
            mxc[:], mxc[:], -1.0), "ev")
        p.wait("scalar", "ev", evn)
        ac = p.ins("scalar", lambda e: e.activation(
            PRT[:], PT0[:], AF.Exp, bias=mxc[:, 0:1]), "ac")
        p.wait("vector", "ac", ac)
        p.ins("vector", lambda e: e.tensor_reduce(
            smc[:], PRT[:], AX.X, ALU.add))
        p.ins("vector", lambda e: e.reciprocal(smc[:], smc[:]))
        p.ins("vector", lambda e: e.tensor_scalar_mul(PRT[:], PRT[:],
                                                      smc[:, 0:1]))
        p.ins("vector", lambda e: e.tensor_reduce(
            mxc[:], PRT[:], AX.X, ALU.max))
        p.ins("vector", lambda e: e.tensor_scalar_mul(mxc[:], mxc[:], -1.0))
        p.ins("vector", lambda e: e.tensor_scalar(
            I1[:], PRT[:], mxc[:, 0:1], 0.0, ALU.add, ALU.is_ge))
        p.ins("vector", lambda e: e.tensor_scalar(
            PM[:], I1[:], -1.0, 1.0, ALU.mult, ALU.add))
        p.ins("vector", lambda e: e.tensor_mul(PM[:], PRT[:], PM[:]))
        p.ins("vector", lambda e: e.tensor_reduce(
            mxc[:], PM[:], AX.X, ALU.max))
        p.ins("vector", lambda e: e.tensor_scalar_mul(mxc[:], mxc[:], -1.0))
        p.ins("vector", lambda e: e.tensor_scalar(
            I2[:], PM[:], mxc[:, 0:1], 0.0, ALU.add, ALU.is_ge))
        p.ins("vector", lambda e: e.tensor_add(I1[:], I1[:], I2[:]))
        ev = p.ins("vector", lambda e: e.tensor_mul(CWT[:], PRT[:], I1[:]),
                   "ev")
        p.wait("tensor", "ev", ev)
        bank2 = 2048 + (t % 2) * 512 + 128
        bid2 = 4 + (t % 2)
        p.banks_req([bid2])
        mmc = p.ins("tensor", lambda e, bank2=bank2: e.transpose(
            pb[0:16, bank2:bank2 + 128], CWT[:], aux["eye_f"][:]), "mm")
        p.wait("vector", "mm", mmc)
        ev = p.ins("vector", lambda e, t=t, bank2=bank2: e.tensor_copy(
            CWFS[0:16, t * 128:(t + 1) * 128],
            pb[0:16, bank2:bank2 + 128]), "ev")
        p.banks_done([bid2], "ev", ev)
    p.wait("sync", "ev", p.cnt("ev"))
    p.dma("sync", xnout, XN[:], "st")
    p.dma("sync", cwout, CWFS[0:16, 0:T], "st")
    p.wait("sync", "st", p.cnt("st"))
    return p


def build_l1f_he(npe=2):
    p = Prog("l1f_he")
    xn_in = p.inp("xn", [128, NCH * T], BF16)
    cw2 = p.inp("cw2", [1, npe * T])
    w1s = p.inp("w1s", [npe, D, 1024])
    he = p.outp("he", [npe, 128, 4 * T], BF16)
    aux = emit_aux(p)
    XN = p.sbuf("XN", [128, NCH * T], BF16)
    pb = p.psum("pb", [128, 4096], F32)
    ah = Arena(p, "AH", 2 * NCH * 1024 + 4 * 512 * 2, BF16)
    af = Arena(p, "AF", T + 3 * 512 + npe * T, F32)
    WB = [ah.take(NCH * 1024), ah.take(NCH * 1024)]
    HEb = [ah.take(4 * 512), ah.take(4 * 512)]
    CWR = af.take(npe * T)
    CWB = af.take(T)
    TF = af.take(512)
    SILF = af.take(512)
    p.dma("sync", XN[:], xn_in, "ld")
    p.dma("sync", CWR[0:1, 0:npe * T], cw2, "ld")
    for eng in ("tensor", "scalar", "vector"):
        p.wait(eng, "ld", p.cnt("ld"))

    def xnap(c, t0, t1):
        return XN[:, c * T + t0:c * T + t1]

    mm_after = {}
    hst_after = {}
    for e_idx in range(npe):
        buf = e_idx % 2
        wc = wload_chunks(p, WB[buf], NCH, 1024, w1s[e_idx],
                          "mm", mm_after.get(e_idx - 2, 0), f"wld{buf}")
        p.banks_req(range(4))
        for j in range(4):
            mmc = p.ins("tensor", lambda e, e_idx=e_idx, j=j: e.matmul(
                pb[:, j * 512:(j + 1) * 512], aux["onesr_f"][0:1, 0:128],
                CWR[0:1, e_idx * T + j * 512:e_idx * T + (j + 1) * 512],
                start=True, stop=True), "mm" if j == 3 else None)
        p.wait("vector", "mm", mmc)
        cwev = p.ins("vector", lambda e: e.tensor_copy(CWB[:, 0:T],
                                                       pb[:, 0:T]), "ev")
        p.banks_done(range(4), "ev", cwev)
        p.wait("tensor", f"wld{buf}", wc)
        for tg in range(4):
            t0 = tg * 512
            hbuf = HEb[tg % 2]
            p.wait("vector", "hst", hst_after.get(
                (e_idx, tg - 2), hst_after.get((e_idx - 1, tg + 2), 0)))
            for rnd in range(2):
                bank_ids = [rnd * 4 + i for i in range(4)]
                p.banks_req(bank_ids)
                for half in range(2):
                    for mi in range(2):
                        m = rnd * 2 + mi
                        och = half * 4 + m
                        bank = (rnd * 4 + half * 2 + mi) * 512
                        for c in range(NCH):
                            mmc = p.ins("tensor", lambda e, och=och, c=c,
                                        bank=bank, t0=t0, buf=buf: e.matmul(
                                pb[:, bank:bank + 512],
                                WB[buf][:, c * 1024 + och * 128:
                                        c * 1024 + och * 128 + 128],
                                xnap(c, t0, t0 + 512),
                                start=(c == 0), stop=(c == NCH - 1)),
                                "mm" if c == NCH - 1 else None)
                mmdone = mmc
                for mi in range(2):
                    m = rnd * 2 + mi
                    ubank = (rnd * 4 + mi) * 512
                    gbank = (rnd * 4 + 2 + mi) * 512
                    p.wait("scalar", "mm", mmdone)
                    p.wait("scalar", "sil", p.cnt("sil"))
                    ac = p.ins("scalar", lambda e, gbank=gbank: e.activation(
                        SILF[:, 0:512], pb[:, gbank:gbank + 512], AF.Exp,
                        scale=-1.0), "ac")
                    p.wait("vector", "ac", ac)
                    p.wait("vector", "mm", mmdone)
                    p.ins("vector", lambda e: e.tensor_scalar_add(
                        SILF[:, 0:512], SILF[:, 0:512], 1.0))
                    p.ins("vector", lambda e: e.reciprocal(
                        SILF[:, 0:512], SILF[:, 0:512]))
                    p.ins("vector", lambda e, gbank=gbank: e.tensor_mul(
                        SILF[:, 0:512], SILF[:, 0:512],
                        pb[:, gbank:gbank + 512]))
                    p.ins("vector", lambda e, ubank=ubank:
                          e.scalar_tensor_tensor(
                        TF[:, 0:512], pb[:, ubank:ubank + 512], 1.0,
                        SILF[:, 0:512], ALU.mult, ALU.mult), "sil")
                    ev = p.ins("vector", lambda e, m=m, hbuf=hbuf, t0=t0:
                               e.scalar_tensor_tensor(
                        hbuf[:, m * 512:(m + 1) * 512], TF[:, 0:512],
                        1.0, CWB[:, t0:t0 + 512], ALU.mult, ALU.mult), "ev")
                p.banks_done(bank_ids, "ev", ev)
            p.wait("sync", "ev", p.cnt("ev"))
            for m in range(4):
                hst_after[(e_idx, tg)] = p.dma(
                    "sync", he[e_idx, :, m * T + t0:m * T + t0 + 512],
                    hbuf[:, m * 512:(m + 1) * 512], "hst")
        mm_after[e_idx] = p.cnt("mm")
    p.wait("sync", "hst", p.cnt("hst"))
    return p


def build_l1f_s():
    p = Prog("l1f_s")
    xn_in = p.inp("xn", [128, NCH * T], BF16)
    ws1 = p.inp("ws1", [D, 2048])
    hsh = p.outp("hsh", [128, 8 * T], BF16)
    aux = emit_aux(p)
    XN = p.sbuf("XN", [128, NCH * T], BF16)
    pb = p.psum("pb", [128, 4096], F32)
    ah = Arena(p, "AH", 2 * NCH * 1024 + 512, BF16)
    af = Arena(p, "AF", 2 * 512, F32)
    WB = [ah.take(NCH * 1024), ah.take(NCH * 1024)]
    HE0 = ah.take(512)
    TF = af.take(512)
    SILF = af.take(512)
    p.dma("sync", XN[:], xn_in, "ld")
    for eng in ("tensor", "scalar", "vector"):
        p.wait(eng, "ld", p.cnt("ld"))

    def xnap(c, t0, t1):
        return XN[:, c * T + t0:c * T + t1]

    wsh = [WB[0], WB[1]]

    def wsh_ap(c, lo, n):
        b = c // 4
        o = (c % 4) * 2048
        return wsh[b][:, o + lo:o + lo + n]
    clast = None
    for k in range(NCH):
        clast = p.dma("gpsimd", wsh_ap(k, 0, 2048),
                      ws1[k * 128:(k + 1) * 128, :], "wld")
    p.wait("tensor", "wld", clast)
    for tg in range(4):
        t0 = tg * 512
        for m in range(8):
            ub = (m % 4) * 512
            gb = (4 + m % 4) * 512
            p.banks_req([m % 4, 4 + m % 4])
            for c in range(NCH):
                p.ins("tensor", lambda e, m=m, c=c, ub=ub, t0=t0: e.matmul(
                    pb[:, ub:ub + 512], wsh_ap(c, m * 128, 128),
                    xnap(c, t0, t0 + 512),
                    start=(c == 0), stop=(c == NCH - 1)))
            for c in range(NCH):
                mmc = p.ins("tensor", lambda e, m=m, c=c, gb=gb, t0=t0:
                            e.matmul(
                    pb[:, gb:gb + 512], wsh_ap(c, (8 + m) * 128, 128),
                    xnap(c, t0, t0 + 512),
                    start=(c == 0), stop=(c == NCH - 1)),
                    "mm" if c == NCH - 1 else None)
            p.wait("scalar", "mm", mmc)
            p.wait("scalar", "sil", p.cnt("sil"))
            ac = p.ins("scalar", lambda e, gb=gb: e.activation(
                SILF[:, 0:512], pb[:, gb:gb + 512], AF.Exp, scale=-1.0), "ac")
            p.wait("vector", "ac", ac)
            p.wait("vector", "mm", mmc)
            p.wait("vector", "hst", p.cnt("hst"))
            p.ins("vector", lambda e: e.tensor_scalar_add(
                SILF[:, 0:512], SILF[:, 0:512], 1.0))
            p.ins("vector", lambda e: e.reciprocal(
                SILF[:, 0:512], SILF[:, 0:512]))
            p.ins("vector", lambda e, gb=gb: e.tensor_mul(
                SILF[:, 0:512], SILF[:, 0:512], pb[:, gb:gb + 512]))
            ev = p.ins("vector", lambda e, ub=ub: e.scalar_tensor_tensor(
                HE0[:, 0:512], pb[:, ub:ub + 512], 1.0, SILF[:, 0:512],
                ALU.mult, ALU.mult), "sil")
            p.banks_done([m % 4, 4 + m % 4], "sil", ev)
            p.wait("sync", "sil", ev)
            p.dma("sync", hsh[:, m * T + t0:m * T + t0 + 512],
                  HE0[:, 0:512], "hst")
    p.wait("sync", "hst", p.cnt("hst"))
    return p


def build_l1f_p2(ne=4, final=False):
    p = Prog("l1f_p2f" if final else "l1f_p2")
    he_in = p.inp("he", [ne, 128, 4 * T], BF16)
    w2s = p.inp("w2s", [ne, 512, D])
    part = p.inp("part", [128, NCH * T])
    if final:
        hsh_in = p.inp("hsh", [128, 8 * T], BF16)
        ws2 = p.inp("ws2", [1024, D])
    pout = p.outp("pout", [128, NCH * T])
    aux = emit_aux(p)
    PX = p.sbuf("PX", [128, NCH * T], F32)
    pb = p.psum("pb", [128, 4096], F32)
    ah = Arena(p, "AH", 2 * 4 * 1024 + 2 * 4 * 512 + NCH * 1024 + 8 * 512,
               BF16)
    W2B = [ah.take(4 * 1024), ah.take(4 * 1024)]
    HB = [ah.take(4 * 512), ah.take(4 * 512)]
    WS2B = ah.take(NCH * 1024)
    HSB = ah.take(8 * 512)
    p.dma("sync", PX[:], part, "ld")
    for eng in ("tensor", "vector"):
        p.wait(eng, "ld", p.cnt("ld"))
    wsc = None
    if final:
        wsc = wload_chunks(p, WS2B, NCH, 1024, ws2)
    guard = {}
    seq = 0
    for tg in range(4):
        t0 = tg * 512
        p.banks_req(range(8))
        for e_idx in range(ne):
            buf = seq % 2
            gk = seq - 2
            wc = wload_chunks(p, W2B[buf], 4, 1024, w2s[e_idx],
                              "mm", guard.get(gk, 0), f"wld{buf}")
            p.wait("sync", "mm", guard.get(gk, 0))
            hc = None
            for m in range(4):
                hc = p.dma("sync", HB[buf][:, m * 512:(m + 1) * 512],
                           he_in[e_idx, :, m * T + t0:m * T + t0 + 512],
                           "hld")
            p.wait("tensor", f"wld{buf}", wc)
            p.wait("tensor", "hld", hc)
            for mo in range(8):
                for c in range(4):
                    guard[seq] = p.ins("tensor", lambda e, mo=mo, c=c,
                          buf=buf, first=(e_idx == 0 and c == 0): e.matmul(
                        pb[:, mo * 512:mo * 512 + 512],
                        W2B[buf][:, c * 1024 + mo * 128:
                                 c * 1024 + mo * 128 + 128],
                        HB[buf][:, c * 512:(c + 1) * 512],
                        start=first, stop=(not final and e_idx == ne - 1
                                           and c == 3)),
                        "mm" if (mo == 7 and c == 3) else None)
            seq += 1
        if final:
            p.wait("sync", "mm", guard[seq - 1] if tg == 0 else shg)
            hc = None
            for m in range(8):
                hc = p.dma("sync", HSB[:, m * 512:(m + 1) * 512],
                           hsh_in[:, m * T + t0:m * T + t0 + 512], "hsld")
            p.wait("tensor", "wld", wsc)
            p.wait("tensor", "hsld", hc)
            for mo in range(8):
                for c in range(NCH):
                    mmc = p.ins("tensor", lambda e, mo=mo, c=c: e.matmul(
                        pb[:, mo * 512:mo * 512 + 512],
                        WS2B[:, c * 1024 + mo * 128:c * 1024 + mo * 128 + 128],
                        HSB[:, c * 512:(c + 1) * 512],
                        start=False, stop=(c == NCH - 1)),
                        "mm" if (c == NCH - 1) else None)
            shg = mmc
        p.wait("vector", "mm", p.cnt("mm"))
        for mo in range(NCH):
            ev = p.ins("vector", lambda e, mo=mo, t0=t0: e.tensor_add(
                PX[:, mo * T + t0:mo * T + t0 + 512],
                pb[:, mo * 512:mo * 512 + 512],
                PX[:, mo * T + t0:mo * T + t0 + 512]), "ev")
        p.banks_done(range(8), "ev", ev)
    p.wait("sync", "ev", p.cnt("ev"))
    stc = p.dma("sync", pout, PX[:], "st")
    p.wait("sync", "st", stc)
    return p


def run_l1ffn_split(x, inputs):
    wg = np.ascontiguousarray(np.asarray(inputs["L1_ffn_Wg"],
                                         np.float32).T)
    W1 = np.asarray(inputs["L1_ffn_W1"], np.float32)
    W2 = np.asarray(inputs["L1_ffn_W2"], np.float32)
    Ws1 = np.asarray(inputs["L1_ffn_Ws1"], np.float32)
    Ws2 = np.asarray(inputs["L1_ffn_Ws2"], np.float32)
    ncr = get_prog("l1f_r", build_l1f_r)
    nche = get_prog("l1f_he", build_l1f_he)
    ncs = get_prog("l1f_s", build_l1f_s)
    ncp = get_prog("l1f_p2", lambda: build_l1f_p2(4, False))
    ncpf = get_prog("l1f_p2f", lambda: build_l1f_p2(4, True))
    xpk = [fm_pack(x[c // 2, (c % 2) * T:(c % 2 + 1) * T])
           for c in range(NCORES)]
    res = run_spmd_cached(ncr, [
        {"xfm": xpk[c], "wg": wg, **host_aux()} for c in range(NCORES)],
        core_ids=list(range(NCORES)))
    xn = [res.results[c]["xn"] for c in range(NCORES)]
    cwfs = [res.results[c]["cwfs"] for c in range(NCORES)]
    he = [[] for _ in range(NCORES)]
    for pair in range(8):
        e0 = pair * 2
        res = run_spmd_cached(nche, [
            {"xn": xn[c],
             "cw2": np.ascontiguousarray(cwfs[c][e0:e0 + 2]).reshape(1, -1),
             "w1s": np.ascontiguousarray(W1[e0:e0 + 2]), **host_aux()}
            for c in range(NCORES)],
            core_ids=list(range(NCORES)))
        for c in range(NCORES):
            he[c].append(res.results[c]["he"])
    res = run_spmd_cached(ncs, [
        {"xn": xn[c], "ws1": Ws1, **host_aux()} for c in range(NCORES)],
        core_ids=list(range(NCORES)))
    hsh = [res.results[c]["hsh"] for c in range(NCORES)]
    part = xpk
    for grp in range(4):
        e0 = grp * 4
        hegrp = [np.concatenate([he[c][grp * 2], he[c][grp * 2 + 1]], axis=0)
                 for c in range(NCORES)]
        nc = ncpf if grp == 3 else ncp
        maps = []
        for c in range(NCORES):
            m = {"he": hegrp[c],
                 "w2s": np.ascontiguousarray(W2[e0:e0 + 4]),
                 "part": part[c], **host_aux()}
            if grp == 3:
                m["hsh"] = hsh[c]
                m["ws2"] = Ws2
            maps.append(m)
        res = run_spmd_cached(nc, maps, core_ids=list(range(NCORES)))
        part = [res.results[c]["pout"] for c in range(NCORES)]
    out = np.empty_like(x)
    for c in range(NCORES):
        b, h = c // 2, c % 2
        out[b, h * T:(h + 1) * T] = fm_unpack(part[c], T)
    return out




def build_attn_a_A():
    p = Prog("attn_a_A")
    TF_ = 4096

    xfull = p.inp("xfull", [128, NCH * TF_])
    wq = p.inp("wq", [NCH, 128, NCH * 128])
    bq = p.inp("bq", [128, NCH])
    wkv = p.inp("wkv", [D, 128])
    bkvk = p.inp("bkvk", [64, 1])
    bkvv = p.inp("bkvv", [1, 64])
    hsel = p.inp("hsel", [128, NCH * 16], BF16)
    aout = p.outp("aout", [128, 4 * TF_], BF16)

    aux = emit_aux(p)
    XN = p.sbuf("XN", [128, NCH * TF_], BF16)
    pb = p.psum("pb", [128, 4096], F32)
    ah = Arena(p, "AH", 44800, BF16)
    af = Arena(p, "AF", 13312, F32)

    af.mark()
    XS = af.take(NCH * 1024)
    for q4 in range(4):
        lc = None
        for c in range(NCH):
            lc = p.dma("sync", XS[:, c * 1024:(c + 1) * 1024],
                       xfull[:, c * TF_ + q4 * 1024:c * TF_ + (q4 + 1) * 1024],
                       "ld")
        for eng in ("tensor", "scalar", "vector"):
            p.wait(eng, "ld", lc)
        emit_ln(p, aux, lambda c, t0, t1, q4=q4: XS[:, c * 1024 + t0:c * 1024 + t1],
                lambda c, q4=q4: XN[:, c * TF_ + q4 * 1024:c * TF_ + (q4 + 1) * 1024],
                1024, pb, af, f"s{q4}")
        p.wait("sync", "ev", p.cnt("ev"))
    af.release()

    KH = ah.take(TF_)
    VA = ah.take(32 * 65)
    Q = ah.take(NCH * T)
    AO = ah.take(4 * TF_)
    WQB = ah.take(2 * 1024)
    WKVB = ah.take(NCH * 128)
    PT = ah.take(2 * 256)
    SQH = ah.take(2048)
    af.mark()
    QSS = af.take(2048)
    RR = af.take(2048)
    NEGMC = p.sbuf("negmc", [128, 1], F32)
    bq8 = p.sbuf("bq8", [128, NCH], F32)
    bkvks = p.sbuf("bkvks", [64, 1], F32)

    wc2 = wload_chunks(p, WKVB, NCH, 128, wkv)
    lc = p.dma("sync", bq8[:], bq, "ld2")
    lc = p.dma("sync", bkvks[:], bkvk, "ld2")
    bkvvs = p.sbuf("bkvvs", [1, 64], BF16)
    lc = p.dma("gpsimd", bkvvs[:], bkvv, "wld")
    hsels = p.sbuf("hsels", [128, NCH * 16], BF16)
    lc = p.dma("sync", hsels[:], hsel, "ld2")
    p.wait("vector", "ld2", p.cnt("ld2"))
    ev = p.ins("vector", lambda e: e.tensor_scalar_mul(bq8[:], bq8[:], 0.125),
               "ev")

    ev = p.ins("vector", lambda e: e.memset(VA[:, 64:32 * 65:65], 1.0), "ev")

    p.wait("tensor", "wld", p.cnt("wld"))
    p.wait("tensor", "ld", p.cnt("ld"))
    p.wait("tensor", "ld2", p.cnt("ld2"))
    p.wait("tensor", "ev", p.cnt("ev"))

    for nt in range(8):
        bk = nt % 2
        p.banks_req([bk])
        for c in range(NCH):
            mmc = p.ins("tensor", lambda e, nt=nt, c=c, bk=bk: e.matmul(
                pb[0:64, bk * 512:bk * 512 + 512],
                WKVB[:, c * 128:c * 128 + 64],
                XN[:, c * TF_ + nt * 512:c * TF_ + (nt + 1) * 512],
                start=(c == 0), stop=(c == NCH - 1)),
                "mm" if c == NCH - 1 else None)
        p.wait("vector", "mm", mmc)
        ac = p.ins("vector", lambda e, nt=nt, bk=bk: e.tensor_scalar_add(
            KH[0:64, nt * 512:(nt + 1) * 512],
            pb[0:64, bk * 512:bk * 512 + 512], bkvks[:, 0:1]), "ev")
        ac = p.ins("vector", lambda e, nt=nt, bk=bk: e.tensor_scalar_add(
            KH[64:128, nt * 512:(nt + 1) * 512],
            pb[0:64, bk * 512:bk * 512 + 512], bkvks[:, 0:1]), "ev")
        p.banks_done([bk], "ev", ac)

    for tt in range(32):
        bk = 2 + tt % 2
        p.banks_req([bk])
        for c in range(NCH):
            p.ins("tensor", lambda e, tt=tt, c=c, bk=bk: e.matmul(
                pb[:, bk * 512:bk * 512 + 64],
                XN[:, c * TF_ + tt * 128:c * TF_ + (tt + 1) * 128],
                WKVB[:, c * 128 + 64:c * 128 + 128],
                start=(c == 0), stop=False))
        mmc = p.ins("tensor", lambda e, tt=tt, bk=bk: e.matmul(
            pb[:, bk * 512:bk * 512 + 64], aux["onesr_h"][:],
            bkvvs[:], start=False, stop=True), "mm")
        p.wait("vector", "mm", mmc)
        ev = p.ins("vector", lambda e, tt=tt, bk=bk: e.tensor_copy(
            VA[:, tt * 65:tt * 65 + 64], pb[:, bk * 512:bk * 512 + 64]), "ev")
        p.banks_done([bk], "ev", ev)

    def xnown(c, nt):
        return XN[:, c * TF_ + 2048 + nt * 512:c * TF_ + 2048 + (nt + 1) * 512]

    qmm = {}
    for m in range(NCH):
        wbuf = m % 2
        if m >= 2:
            p.wait("gpsimd", "mm", qmm[m - 2])
        wc = p.dma("gpsimd", WQB[:, wbuf * 1024:(wbuf + 1) * 1024],
                   wq[m], f"wq{wbuf}")
        p.wait("tensor", f"wq{wbuf}", wc)
        for nt in range(4):
            bk = 4 + (m * 4 + nt) % 2
            p.banks_req([bk])
            for c in range(NCH):
                mmc = p.ins("tensor", lambda e, m=m, nt=nt, c=c, bk=bk,
                            wbuf=wbuf: e.matmul(
                    pb[:, bk * 512:bk * 512 + 512],
                    WQB[:, wbuf * 1024 + c * 128:wbuf * 1024 + c * 128 + 128],
                    xnown(c, nt),
                    start=(c == 0), stop=(c == NCH - 1)),
                    "mm" if c == NCH - 1 else None)
            p.wait("vector", "mm", mmc)
            ac = p.ins("vector", lambda e, m=m, nt=nt, bk=bk: e.tensor_scalar(
                Q[:, m * T + nt * 512:m * T + (nt + 1) * 512],
                pb[:, bk * 512:bk * 512 + 512],
                0.125, bq8[:, m:m + 1], ALU.mult, ALU.add), "ev")
            p.banks_done([bk], "ev", ac)
        qmm[m] = p.cnt("mm")

    p.wait("scalar", "ev", p.cnt("ev"))
    p.banks_req([4, 5, 6, 7])
    for c in range(NCH):
        ac = p.ins("scalar", lambda e, c=c: e.activation(
            SQH[:, 0:2048], Q[:, c * T:(c + 1) * T], AF.Square), "ac")
        p.wait("tensor", "ac", ac)
        for nt in range(4):
            mmc = p.ins("tensor", lambda e, c=c, nt=nt: e.matmul(
                pb[0:16, 2048 + nt * 512:2048 + nt * 512 + 512],
                hsels[:, c * 16:(c + 1) * 16],
                SQH[:, nt * 512:(nt + 1) * 512],
                start=(c == 0), stop=(c == NCH - 1)),
                "mm" if nt == 3 else None)
        p.wait("scalar", "mm", mmc)
    p.wait("vector", "mm", p.cnt("mm"))
    ev = p.ins("vector", lambda e: e.tensor_copy(
        QSS[0:16, 0:2048], pb[0:16, 2048:4096]), "ev")
    p.banks_done([4, 5, 6, 7], "ev", ev)
    p.ins("vector", lambda e: e.tensor_reduce(
        RR[0:16, 0:1], QSS[0:16, 0:2048], AX.X, ALU.max))
    ev = p.ins("vector", lambda e: e.engine_nop(), "ev")
    p.wait("tensor", "ev", ev)
    p.banks_req([6])
    mmc = p.ins("tensor", lambda e: e.transpose(
        pb[0:1, 3072:3072 + 16], RR[0:16, 0:1], aux["eye_f"][0:16, 0:16]),
        "mm")
    p.wait("vector", "mm", mmc)
    ev = p.ins("vector", lambda e: e.tensor_copy(RR[0:1, 16:32],
                                                 pb[0:1, 3072:3088]), "ev")
    p.banks_done([6], "ev", ev)
    p.ins("vector", lambda e: e.tensor_reduce(
        RR[0:1, 32:33], RR[0:1, 16:32], AX.X, ALU.max))
    for hh in range(2):
        p.wait("scalar", "ev", p.cnt("ev"))
        ac = p.ins("scalar", lambda e, hh=hh: e.activation(
            SQH[0:64, 0:2048], KH[0:64, hh * 2048:(hh + 1) * 2048],
            AF.Square), "ac")
        p.wait("tensor", "ac", ac)
        p.banks_req([4, 5, 6, 7])
        for nt in range(4):
            mmc = p.ins("tensor", lambda e, nt=nt: e.matmul(
                pb[0:1, 2048 + nt * 512:2048 + nt * 512 + 512],
                aux["onesc_h"][0:64, 0:1],
                SQH[0:64, nt * 512:(nt + 1) * 512],
                start=True, stop=True), "mm")
        p.wait("vector", "mm", mmc)
        ev = p.ins("vector", lambda e: e.tensor_copy(
            RR[32:33, 0:2048], pb[0:1, 2048:4096]), "ev")
        p.banks_done([4, 5, 6, 7], "ev", ev)
        p.ins("vector", lambda e, hh=hh: e.tensor_reduce(
            RR[0:1, 40 + hh:41 + hh], RR[32:33, 0:2048], AX.X, ALU.max))
        ev = p.ins("vector", lambda e: e.engine_nop(), "ev")
    p.ins("vector", lambda e: e.tensor_max(
        RR[0:1, 33:34], RR[0:1, 40:41], RR[0:1, 41:42]))
    p.ins("vector", lambda e: e.tensor_mul(
        RR[0:1, 34:35], RR[0:1, 32:33], RR[0:1, 33:34]))
    ev = p.ins("vector", lambda e: e.engine_nop(), "ev")
    p.wait("scalar", "ev", ev)
    ac = p.ins("scalar", lambda e: e.activation(
        RR[0:1, 34:35], RR[0:1, 34:35], AF.Sqrt), "ac")
    p.wait("vector", "ac", ac)
    ev = p.ins("vector", lambda e: e.tensor_scalar_mul(
        RR[0:1, 34:35], RR[0:1, 34:35], -1.0), "ev")
    p.wait("tensor", "ev", ev)
    p.banks_req([6])
    mmc = p.ins("tensor", lambda e: e.matmul(
        pb[:, 3072:3073], aux["onesr_f"][:], RR[0:1, 34:35],
        start=True, stop=True), "mm")
    p.wait("vector", "mm", mmc)
    ev = p.ins("vector", lambda e: e.tensor_copy(NEGMC[:], pb[:, 3072:3073]),
               "ev")
    p.banks_done([6], "ev", ev)

    p.wait("scalar", "ev", p.cnt("ev"))
    p.wait("tensor", "ev", p.cnt("ev"))
    TMPA = QSS
    for j in range(8):
        for s in range(16):
            base = (s % 2) * 64
            qap = Q[base:base + 64,
                    (s // 2) * T + j * 256:(s // 2) * T + (j + 1) * 256]
            pv0 = p.cnt("pv")
            for kc in range(32):
                slot = kc % 4
                bk = 4 + slot // 2
                col = bk * 512 + (slot % 2) * 256
                p.banks_req([bk])
                mmc = p.ins("tensor", lambda e, kc=kc, col=col, base=base,
                            qap=qap: e.matmul(
                    pb[:, col:col + 256],
                    KH[base:base + 64, kc * 128:(kc + 1) * 128],
                    qap, start=True, stop=True), "mm")
                p.wait("scalar", "mm", mmc)
                if kc >= 2:
                    p.wait("scalar", "pv", pv0 + kc - 1)
                ac = p.ins("scalar", lambda e, kc=kc, col=col: e.activation(
                    PT[:, (kc % 2) * 256:(kc % 2) * 256 + 256],
                    pb[:, col:col + 256], AF.Exp, bias=NEGMC[:, 0:1]), "ac")
                p.banks_done([bk], "ac", ac)
                p.wait("tensor", "ac", ac)
                if kc == 0:
                    p.banks_req([6])
                p.ins("tensor", lambda e, kc=kc: e.matmul(
                    pb[0:65, 3072:3072 + 256],
                    VA[:, kc * 65:(kc + 1) * 65],
                    PT[:, (kc % 2) * 256:(kc % 2) * 256 + 256],
                    start=(kc == 0), stop=(kc == 31)), "pv")
            mmpv = p.cnt("pv")
            p.wait("vector", "pv", mmpv)
            p.ins("vector", lambda e: e.tensor_copy(
                RR[0:1, 64:320], pb[64:65, 3072:3072 + 256]))
            p.ins("vector", lambda e: e.reciprocal(
                RR[0:1, 64:320], RR[0:1, 64:320]))
            ev = p.ins("vector", lambda e: e.tensor_copy(
                TMPA[64:128, 0:256], pb[0:64, 3072:3072 + 256]), "ev")
            p.banks_done([6], "ev", ev)
            p.wait("tensor", "ev", ev)
            p.banks_req([7])
            mmc = p.ins("tensor", lambda e: e.matmul(
                pb[0:64, 3584:3584 + 256], aux["onesr_f"][0:1, 0:64],
                RR[0:1, 64:320], start=True, stop=True), "mm")
            p.wait("vector", "mm", mmc)
            och = j // 2
            obase = (j % 2) * 64
            ev = p.ins("vector", lambda e, och=och, obase=obase, s=s:
                       e.tensor_mul(
                AO[obase:obase + 64, och * TF_ + s:och * TF_ + TF_:16],
                TMPA[64:128, 0:256], pb[0:64, 3584:3584 + 256]), "ev")
            p.banks_done([7], "ev", ev)

    p.wait("sync", "ev", p.cnt("ev"))
    stc = p.dma("sync", aout, AO[:, 0:4 * TF_], "st")
    p.wait("sync", "st", stc)
    return p




def build_attn_a_B():
    p = Prog("attn_a_B")
    xin = p.inp("xfm", [128, NCH * T])
    attn = p.inp("attn", [128, NCH * T], BF16)
    wo = p.inp("wo", [NCH, 128, NCH * 128])
    bo = p.inp("bo", [128, NCH])
    yout = p.outp("yfm", [128, NCH * T])
    aux = emit_aux(p)
    X = p.sbuf("X", [128, NCH * T], F32)
    A = p.sbuf("A", [128, NCH * T], BF16)
    WB = p.sbuf("WB", [128, 2 * 1024], BF16)
    bos = p.sbuf("bos", [128, NCH], F32)
    pb = p.psum("pb", [128, 4096], F32)
    p.dma("sync", X[:], xin, "ld")
    p.dma("sync", A[:], attn, "ld")
    lc = p.dma("sync", bos[:], bo, "ld")
    for eng in ("tensor", "vector"):
        p.wait(eng, "ld", lc)
    qmm = {}
    for m in range(NCH):
        wbuf = m % 2
        if m >= 2:
            p.wait("gpsimd", "mm", qmm[m - 2])
        wc = p.dma("gpsimd", WB[:, wbuf * 1024:(wbuf + 1) * 1024], wo[m],
                   f"wld{wbuf}")
        p.wait("tensor", f"wld{wbuf}", wc)
        for nt in range(4):
            bk = (m * 4 + nt) % 4
            p.banks_req([bk])
            for c in range(NCH):
                mmc = p.ins("tensor", lambda e, m=m, nt=nt, c=c, bk=bk,
                            wbuf=wbuf: e.matmul(
                    pb[:, bk * 512:bk * 512 + 512],
                    WB[:, wbuf * 1024 + c * 128:wbuf * 1024 + c * 128 + 128],
                    A[:, c * T + nt * 512:c * T + (nt + 1) * 512],
                    start=(c == 0), stop=(c == NCH - 1)),
                    "mm" if c == NCH - 1 else None)
            p.wait("vector", "mm", mmc)
            p.ins("vector", lambda e, m=m, nt=nt, bk=bk: e.tensor_scalar_add(
                pb[:, bk * 512:bk * 512 + 512],
                pb[:, bk * 512:bk * 512 + 512], bos[:, m:m + 1]))
            ev = p.ins("vector", lambda e, m=m, nt=nt, bk=bk: e.tensor_add(
                X[:, m * T + nt * 512:m * T + (nt + 1) * 512],
                pb[:, bk * 512:bk * 512 + 512],
                X[:, m * T + nt * 512:m * T + (nt + 1) * 512]), "ev")
            p.banks_done([bk], "ev", ev)
        qmm[m] = p.cnt("mm")
    p.wait("sync", "ev", p.cnt("ev"))
    stc = p.dma("sync", yout, X[:], "st")
    p.wait("sync", "st", stc)
    return p


def attn_a_inputs(inputs):
    wqT = np.ascontiguousarray(np.asarray(inputs["L0_attn_a_Wq"]).T)
    wkvT = np.ascontiguousarray(np.asarray(inputs["L0_attn_a_Wkv"]).T)
    woT = np.ascontiguousarray(np.asarray(inputs["L0_attn_a_Wo"]).T)
    bq = np.asarray(inputs["L0_attn_a_bq"])
    bkv = np.asarray(inputs["L0_attn_a_bkv"])
    bo = np.asarray(inputs["L0_attn_a_bo"])
    hsel = np.zeros((128, NCH * 16), BF)
    for c in range(NCH):
        hsel[0:64, c * 16 + 2 * c] = 1
        hsel[64:128, c * 16 + 2 * c + 1] = 1
    dA = {
        "wq": pack_wio(wqT),
        "wkv": wkvT,
        "bq": col_pack(bq),
        "bkvk": np.ascontiguousarray(bkv[:64].reshape(64, 1)),
        "bkvv": np.ascontiguousarray(bkv[64:].reshape(1, 64)),
        "hsel": hsel,
    }
    dA.update(host_aux())
    dB = {
        "wo": pack_wio(woT),
        "bo": col_pack(bo),
    }
    dB.update(host_aux())
    return dA, dB


def run_attn_a(x, inputs):
    ncA = get_prog("attn_a_A", build_attn_a_A)
    ncB = get_prog("attn_a_B", build_attn_a_B)
    dA, dB = attn_a_inputs(inputs)
    in_maps = []
    for c in range(NCORES):
        b, h = c // 2, c % 2
        m = dict(dA)
        xrot = np.concatenate([x[b, (1 - h) * T:(2 - h) * T],
                               x[b, h * T:(h + 1) * T]], axis=0)
        m["xfull"] = fm_pack(xrot)
        in_maps.append(m)
    res = run_spmd_cached(ncA, in_maps, core_ids=list(range(NCORES)))
    in_maps = []
    for c in range(NCORES):
        b, h = c // 2, c % 2
        a0 = np.asarray(res.results[2 * b]["aout"]).reshape(128, 4, 4096)
        a1 = np.asarray(res.results[2 * b + 1]["aout"]).reshape(128, 4, 4096)
        attn = np.concatenate([a0[:, :, h * T:(h + 1) * T],
                               a1[:, :, h * T:(h + 1) * T]], axis=1)
        m = dict(dB)
        m["attn"] = np.ascontiguousarray(attn.reshape(128, NCH * T))
        m["xfm"] = fm_pack(x[b, h * T:(h + 1) * T])
        in_maps.append(m)
    res = run_spmd_cached(ncB, in_maps, core_ids=list(range(NCORES)))
    out = np.empty_like(x)
    for c in range(NCORES):
        b, h = c // 2, c % 2
        out[b, h * T:(h + 1) * T] = fm_unpack(res.results[c]["yfm"], T)
    return out





def build_attn_b(nheads=16):
    p = Prog("attn_b")
    TF_ = 4096
    NKC = 32

    xfull = p.inp("xfull", [128, NCH * TF_])
    xown = p.inp("xown", [128, NCH * T])
    wq = p.inp("wq", [NCH, 128, NCH * 128])
    wk = p.inp("wk", [16, 128, NCH * 64])
    wv = p.inp("wv", [16, 128, NCH * 64])
    wproj = p.inp("wproj", [NCH, 128, NCH * 128])
    hsel = p.inp("hsel", [128, NCH * 16], BF16)
    invtab = p.inp("invtab", [128, 128])
    strip = p.inp("strip", [128, 4 * 512], BF16)
    yout = p.outp("yfm", [128, NCH * T])

    aux = emit_aux(p)
    XN = p.sbuf("XN", [128, NCH * TF_], BF16)
    pb = p.psum("pb", [128, 4096], F32)
    ah = Arena(p, "AH", 49184, BF16)
    af = Arena(p, "AF", 8704, F32)

    af.mark()
    XS = af.take(NCH * 512)
    for q8 in range(8):
        lc = None
        for c in range(NCH):
            lc = p.dma("sync", XS[:, c * 512:(c + 1) * 512],
                       xfull[:, c * TF_ + q8 * 512:c * TF_ + (q8 + 1) * 512],
                       "ld")
        for eng in ("tensor", "scalar", "vector"):
            p.wait(eng, "ld", lc)
        emit_ln(p, aux, lambda c, t0, t1, q8=q8: XS[:, c * 512 + t0:c * 512 + t1],
                lambda c, q8=q8: XN[:, c * TF_ + q8 * 512:c * TF_ + (q8 + 1) * 512],
                512, pb, af, f"s{q8}")
        p.wait("sync", "ev", p.cnt("ev"))
    af.release()

    KH = ah.take(TF_)
    VA = ah.take(32 * 65)
    Q = ah.take(NCH * T)
    AO = ah.take(NCH * T)
    WB = ah.take(2 * 1024)
    WKVB = ah.take(2 * 512)
    PT = ah.take(2 * 512)
    SQH = ah.take(2048)
    STR = ah.take(4 * 512)
    af.mark()
    QSS = af.take(2048)
    RR = af.take(2048)
    X2 = af.take(0)
    NEGB = p.sbuf("negb", [128, 128], F32)
    NEGMC = p.sbuf("negmc", [128, 1], F32)
    invs = p.sbuf("invs", [128, 128], F32)
    hsels = p.sbuf("hsels", [128, NCH * 16], BF16)

    lc = p.dma("sync", hsels[:], hsel, "ld2")
    lc = p.dma("sync", invs[:], invtab, "ld2")
    lc = p.dma("gpsimd", STR[:, 0:4 * 512], strip, "wld")
    ev = p.ins("vector", lambda e: e.memset(VA[:, 64:32 * 65:65], 1.0), "ev")
    p.wait("tensor", "ld2", p.cnt("ld2"))
    p.wait("tensor", "ev", p.cnt("ev"))
    p.wait("tensor", "wld", p.cnt("wld"))
    p.wait("scalar", "ld2", p.cnt("ld2"))
    p.wait("vector", "ld2", p.cnt("ld2"))

    qmm = {}
    for m in range(NCH):
        wbuf = m % 2
        if m >= 2:
            p.wait("gpsimd", "mm", qmm[m - 2])
        wc = p.dma("gpsimd", WB[:, wbuf * 1024:(wbuf + 1) * 1024], wq[m],
                   f"wq{wbuf}")
        p.wait("tensor", f"wq{wbuf}", wc)
        for nt in range(4):
            bk = (m * 4 + nt) % 4
            p.banks_req([bk])
            for c in range(NCH):
                mmc = p.ins("tensor", lambda e, m=m, nt=nt, c=c, bk=bk,
                            wbuf=wbuf: e.matmul(
                    pb[:, bk * 512:bk * 512 + 512],
                    WB[:, wbuf * 1024 + c * 128:wbuf * 1024 + c * 128 + 128],
                    XN[:, c * TF_ + 2048 + nt * 512:
                       c * TF_ + 2048 + (nt + 1) * 512],
                    start=(c == 0), stop=(c == NCH - 1)),
                    "mm" if c == NCH - 1 else None)
            p.wait("vector", "mm", mmc)
            ev = p.ins("vector", lambda e, m=m, nt=nt, bk=bk:
                       e.tensor_scalar_mul(
                Q[:, m * T + nt * 512:m * T + (nt + 1) * 512],
                pb[:, bk * 512:bk * 512 + 512], 0.125), "ev")
            p.banks_done([bk], "ev", ev)
        qmm[m] = p.cnt("mm")

    p.wait("scalar", "ev", p.cnt("ev"))
    p.banks_req([4, 5, 6, 7])
    for c in range(NCH):
        ac = p.ins("scalar", lambda e, c=c: e.activation(
            SQH[:, 0:2048], Q[:, c * T:(c + 1) * T], AF.Square), "ac")
        p.wait("tensor", "ac", ac)
        for nt in range(4):
            mmc = p.ins("tensor", lambda e, c=c, nt=nt: e.matmul(
                pb[0:16, 2048 + nt * 512:2048 + nt * 512 + 512],
                hsels[:, c * 16:(c + 1) * 16],
                SQH[:, nt * 512:(nt + 1) * 512],
                start=(c == 0), stop=(c == NCH - 1)),
                "mm" if nt == 3 else None)
        p.wait("scalar", "mm", mmc)
    p.wait("vector", "mm", p.cnt("mm"))
    ev = p.ins("vector", lambda e: e.tensor_copy(
        QSS[0:16, 0:2048], pb[0:16, 2048:4096]), "ev")
    p.banks_done([4, 5, 6, 7], "ev", ev)
    p.ins("vector", lambda e: e.tensor_reduce(
        RR[0:16, 0:1], QSS[0:16, 0:2048], AX.X, ALU.max))
    ev = p.ins("vector", lambda e: e.engine_nop(), "ev")
    p.wait("tensor", "ev", ev)
    p.banks_req([6])
    mmc = p.ins("tensor", lambda e: e.transpose(
        pb[0:1, 3072:3072 + 16], RR[0:16, 0:1], aux["eye_f"][0:16, 0:16]),
        "mm")
    p.wait("vector", "mm", mmc)
    ev = p.ins("vector", lambda e: e.tensor_copy(RR[0:1, 0:16],
                                                 pb[0:1, 3072:3088]), "ev")
    p.banks_done([6], "ev", ev)

    if nheads < 16:
        p.ins("vector", lambda e: e.memset(AO[:, 0:NCH * T], 0.0), "ev")
    for h in range(nheads):
        base = (h % 2) * 64
        qch = h // 2
        kbuf = h % 2
        if h >= 2:
            p.wait("gpsimd", "mm", p.cnt("mm") - 8)
        wc = p.dma("gpsimd", WKVB[:, kbuf * 512:kbuf * 512 + 512], wk[h],
                   f"wkv{kbuf}")
        p.wait("tensor", f"wkv{kbuf}", wc)
        for nt in range(8):
            bk = nt % 2
            p.banks_req([bk])
            for c in range(NCH):
                mmc = p.ins("tensor", lambda e, nt=nt, c=c, bk=bk, kbuf=kbuf:
                            e.matmul(
                    pb[0:64, bk * 512:bk * 512 + 512],
                    WKVB[:, kbuf * 512 + c * 64:kbuf * 512 + (c + 1) * 64],
                    XN[:, c * TF_ + nt * 512:c * TF_ + (nt + 1) * 512],
                    start=(c == 0), stop=(c == NCH - 1)),
                    "mm" if c == NCH - 1 else None)
            p.wait("vector", "mm", mmc)
            p.ins("vector", lambda e, nt=nt, bk=bk: e.tensor_copy(
                KH[0:64, nt * 512:(nt + 1) * 512],
                pb[0:64, bk * 512:bk * 512 + 512]))
            ev = p.ins("vector", lambda e, nt=nt, bk=bk: e.tensor_copy(
                KH[64:128, nt * 512:(nt + 1) * 512],
                pb[0:64, bk * 512:bk * 512 + 512]), "ev")
            p.banks_done([bk], "ev", ev)
        kmm = p.cnt("mm")
        p.wait("gpsimd", "mm", kmm)
        wcv = p.dma("gpsimd", WKVB[:, kbuf * 512:kbuf * 512 + 512], wv[h],
                    f"wkv{kbuf}")
        p.wait("tensor", f"wkv{kbuf}", wcv)
        for tt in range(32):
            bk = 2 + tt % 2
            p.banks_req([bk])
            for c in range(NCH):
                mmc = p.ins("tensor", lambda e, tt=tt, c=c, bk=bk, kbuf=kbuf:
                            e.matmul(
                    pb[:, bk * 512:bk * 512 + 64],
                    XN[:, c * TF_ + tt * 128:c * TF_ + (tt + 1) * 128],
                    WKVB[:, kbuf * 512 + c * 64:kbuf * 512 + (c + 1) * 64],
                    start=(c == 0), stop=(c == NCH - 1)),
                    "mm" if c == NCH - 1 else None)
            p.wait("vector", "mm", mmc)
            ev = p.ins("vector", lambda e, tt=tt, bk=bk: e.tensor_copy(
                VA[:, tt * 65:tt * 65 + 64], pb[:, bk * 512:bk * 512 + 64]),
                "ev")
            p.banks_done([bk], "ev", ev)
        p.wait("scalar", "ev", p.cnt("ev"))
        for hh in range(2):
            ac = p.ins("scalar", lambda e, hh=hh: e.activation(
                SQH[0:64, 0:2048], KH[0:64, hh * 2048:(hh + 1) * 2048],
                AF.Square), "ac")
            p.wait("tensor", "ac", ac)
            p.banks_req([4, 5, 6, 7])
            for nt in range(4):
                mmc = p.ins("tensor", lambda e, nt=nt: e.matmul(
                    pb[0:1, 2048 + nt * 512:2048 + nt * 512 + 512],
                    aux["onesc_h"][0:64, 0:1],
                    SQH[0:64, nt * 512:(nt + 1) * 512],
                    start=True, stop=True), "mm")
            p.wait("vector", "mm", mmc)
            ev = p.ins("vector", lambda e: e.tensor_copy(
                RR[32:33, 0:2048], pb[0:1, 2048:4096]), "ev")
            p.banks_done([4, 5, 6, 7], "ev", ev)
            p.ins("vector", lambda e, hh=hh: e.tensor_reduce(
                RR[0:1, 40 + hh:41 + hh], RR[32:33, 0:2048], AX.X, ALU.max))
            ev = p.ins("vector", lambda e: e.engine_nop(), "ev")
            p.wait("scalar", "ev", ev)
        p.ins("vector", lambda e: e.tensor_max(
            RR[0:1, 42:43], RR[0:1, 40:41], RR[0:1, 41:42]))
        p.ins("vector", lambda e, h=h: e.tensor_mul(
            RR[0:1, 43:44], RR[0:1, h:h + 1], RR[0:1, 42:43]))
        ev = p.ins("vector", lambda e: e.engine_nop(), "ev")
        p.wait("scalar", "ev", ev)
        ac = p.ins("scalar", lambda e: e.activation(
            RR[0:1, 43:44], RR[0:1, 43:44], AF.Sqrt), "ac")
        p.wait("vector", "ac", ac)
        ev = p.ins("vector", lambda e: e.tensor_scalar_mul(
            RR[0:1, 43:44], RR[0:1, 43:44], -1.0), "ev")
        p.wait("tensor", "ev", ev)
        p.banks_req([6])
        mmc = p.ins("tensor", lambda e: e.matmul(
            pb[:, 3072:3073], aux["onesr_f"][:], RR[0:1, 43:44],
            start=True, stop=True), "mm")
        p.wait("vector", "mm", mmc)
        ev = p.ins("vector", lambda e: e.tensor_copy(NEGMC[:],
                                                     pb[:, 3072:3073]), "ev")
        p.banks_done([6], "ev", ev)
        ev = p.ins("vector", lambda e: e.tensor_scalar(
            NEGB[:], invs[:], -1.0e30, NEGMC[:, 0:1], ALU.mult, ALU.add),
            "ev")
        p.wait("scalar", "ev", ev)
        p.wait("tensor", "ev", ev)

        for qg in range(4):
            qap = Q[base:base + 64, qch * T + qg * 512:qch * T + (qg + 1) * 512]
            nkc = 20 + 4 * qg
            pv0 = p.cnt("pv")
            for kc in range(nkc):
                slot = kc % 2
                bk = 4 + slot
                col = bk * 512
                p.banks_req([bk])
                mmc = p.ins("tensor", lambda e, kc=kc, col=col, base=base,
                            qap=qap: e.matmul(
                    pb[:, col:col + 512],
                    KH[base:base + 64, kc * 128:(kc + 1) * 128],
                    qap, start=True, stop=True), "mm")
                p.wait("scalar", "mm", mmc)
                if kc >= 2:
                    p.wait("scalar", "pv", pv0 + kc - 1)
                ac = p.ins("scalar", lambda e, kc=kc, col=col, qg=qg:
                           e.activation(
                    PT[:, (kc % 2) * 512:(kc % 2) * 512 + 512],
                    pb[:, col:col + 512], AF.Exp,
                    bias=NEGB[:, qg * 32 + kc:qg * 32 + kc + 1]), "ac")
                p.banks_done([bk], "ac", ac)
                dj = -1
                if 16 + 4 * qg <= kc < 16 + 4 * qg + 4:
                    dj = kc - 16 - 4 * qg
                if dj >= 0:
                    p.wait("vector", "ac", ac)
                    sv = p.ins("vector", lambda e, kc=kc, dj=dj: e.tensor_mul(
                        PT[:, (kc % 2) * 512:(kc % 2) * 512 + 512],
                        PT[:, (kc % 2) * 512:(kc % 2) * 512 + 512],
                        STR[:, dj * 512:(dj + 1) * 512]), "sv")
                    p.wait("tensor", "sv", sv)
                p.wait("tensor", "ac", ac)
                if kc == 0:
                    p.banks_req([6])
                p.ins("tensor", lambda e, kc=kc, nkc=nkc: e.matmul(
                    pb[0:65, 3072:3072 + 512],
                    VA[:, kc * 65:(kc + 1) * 65],
                    PT[:, (kc % 2) * 512:(kc % 2) * 512 + 512],
                    start=(kc == 0), stop=(kc == nkc - 1)), "pv")
            p.wait("vector", "pv", p.cnt("pv"))
            p.ins("vector", lambda e: e.tensor_copy(
                RR[0:1, 64:576], pb[64:65, 3072:3072 + 512]))
            p.ins("vector", lambda e: e.reciprocal(
                RR[0:1, 64:576], RR[0:1, 64:576]))
            ev = p.ins("vector", lambda e: e.tensor_copy(
                QSS[64:128, 0:512], pb[0:64, 3072:3072 + 512]), "ev")
            p.banks_done([6], "ev", ev)
            p.wait("tensor", "ev", ev)
            p.banks_req([7])
            mmc = p.ins("tensor", lambda e: e.matmul(
                pb[0:64, 3584:3584 + 512], aux["onesr_f"][0:1, 0:64],
                RR[0:1, 64:576], start=True, stop=True), "mm")
            p.wait("vector", "mm", mmc)
            ev = p.ins("vector", lambda e, base=base, qch=qch, qg=qg:
                       e.tensor_mul(
                AO[base:base + 64, qch * T + qg * 512:qch * T + (qg + 1) * 512],
                QSS[64:128, 0:512], pb[0:64, 3584:3584 + 512]), "ev")
            p.banks_done([7], "ev", ev)

    XB = af.take(2 * 512)
    YB = af.take(2 * 512)
    p.wait("tensor", "ev", p.cnt("ev"))
    qmm = {}
    stbuf = {0: 0, 1: 0}
    k_it = 0
    for m in range(NCH):
        wbuf = m % 2
        if m >= 2:
            p.wait("gpsimd", "mm", qmm[m - 2])
        wc = p.dma("gpsimd", WB[:, wbuf * 1024:(wbuf + 1) * 1024], wproj[m],
                   f"wq{wbuf}")
        p.wait("tensor", f"wq{wbuf}", wc)
        for nt in range(4):
            xb = k_it % 2
            p.wait("sync", f"st{xb}", stbuf[xb])
            lc = p.dma("sync", XB[:, xb * 512:(xb + 1) * 512],
                       xown[:, m * T + nt * 512:m * T + (nt + 1) * 512],
                       f"xld{xb}")
            bk = (m * 4 + nt) % 4
            p.banks_req([bk])
            for c in range(NCH):
                mmc = p.ins("tensor", lambda e, m=m, nt=nt, c=c, bk=bk,
                            wbuf=wbuf: e.matmul(
                    pb[:, bk * 512:bk * 512 + 512],
                    WB[:, wbuf * 1024 + c * 128:wbuf * 1024 + c * 128 + 128],
                    AO[:, c * T + nt * 512:c * T + (nt + 1) * 512],
                    start=(c == 0), stop=(c == NCH - 1)),
                    "mm" if c == NCH - 1 else None)
            p.wait("vector", "mm", mmc)
            p.wait("vector", f"xld{xb}", lc)
            p.wait("vector", f"st{xb}", stbuf[xb])
            ev = p.ins("vector", lambda e, bk=bk, xb=xb: e.tensor_add(
                YB[:, xb * 512:(xb + 1) * 512],
                pb[:, bk * 512:bk * 512 + 512],
                XB[:, xb * 512:(xb + 1) * 512]), "ev")
            p.banks_done([bk], "ev", ev)
            p.wait("sync", "ev", ev)
            stbuf[xb] = p.dma("sync",
                              yout[:, m * T + nt * 512:m * T + (nt + 1) * 512],
                              YB[:, xb * 512:(xb + 1) * 512], f"st{xb}")
            k_it += 1
        qmm[m] = p.cnt("mm")
    p.wait("sync", "st0", p.cnt("st0"))
    p.wait("sync", "st1", p.cnt("st1"))
    return p


def attn_b_inputs(inputs):
    W = np.asarray(inputs["L0_attn_b_W_attn"])
    Wp = np.asarray(inputs["L0_attn_b_W_proj"])
    hsel = np.zeros((128, NCH * 16), BF)
    for c in range(NCH):
        hsel[0:64, c * 16 + 2 * c] = 1
        hsel[64:128, c * 16 + 2 * c + 1] = 1
    wk = np.stack([np.ascontiguousarray(
        W[:, 1024 + h * 64:1024 + (h + 1) * 64]
        .reshape(NCH, 128, 64).transpose(1, 0, 2).reshape(128, NCH * 64))
        for h in range(16)])
    wv = np.stack([np.ascontiguousarray(
        W[:, 2048 + h * 64:2048 + (h + 1) * 64]
        .reshape(NCH, 128, 64).transpose(1, 0, 2).reshape(128, NCH * 64))
        for h in range(16)])
    d = {
        "wq": pack_wio(np.ascontiguousarray(W[:, :1024])),
        "wk": wk,
        "wv": wv,
        "wproj": pack_wio(Wp),
        "hsel": hsel,
    }
    d.update(host_aux())
    percore = []
    p_ = np.arange(128)[:, None]
    f_ = np.arange(512)[None, :]
    st = np.zeros((128, 4 * 512), BF)
    for j in range(4):
        st[:, j * 512:(j + 1) * 512] = ((j * 128 + p_) <= f_)
    for h in range(2):
        inv = np.zeros((128, 128), np.float32)
        for qg in range(4):
            for kc in range(32):
                if kc < 16:
                    inv[:, qg * 32 + kc] = 1.0 if h == 0 else 0.0
                else:
                    keys = (kc - 16) * 128 + np.arange(128)
                    inv[:, qg * 32 + kc] = (keys > qg * 512 + 511)
            for j in range(4):
                inv[:, qg * 32 + 16 + 4 * qg + j] = 0
        percore.append({"invtab": inv, "strip": st})
    return d, percore


def run_attn_b(x, inputs):
    nc = get_prog("attn_b", build_attn_b)
    d, percore = attn_b_inputs(inputs)
    in_maps = []
    for c in range(NCORES):
        b, h = c // 2, c % 2
        m = dict(d)
        m.update(percore[h])
        xrot = np.concatenate([x[b, (1 - h) * T:(2 - h) * T],
                               x[b, h * T:(h + 1) * T]], axis=0)
        m["xfull"] = fm_pack(xrot)
        m["xown"] = fm_pack(x[b, h * T:(h + 1) * T])
        in_maps.append(m)
    res = run_spmd_cached(nc, in_maps, core_ids=list(range(NCORES)))
    out = np.empty_like(x)
    for c in range(NCORES):
        b, h = c // 2, c % 2
        out[b, h * T:(h + 1) * T] = fm_unpack(res.results[c]["yfm"], T)
    return out





def build_l1_attn_b(dbg_qss=False):
    p = Prog("l1_attn_b")
    TF_ = 4096

    xfull = p.inp("xfull", [128, NCH * TF_])
    xown = p.inp("xown", [128, NCH * T])
    wq = p.inp("wq", [NCH, 128, NCH * 128])
    wk = p.inp("wk", [NCH, 128, NCH * 128])
    wv = p.inp("wv", [D, D])
    wo = p.inp("wo", [NCH, 128, NCH * 128])
    ob = p.inp("ob", [128, NCH])
    yout = p.outp("yfm", [128, NCH * T])

    aux = emit_aux(p)
    pb = p.psum("pb", [128, 4096], F32)
    ah = Arena(p, "AH", 82000, BF16)
    af = Arena(p, "AF", 9728, F32)
    v_scr = p.dram("v_scr", [32, 128, 1024], BF16)

    KF = ah.take(NCH * TF_)
    Q = ah.take(NCH * T)
    WVB = ah.take(NCH * 1024)
    WQB = ah.take(2 * 1024)
    VT = ah.take(2 * 1024)
    SQB = ah.take(4096)
    ah.mark()
    XNH = ah.take(NCH * T)
    obs = p.sbuf("obs", [128, NCH], F32)
    NEGMC = p.sbuf("negmc", [128, 1], F32)
    lc = p.dma("sync", obs[:], ob, "ld2")

    gwq = {}
    gq = [0]
    vstb = {0: 0, 1: 0}

    def wq_guard():
        p.wait("gpsimd", "mm", gwq.get(gq[0] - 2, 0))

    def wq_done():
        gwq[gq[0]] = p.cnt("mm")
        gq[0] += 1

    for half in range(2):
        af.mark()
        XS = af.take(NCH * 512)
        for q8 in range(4):
            lc = None
            for c in range(NCH):
                lc = p.dma("sync", XS[:, c * 512:(c + 1) * 512],
                           xfull[:, c * TF_ + half * T + q8 * 512:
                                 c * TF_ + half * T + (q8 + 1) * 512], "ld")
            for eng in ("tensor", "scalar", "vector"):
                p.wait(eng, "ld", lc)
            emit_ln(p, aux,
                    lambda c, t0, t1, q8=q8: XS[:, c * 512 + t0:c * 512 + t1],
                    lambda c, q8=q8: XNH[:, c * T + q8 * 512:
                                         c * T + (q8 + 1) * 512],
                    512, pb, af, f"h{half}s{q8}")
            p.wait("sync", "ev", p.cnt("ev"))
        af.release()
        p.wait("tensor", "ev", p.cnt("ev"))

        for m in range(NCH):
            wbuf = gq[0] % 2
            wq_guard()
            wc = p.dma("gpsimd", WQB[:, wbuf * 1024:(wbuf + 1) * 1024],
                       wk[m], f"wq{wbuf}")
            p.wait("tensor", f"wq{wbuf}", wc)
            for nt in range(4):
                bk = (m * 4 + nt) % 4
                p.banks_req([bk])
                for c in range(NCH):
                    mmc = p.ins("tensor", lambda e, m=m, nt=nt, c=c, bk=bk,
                                wbuf=wbuf: e.matmul(
                        pb[:, bk * 512:bk * 512 + 512],
                        WQB[:, wbuf * 1024 + c * 128:
                            wbuf * 1024 + c * 128 + 128],
                        XNH[:, c * T + nt * 512:c * T + (nt + 1) * 512],
                        start=(c == 0), stop=(c == NCH - 1)),
                        "mm" if c == NCH - 1 else None)
                p.wait("vector", "mm", mmc)
                ev = p.ins("vector", lambda e, m=m, nt=nt, bk=bk, half=half:
                           e.tensor_copy(
                    KF[:, m * TF_ + half * T + nt * 512:
                       m * TF_ + half * T + (nt + 1) * 512],
                    pb[:, bk * 512:bk * 512 + 512]), "ev")
                p.banks_done([bk], "ev", ev)
            wq_done()

        wcv = wload_chunks(p, WVB, NCH, 1024, wv, "mm", gwq[gq[0] - 2])
        p.wait("tensor", "wld", wcv)
        for tt in range(16):
            vb = tt % 2
            for n2 in range(2):
                bk = 4 + (tt * 2 + n2) % 2
                p.banks_req([bk])
                for c in range(NCH):
                    mmc = p.ins("tensor", lambda e, tt=tt, n2=n2, c=c, bk=bk:
                                e.matmul(
                        pb[:, bk * 512:bk * 512 + 512],
                        XNH[:, c * T + tt * 128:c * T + (tt + 1) * 128],
                        WVB[:, c * 1024 + n2 * 512:c * 1024 + (n2 + 1) * 512],
                        start=(c == 0), stop=(c == NCH - 1)),
                        "mm" if c == NCH - 1 else None)
                p.wait("vector", "mm", mmc)
                p.wait("vector", f"vst{vb}", vstb[vb])
                ev = p.ins("vector", lambda e, vb=vb, n2=n2, bk=bk:
                           e.tensor_copy(
                    VT[:, vb * 1024 + n2 * 512:vb * 1024 + (n2 + 1) * 512],
                    pb[:, bk * 512:bk * 512 + 512]), "ev")
                p.banks_done([bk], "ev", ev)
            p.wait("sync", "ev", p.cnt("ev"))
            vstb[vb] = p.dma("sync", v_scr.ap()[half * 16 + tt],
                             VT[:, vb * 1024:(vb + 1) * 1024], f"vst{vb}")

        if half == 1:
            for m in range(NCH):
                wbuf = gq[0] % 2
                wq_guard()
                wc = p.dma("gpsimd", WQB[:, wbuf * 1024:(wbuf + 1) * 1024],
                           wq[m], f"wq{wbuf}")
                p.wait("tensor", f"wq{wbuf}", wc)
                for nt in range(4):
                    bk = (m * 4 + nt) % 4
                    p.banks_req([bk])
                    for c in range(NCH):
                        mmc = p.ins("tensor", lambda e, m=m, nt=nt, c=c,
                                    bk=bk, wbuf=wbuf: e.matmul(
                            pb[:, bk * 512:bk * 512 + 512],
                            WQB[:, wbuf * 1024 + c * 128:
                                wbuf * 1024 + c * 128 + 128],
                            XNH[:, c * T + nt * 512:c * T + (nt + 1) * 512],
                            start=(c == 0), stop=(c == NCH - 1)),
                            "mm" if c == NCH - 1 else None)
                    p.wait("vector", "mm", mmc)
                    ev = p.ins("vector", lambda e, m=m, nt=nt, bk=bk:
                               e.tensor_scalar_mul(
                        Q[:, m * T + nt * 512:m * T + (nt + 1) * 512],
                        pb[:, bk * 512:bk * 512 + 512], 1.0 / 32.0), "ev")
                    p.banks_done([bk], "ev", ev)
                wq_done()
    ah.release()
    PROBS = ah.take(32 * 512)
    TMPF = af.take(512)
    RR = af.take(2048)

    p.wait("scalar", "ev", p.cnt("ev"))
    p.banks_req([0, 1, 2, 3])
    for c in range(NCH):
        ac = p.ins("scalar", lambda e, c=c: e.activation(
            SQB[:, 0:2048], Q[:, c * T:(c + 1) * T], AF.Square), "ac")
        p.wait("tensor", "ac", ac)
        for nt in range(4):
            mmc = p.ins("tensor", lambda e, c=c, nt=nt: e.matmul(
                pb[0:1, nt * 512:nt * 512 + 512],
                aux["onesc_h"][:],
                SQB[:, nt * 512:(nt + 1) * 512],
                start=(c == 0), stop=(c == NCH - 1)),
                "mm" if nt == 3 else None)
        p.wait("scalar", "mm", mmc)
    p.wait("vector", "mm", p.cnt("mm"))
    ev = p.ins("vector", lambda e: e.tensor_copy(RR[0:1, 0:2048],
                                                 pb[0:1, 0:2048]), "ev")
    p.banks_done([0, 1, 2, 3], "ev", ev)
    if dbg_qss:
        dq = p.outp("d_qss", [1, 2048], F32)
        p.wait("sync", "ev", p.cnt("ev"))
        p.dma("sync", dq, RR[0:1, 0:2048], "st0")
        dq2 = p.outp("d_sqb", [128, 2048], BF16)
        p.dma("sync", dq2, SQB[:, 0:2048], "st0")
    p.ins("vector", lambda e: e.tensor_reduce(
        TMPF[0:1, 0:1], RR[0:1, 0:2048], AX.X, ALU.max))
    ev = p.ins("vector", lambda e: e.engine_nop(), "ev")
    for kk in range(2):
        p.wait("scalar", "ev", p.cnt("ev"))
        p.banks_req([0, 1, 2, 3])
        for c in range(NCH):
            ac = p.ins("scalar", lambda e, c=c, kk=kk: e.activation(
                SQB[:, 0:2048], KF[:, c * TF_ + kk * T:c * TF_ + (kk + 1) * T],
                AF.Square), "ac")
            p.wait("tensor", "ac", ac)
            for nt in range(4):
                mmc = p.ins("tensor", lambda e, c=c, nt=nt: e.matmul(
                    pb[0:1, nt * 512:nt * 512 + 512],
                    aux["onesc_h"][:],
                    SQB[:, nt * 512:(nt + 1) * 512],
                    start=(c == 0), stop=(c == NCH - 1)),
                    "mm" if nt == 3 else None)
            p.wait("scalar", "mm", mmc)
        p.wait("vector", "mm", p.cnt("mm"))
        ev = p.ins("vector", lambda e: e.tensor_copy(RR[0:1, 0:2048],
                                                     pb[0:1, 0:2048]), "ev")
        p.banks_done([0, 1, 2, 3], "ev", ev)
        p.ins("vector", lambda e, kk=kk: e.tensor_reduce(
            TMPF[0:1, 1 + kk:2 + kk], RR[0:1, 0:2048], AX.X, ALU.max))
        ev = p.ins("vector", lambda e: e.engine_nop(), "ev")
    p.ins("vector", lambda e: e.tensor_max(
        TMPF[0:1, 3:4], TMPF[0:1, 1:2], TMPF[0:1, 2:3]))
    p.ins("vector", lambda e: e.tensor_mul(
        TMPF[0:1, 4:5], TMPF[0:1, 0:1], TMPF[0:1, 3:4]))
    ev = p.ins("vector", lambda e: e.engine_nop(), "ev")
    p.wait("scalar", "ev", ev)
    ac = p.ins("scalar", lambda e: e.activation(
        TMPF[0:1, 4:5], TMPF[0:1, 4:5], AF.Sqrt), "ac")
    p.wait("vector", "ac", ac)
    ev = p.ins("vector", lambda e: e.tensor_scalar_mul(
        TMPF[0:1, 4:5], TMPF[0:1, 4:5], -1.0), "ev")
    p.wait("tensor", "ev", ev)
    p.banks_req([6])
    mmc = p.ins("tensor", lambda e: e.matmul(
        pb[:, 3072:3073], aux["onesr_f"][:], TMPF[0:1, 4:5],
        start=True, stop=True), "mm")
    p.wait("vector", "mm", mmc)
    ev = p.ins("vector", lambda e: e.tensor_copy(NEGMC[:], pb[:, 3072:3073]),
               "ev")
    p.banks_done([6], "ev", ev)
    p.wait("scalar", "ev", p.cnt("ev"))
    p.wait("tensor", "ev", p.cnt("ev"))

    stbuf = {0: 0, 1: 0}
    vld = {0: 0, 1: 0}
    YF = SQB
    XB2 = af.take(2 * 512)
    YB2 = af.take(2 * 512)
    k_it = 0
    p.wait("sync", "vst0", p.cnt("vst0"))
    p.wait("sync", "vst1", p.cnt("vst1"))
    for qg in range(4):
        for kc in range(32):
            bk = kc % 2
            p.banks_req([bk])
            for c in range(NCH):
                mmc = p.ins("tensor", lambda e, kc=kc, c=c, bk=bk, qg=qg:
                            e.matmul(
                    pb[:, bk * 512:bk * 512 + 512],
                    KF[:, c * TF_ + kc * 128:c * TF_ + (kc + 1) * 128],
                    Q[:, c * T + qg * 512:c * T + (qg + 1) * 512],
                    start=(c == 0), stop=(c == NCH - 1)),
                    "mm" if c == NCH - 1 else None)
            p.wait("scalar", "mm", mmc)
            ac = p.ins("scalar", lambda e, kc=kc, bk=bk: e.activation(
                PROBS[:, kc * 512:(kc + 1) * 512],
                pb[:, bk * 512:bk * 512 + 512], AF.Exp,
                bias=NEGMC[:, 0:1]), "ac")
            p.banks_done([bk], "ac", ac)
        p.wait("tensor", "ac", p.cnt("ac"))
        p.banks_req(range(8))
        for kc in range(32):
            vb = kc % 2
            p.wait("sync", "pv", vld[vb])
            lcv = p.dma("sync", VT[:, vb * 1024:(vb + 1) * 1024],
                        v_scr.ap()[kc], f"vld{vb}")
            p.wait("tensor", f"vld{vb}", lcv)
            for dc in range(7):
                p.ins("tensor", lambda e, kc=kc, dc=dc, vb=vb: e.matmul(
                    pb[:, dc * 512:dc * 512 + 512],
                    VT[:, vb * 1024 + dc * 128:vb * 1024 + dc * 128 + 128],
                    PROBS[:, kc * 512:(kc + 1) * 512],
                    start=(kc == 0), stop=(kc == 31)))
            vld[vb] = p.ins("tensor", lambda e, kc=kc: e.matmul(
                pb[0:1, 3584:3584 + 512], aux["onesc_h"][:],
                PROBS[:, kc * 512:(kc + 1) * 512],
                start=(kc == 0), stop=(kc == 31)), "pv")
        p.wait("vector", "pv", p.cnt("pv"))
        p.ins("vector", lambda e: e.tensor_copy(RR[0:1, 0:512],
                                                pb[0:1, 3584:3584 + 512]))
        ev = p.ins("vector", lambda e: e.reciprocal(RR[0:1, 0:512],
                                                    RR[0:1, 0:512]), "ev")
        p.wait("tensor", "ev", ev)
        mmc = p.ins("tensor", lambda e: e.matmul(
            pb[:, 3584:3584 + 512], aux["onesr_f"][:], RR[0:1, 0:512],
            start=True, stop=True), "mm")
        p.wait("vector", "mm", mmc)
        for dc in range(7):
            p.ins("vector", lambda e, dc=dc: e.tensor_copy(
                TMPF[:, 0:512], pb[:, dc * 512:dc * 512 + 512]))
            ev = p.ins("vector", lambda e, dc=dc: e.tensor_mul(
                YF[:, dc * 512:(dc + 1) * 512], TMPF[:, 0:512],
                pb[:, 3584:3584 + 512]), "ev")
            p.banks_done([dc], "ev", ev)
        p.banks_req([0])
        for kc in range(32):
            vb = kc % 2
            p.wait("sync", "pv", vld[vb])
            lcv = p.dma("sync", VT[:, vb * 1024:(vb + 1) * 1024],
                        v_scr.ap()[kc], f"vld{vb}")
            p.wait("tensor", f"vld{vb}", lcv)
            vld[vb] = p.ins("tensor", lambda e, kc=kc, vb=vb: e.matmul(
                pb[:, 0:512],
                VT[:, vb * 1024 + 7 * 128:vb * 1024 + 7 * 128 + 128],
                PROBS[:, kc * 512:(kc + 1) * 512],
                start=(kc == 0), stop=(kc == 31)), "pv")
        p.wait("vector", "pv", p.cnt("pv"))
        p.ins("vector", lambda e: e.tensor_copy(TMPF[:, 0:512], pb[:, 0:512]))
        ev = p.ins("vector", lambda e: e.tensor_mul(
            YF[:, 7 * 512:8 * 512], TMPF[:, 0:512],
            pb[:, 3584:3584 + 512]), "ev")
        p.banks_done([0], "ev", ev)
        p.banks_done([7], "ev", ev)

        p.wait("tensor", "ev", p.cnt("ev"))
        p.wait("vector", "ld2", p.cnt("ld2"))
        for m in range(NCH):
            wbuf = gq[0] % 2
            wq_guard()
            wc = p.dma("gpsimd", WQB[:, wbuf * 1024:(wbuf + 1) * 1024],
                       wo[m], f"wq{wbuf}")
            p.wait("tensor", f"wq{wbuf}", wc)
            xb = k_it % 2
            p.wait("sync", f"st{xb}", stbuf[xb])
            lc = p.dma("sync", XB2[:, xb * 512:(xb + 1) * 512],
                       xown[:, m * T + qg * 512:m * T + (qg + 1) * 512],
                       f"xld{xb}")
            bk = 1 + (m % 2)
            p.banks_req([bk])
            for c in range(NCH):
                mmc = p.ins("tensor", lambda e, m=m, c=c, bk=bk, wbuf=wbuf:
                            e.matmul(
                    pb[:, bk * 512:bk * 512 + 512],
                    WQB[:, wbuf * 1024 + c * 128:wbuf * 1024 + c * 128 + 128],
                    YF[:, c * 512:(c + 1) * 512],
                    start=(c == 0), stop=(c == NCH - 1)),
                    "mm" if c == NCH - 1 else None)
            p.wait("vector", "mm", mmc)
            p.wait("vector", f"xld{xb}", lc)
            p.wait("vector", f"st{xb}", stbuf[xb])
            p.ins("vector", lambda e, m=m, bk=bk: e.tensor_scalar_add(
                pb[:, bk * 512:bk * 512 + 512],
                pb[:, bk * 512:bk * 512 + 512], obs[:, m:m + 1]))
            ev = p.ins("vector", lambda e, bk=bk, xb=xb: e.tensor_add(
                YB2[:, xb * 512:(xb + 1) * 512],
                pb[:, bk * 512:bk * 512 + 512],
                XB2[:, xb * 512:(xb + 1) * 512]), "ev")
            p.banks_done([bk], "ev", ev)
            p.wait("sync", "ev", ev)
            stbuf[xb] = p.dma(
                "sync", yout[:, m * T + qg * 512:m * T + (qg + 1) * 512],
                YB2[:, xb * 512:(xb + 1) * 512], f"st{xb}")
            k_it += 1
            wq_done()
    p.wait("sync", "st0", p.cnt("st0"))
    p.wait("sync", "st1", p.cnt("st1"))
    return p


def l1_attn_b_inputs(inputs):
    W = np.asarray(inputs["L1_attn_b_weight_qkv"])
    wo = np.ascontiguousarray(np.asarray(inputs["L1_attn_b_out_w"]).T)
    d = {
        "wq": pack_wio(np.ascontiguousarray(W[:, 0:1024])),
        "wk": pack_wio(np.ascontiguousarray(W[:, 1024:2048])),
        "wv": np.ascontiguousarray(W[:, 2048:3072]),
        "wo": pack_wio(wo),
        "ob": col_pack(np.asarray(inputs["L1_attn_b_out_b"])),
    }
    d.update(host_aux())
    return d


def run_l1_attn_b(x, inputs):
    nc = get_prog("l1_attn_b", build_l1_attn_b)
    d = l1_attn_b_inputs(inputs)
    in_maps = []
    for c in range(NCORES):
        b, h = c // 2, c % 2
        m = dict(d)
        xrot = np.concatenate([x[b, (1 - h) * T:(2 - h) * T],
                               x[b, h * T:(h + 1) * T]], axis=0)
        m["xfull"] = fm_pack(xrot)
        m["xown"] = fm_pack(x[b, h * T:(h + 1) * T])
        in_maps.append(m)
    res = run_spmd_cached(nc, in_maps, core_ids=list(range(NCORES)))
    out = np.empty_like(x)
    for c in range(NCORES):
        b, h = c // 2, c % 2
        out[b, h * T:(h + 1) * T] = fm_unpack(res.results[c]["yfm"], T)
    return out





def build_l1_attn_a():
    p = Prog("l1_attn_a")
    T3 = 3072
    KR = 2560

    xbuf = p.inp("xbuf", [128, NCH * T3])
    xown = p.inp("xown", [128, NCH * T])
    wq = p.inp("wq", [NCH, 128, NCH * 128])
    wk = p.inp("wk", [NCH, 128, NCH * 128])
    wv = p.inp("wv", [NCH, 128, NCH * 128])
    wo = p.inp("wo", [NCH, 128, NCH * 128])
    bq = p.inp("bq", [128, NCH])
    bk = p.inp("bk", [128, NCH])
    bv = p.inp("bv", [128, NCH])
    bo = p.inp("bo", [128, NCH])
    cqw = p.inp("cqw", [128, NCH * 4])
    ckw = p.inp("ckw", [128, NCH * 4])
    cvw = p.inp("cvw", [128, NCH * 4])
    cqb = p.inp("cqb", [128, NCH])
    ckb = p.inp("ckb", [128, NCH])
    cvb = p.inp("cvb", [128, NCH])
    tcos = p.inp("tcos", [128, T3])
    tsin = p.inp("tsin", [128, T3])
    hsel2 = p.inp("hsel2", [128, 2])
    hsel2t = p.inp("hsel2t", [2, 128])
    biastab = p.inp("biastab", [128, 8])
    hflag = p.inp("hflag", [128, 1])
    stripd = p.inp("stripd", [128, 2 * 128], BF16)
    yout = p.outp("yfm", [128, NCH * T])

    aux = emit_aux(p)
    pb = p.psum("pb", [128, 4096], F32)
    ah = Arena(p, "AH", 26400, BF16)
    af = Arena(p, "AF", 27000, F32)

    XN3 = p.sbuf("XN3", [128, NCH * T3], BF16)
    smalls = {}
    for nm, ap_, sh in (("bq", bq, [128, NCH]), ("bk", bk, [128, NCH]),
                        ("bv", bv, [128, NCH]), ("bo", bo, [128, NCH]),
                        ("cqw", cqw, [128, NCH * 4]),
                        ("ckw", ckw, [128, NCH * 4]),
                        ("cvw", cvw, [128, NCH * 4]),
                        ("cqb", cqb, [128, NCH]), ("ckb", ckb, [128, NCH]),
                        ("cvb", cvb, [128, NCH]),
                        ("hsel2", hsel2, [128, 2]), ("hsel2t", hsel2t, [2, 128]),
                        ("biastab", biastab, [128, 8]),
                        ("hflag", hflag, [128, 1])):
        smalls[nm] = p.sbuf("s_" + nm, sh, F32)
        p.dma("sync", smalls[nm][:], ap_, "ld2")
    STRD = ah.take(2 * 128)
    p.dma("gpsimd", STRD[:, 0:256], stripd, "wld")
    TC = af.take(T3)
    TS = af.take(T3)
    p.dma("sync", TC[:, 0:T3], tcos, "ld2")
    p.dma("sync", TS[:, 0:T3], tsin, "ld2")

    af.mark()
    XS = af.take(NCH * 512)
    for q6 in range(6):
        lc = None
        for c in range(NCH):
            lc = p.dma("sync", XS[:, c * 512:(c + 1) * 512],
                       xbuf[:, c * T3 + q6 * 512:c * T3 + (q6 + 1) * 512],
                       "ld")
        for eng in ("tensor", "scalar", "vector"):
            p.wait(eng, "ld", lc)
        emit_ln(p, aux,
                lambda c, t0, t1, q6=q6: XS[:, c * 512 + t0:c * 512 + t1],
                lambda c, q6=q6: XN3[:, c * T3 + q6 * 512:
                                     c * T3 + (q6 + 1) * 512],
                512, pb, af, f"s{q6}")
        p.wait("sync", "ev", p.cnt("ev"))
    af.release()
    p.wait("tensor", "ev", p.cnt("ev"))
    p.wait("tensor", "ld2", p.cnt("ld2"))
    p.wait("vector", "ld2", p.cnt("ld2"))
    p.wait("scalar", "ld2", p.cnt("ld2"))
    p.wait("tensor", "wld", p.cnt("wld"))

    PROJ = af.take(T3)
    QC = af.take(T)
    KC = af.take(KR)
    VC = af.take(KR)
    SQ = af.take(T3)
    XR = SQ
    R2 = af.take(KR)
    WBUF = ah.take(2 * 1024)
    QH = ah.take(T)
    KH2 = ah.take(KR)
    VT = ah.take(2 * 20 * 65)
    AO = ah.take(NCH * T)
    PT = ah.take(2 * 128)
    NEGC = p.sbuf("negc", [128, 1], F32)
    ev = p.ins("vector", lambda e: e.memset(VT[:, 64:2 * 20 * 65:65], 1.0),
               "ev")

    gwq = {}
    gq = [0]

    def wq_guard():
        p.wait("gpsimd", "mm", gwq.get(gq[0] - 2, 0))

    def wq_done():
        gwq[gq[0]] = p.cnt("mm")
        gq[0] += 1

    def proj(dst, wsrc, hc, width, off3, bias_col):
        wbuf = gq[0] % 2
        wq_guard()
        wc = p.dma("gpsimd", WBUF[:, wbuf * 1024:(wbuf + 1) * 1024],
                   wsrc[hc], f"wq{wbuf}")
        p.wait("tensor", f"wq{wbuf}", wc)
        ntile = width // 512
        for nt in range(ntile):
            bkk = nt % 4
            p.banks_req([bkk])
            for c in range(NCH):
                mmc = p.ins("tensor", lambda e, nt=nt, c=c, bkk=bkk,
                            wbuf=wbuf, off3=off3: e.matmul(
                    pb[:, bkk * 512:bkk * 512 + 512],
                    WBUF[:, wbuf * 1024 + c * 128:wbuf * 1024 + c * 128 + 128],
                    XN3[:, c * T3 + off3 + nt * 512:
                        c * T3 + off3 + (nt + 1) * 512],
                    start=(c == 0), stop=(c == NCH - 1)),
                    "mm" if c == NCH - 1 else None)
            p.wait("vector", "mm", mmc)
            ev = p.ins("vector", lambda e, nt=nt, bkk=bkk, dst=dst,
                       bias_col=bias_col: e.tensor_scalar_add(
                dst[:, nt * 512:(nt + 1) * 512],
                pb[:, bkk * 512:bkk * 512 + 512], bias_col), "ev")
            p.banks_done([bkk], "ev", ev)
        wq_done()

    def conv(dst, srcp, wcol, bcol, dst_w, src_off):
        p.ins("vector", lambda e: e.tensor_scalar_mul(
            dst[:, 0:dst_w], srcp[:, src_off - 3:src_off - 3 + dst_w],
            wcol(0)))
        for j in range(1, 4):
            p.ins("vector", lambda e, j=j: e.scalar_tensor_tensor(
                dst[:, 0:dst_w], srcp[:, src_off - 3 + j:src_off - 3 + j + dst_w],
                wcol(j), dst[:, 0:dst_w], ALU.mult, ALU.add))
        ev = p.ins("vector", lambda e: e.tensor_scalar_add(
            dst[:, 0:dst_w], dst[:, 0:dst_w], bcol), "ev")
        return ev

    def rotary(buf, w, toff):
        for (d0, s0, sgn) in ((0, 32, -1.0), (32, 0, 1.0), (64, 96, -1.0),
                              (96, 64, 1.0)):
            p.ins("vector", lambda e, d0=d0, s0=s0, sgn=sgn:
                  e.tensor_scalar_mul(
                XR[d0:d0 + 32, 0:w], buf[s0:s0 + 32, 0:w], sgn))
        p.ins("vector", lambda e: e.tensor_mul(
            XR[:, 0:w], XR[:, 0:w], TS[:, toff:toff + w]))
        p.ins("vector", lambda e: e.tensor_mul(
            buf[:, 0:w], buf[:, 0:w], TC[:, toff:toff + w]))
        ev = p.ins("vector", lambda e: e.tensor_add(
            buf[:, 0:w], buf[:, 0:w], XR[:, 0:w]), "ev")
        return ev

    def l2norm_to(dst_bf, buf, w):
        p.wait("scalar", "ev", p.cnt("ev"))
        ac = p.ins("scalar", lambda e: e.activation(
            SQ[:, 0:w], buf[:, 0:w], AF.Square), "ac")
        p.wait("tensor", "ac", ac)
        for nt in range((w + 511) // 512):
            n = min(512, w - nt * 512)
            col = 2048 + (nt % 2) * 512
            p.banks_req([4 + nt % 2])
            mmc = p.ins("tensor", lambda e, nt=nt, n=n, col=col: e.matmul(
                pb[0:2, col:col + n],
                smalls["hsel2"][:], SQ[:, nt * 512:nt * 512 + n],
                start=True, stop=True), "mm")
            p.wait("vector", "mm", mmc)
            ev = p.ins("vector", lambda e, nt=nt, n=n, col=col: e.tensor_copy(
                R2[0:2, nt * 512:nt * 512 + n], pb[0:2, col:col + n]), "ev")
            p.banks_done([4 + nt % 2], "ev", ev)
        ev = p.ins("vector", lambda e: e.tensor_scalar_add(
            R2[0:2, 0:w], R2[0:2, 0:w], 1e-12), "ev")
        p.wait("scalar", "ev", ev)
        ac = p.ins("scalar", lambda e: e.activation(
            R2[0:2, 0:w], R2[0:2, 0:w], AF.Sqrt), "ac")
        p.wait("vector", "ac", ac)
        ev = p.ins("vector", lambda e: e.reciprocal(R2[0:2, 0:w],
                                                    R2[0:2, 0:w]), "ev")
        p.wait("tensor", "ev", ev)
        for nt in range((w + 511) // 512):
            n = min(512, w - nt * 512)
            col = 3072 + (nt % 2) * 512
            p.banks_req([6 + nt % 2])
            mmc = p.ins("tensor", lambda e, nt=nt, n=n, col=col: e.matmul(
                pb[:, col:col + n],
                smalls["hsel2t"][:], R2[0:2, nt * 512:nt * 512 + n],
                start=True, stop=True), "mm")
            p.wait("vector", "mm", mmc)
            ev = p.ins("vector", lambda e, nt=nt, n=n, col=col,
                       dst_bf=dst_bf: e.tensor_mul(
                dst_bf[:, nt * 512:nt * 512 + n],
                buf[:, nt * 512:nt * 512 + n], pb[:, col:col + n]), "ev")
            p.banks_done([6 + nt % 2], "ev", ev)
        return ev

    ev = p.ins("vector", lambda e: e.memset(NEGC[:], -0.125), "ev")
    p.wait("scalar", "ev", p.cnt("ev"))

    for hc in range(NCH):
        p.wait("vector", "ac", p.cnt("ac"))
        def zpad():
            p.ins("vector", lambda e: e.tensor_scalar_mul(
                PROJ[:, 1021:1024], PROJ[:, 1021:1024],
                smalls["hflag"][:, 0:1]))

        proj(PROJ, wq, hc, T3, 0, smalls["bq"][:, hc:hc + 1])
        zpad()
        conv(QC, PROJ, lambda j: smalls["cqw"][:, hc * 4 + j:hc * 4 + j + 1],
             smalls["cqb"][:, hc:hc + 1], T, 1024)
        proj(PROJ, wk, hc, T3, 0, smalls["bk"][:, hc:hc + 1])
        zpad()
        conv(KC, PROJ, lambda j: smalls["ckw"][:, hc * 4 + j:hc * 4 + j + 1],
             smalls["ckb"][:, hc:hc + 1], KR, 512)
        proj(PROJ, wv, hc, T3, 0, smalls["bv"][:, hc:hc + 1])
        zpad()
        ev = conv(VC, PROJ,
                  lambda j: smalls["cvw"][:, hc * 4 + j:hc * 4 + j + 1],
                  smalls["cvb"][:, hc:hc + 1], KR, 512)
        rotary(QC, T, 1024)
        rotary(KC, KR, 512)
        ev = l2norm_to(QH, QC, T)
        ev = l2norm_to(KH2, KC, KR)
        p.wait("tensor", "ev", p.cnt("ev"))
        for tt in range(20):
            bkk = 4 + tt % 2
            p.banks_req([bkk])
            mmc = p.ins("tensor", lambda e, tt=tt, bkk=bkk: e.transpose(
                pb[:, bkk * 512:bkk * 512 + 128],
                VC[:, tt * 128:(tt + 1) * 128], aux["eye_f"][:]), "mm")
            p.wait("vector", "mm", mmc)
            p.ins("vector", lambda e, tt=tt, bkk=bkk: e.tensor_copy(
                VT[:, tt * 65:tt * 65 + 64],
                pb[0:128, bkk * 512:bkk * 512 + 64]))
            ev = p.ins("vector", lambda e, tt=tt, bkk=bkk: e.tensor_copy(
                VT[:, (20 + tt) * 65:(20 + tt) * 65 + 64],
                pb[0:128, bkk * 512 + 64:bkk * 512 + 128]), "ev")
            p.banks_done([bkk], "ev", ev)
        p.wait("tensor", "ev", p.cnt("ev"))
        for hh in range(2):
            base = hh * 64
            for qt in range(16):
                pv0 = p.cnt("pv")
                for c in range(5):
                    s = qt + c
                    slot = c % 2
                    bkk = 4 + slot
                    p.banks_req([bkk])
                    mmc = p.ins("tensor", lambda e, s=s, qt=qt, base=base,
                                bkk=bkk: e.matmul(
                        pb[:, bkk * 512:bkk * 512 + 128],
                        KH2[base:base + 64, s * 128:(s + 1) * 128],
                        QH[base:base + 64, qt * 128:(qt + 1) * 128],
                        start=True, stop=True), "mm")
                    p.wait("scalar", "mm", mmc)
                    if c >= 2:
                        p.wait("scalar", "pv", pv0 + c - 1)
                    ac = p.ins("scalar", lambda e, s=s, qt=qt, bkk=bkk,
                               slot=slot: e.activation(
                        PT[:, slot * 128:slot * 128 + 128],
                        pb[:, bkk * 512:bkk * 512 + 128], AF.Exp,
                        scale=0.125,
                        bias=smalls["biastab"][:, min(s, 4) if False else
                                               (s if s < 4 else 4):
                                               (s if s < 4 else 4) + 1]), "ac")
                    p.banks_done([bkk], "ac", ac)
                    sv = None
                    if c == 0 or c == 4:
                        p.wait("vector", "ac", ac)
                        sv = p.ins("vector", lambda e, slot=slot,
                                   dj=(0 if c == 0 else 1): e.tensor_mul(
                            PT[:, slot * 128:slot * 128 + 128],
                            PT[:, slot * 128:slot * 128 + 128],
                            STRD[:, dj * 128:(dj + 1) * 128]), "sv")
                        p.wait("tensor", "sv", sv)
                    p.wait("tensor", "ac", ac)
                    if c == 0:
                        p.banks_req([6])
                    p.ins("tensor", lambda e, s=s, hh=hh, slot=slot, c=c:
                          e.matmul(
                        pb[0:65, 3072:3072 + 128],
                        VT[:, (hh * 20 + s) * 65:(hh * 20 + s) * 65 + 65],
                        PT[:, slot * 128:slot * 128 + 128],
                        start=(c == 0), stop=(c == 4)), "pv")
                p.wait("vector", "pv", p.cnt("pv"))
                p.ins("vector", lambda e: e.tensor_copy(
                    R2[0:1, 0:128], pb[64:65, 3072:3072 + 128]))
                p.ins("vector", lambda e: e.reciprocal(
                    R2[0:1, 0:128], R2[0:1, 0:128]))
                ev = p.ins("vector", lambda e: e.tensor_copy(
                    SQ[64:128, 0:128], pb[0:64, 3072:3072 + 128]), "ev")
                p.banks_done([6], "ev", ev)
                p.wait("tensor", "ev", ev)
                p.banks_req([7])
                mmc = p.ins("tensor", lambda e: e.matmul(
                    pb[0:64, 3584:3584 + 128], aux["onesr_f"][0:1, 0:64],
                    R2[0:1, 0:128], start=True, stop=True), "mm")
                p.wait("vector", "mm", mmc)
                ev = p.ins("vector", lambda e, hc=hc, base=base, qt=qt:
                           e.tensor_mul(
                    AO[base:base + 64, hc * T + qt * 128:
                       hc * T + (qt + 1) * 128],
                    SQ[64:128, 0:128], pb[0:64, 3584:3584 + 128]), "ev")
                p.banks_done([7], "ev", ev)

    stbuf = {0: 0, 1: 0}
    XB2 = af.take(2 * 512)
    YB2 = af.take(2 * 512)
    p.wait("tensor", "ev", p.cnt("ev"))
    k_it = 0
    for m in range(NCH):
        wbuf = gq[0] % 2
        wq_guard()
        wc = p.dma("gpsimd", WBUF[:, wbuf * 1024:(wbuf + 1) * 1024], wo[m],
                   f"wq{wbuf}")
        p.wait("tensor", f"wq{wbuf}", wc)
        for nt in range(4):
            xb = k_it % 2
            p.wait("sync", f"st{xb}", stbuf[xb])
            lc = p.dma("sync", XB2[:, xb * 512:(xb + 1) * 512],
                       xown[:, m * T + nt * 512:m * T + (nt + 1) * 512],
                       f"xld{xb}")
            bkk = (m * 4 + nt) % 4
            p.banks_req([bkk])
            for c in range(NCH):
                mmc = p.ins("tensor", lambda e, m=m, nt=nt, c=c, bkk=bkk,
                            wbuf=wbuf: e.matmul(
                    pb[:, bkk * 512:bkk * 512 + 512],
                    WBUF[:, wbuf * 1024 + c * 128:wbuf * 1024 + c * 128 + 128],
                    AO[:, c * T + nt * 512:c * T + (nt + 1) * 512],
                    start=(c == 0), stop=(c == NCH - 1)),
                    "mm" if c == NCH - 1 else None)
            p.wait("vector", "mm", mmc)
            p.wait("vector", f"xld{xb}", lc)
            p.wait("vector", f"st{xb}", stbuf[xb])
            p.ins("vector", lambda e, m=m, bkk=bkk: e.tensor_scalar_add(
                pb[:, bkk * 512:bkk * 512 + 512],
                pb[:, bkk * 512:bkk * 512 + 512], smalls["bo"][:, m:m + 1]))
            ev = p.ins("vector", lambda e, bkk=bkk, xb=xb: e.tensor_add(
                YB2[:, xb * 512:(xb + 1) * 512],
                pb[:, bkk * 512:bkk * 512 + 512],
                XB2[:, xb * 512:(xb + 1) * 512]), "ev")
            p.banks_done([bkk], "ev", ev)
            p.wait("sync", "ev", ev)
            stbuf[xb] = p.dma(
                "sync", yout[:, m * T + nt * 512:m * T + (nt + 1) * 512],
                YB2[:, xb * 512:(xb + 1) * 512], f"st{xb}")
            k_it += 1
        wq_done()
    p.wait("sync", "st0", p.cnt("st0"))
    p.wait("sync", "st1", p.cnt("st1"))
    return p




def l1_attn_a_inputs(inputs):
    g = lambda n: np.asarray(inputs["L1_attn_a_" + n])
    cw = lambda w: np.ascontiguousarray(
        w.reshape(NCH, 128, 4).transpose(1, 0, 2).reshape(128, NCH * 4))
    d = {
        "wq": pack_wio(np.ascontiguousarray(g("Wq").T)),
        "wk": pack_wio(np.ascontiguousarray(g("Wk").T)),
        "wv": pack_wio(np.ascontiguousarray(g("Wv").T)),
        "wo": pack_wio(np.ascontiguousarray(g("Wo").T)),
        "bq": col_pack(g("bq")), "bk": col_pack(g("bk")),
        "bv": col_pack(g("bv")), "bo": col_pack(g("bo")),
        "cqw": cw(g("cqw")), "ckw": cw(g("ckw")), "cvw": cw(g("cvw")),
        "cqb": col_pack(g("cqb")), "ckb": col_pack(g("ckb")),
        "cvb": col_pack(g("cvb")),
    }
    hsel2 = np.zeros((128, 2), np.float32)
    hsel2[0:64, 0] = 1; hsel2[64:128, 1] = 1
    hsel2t = np.zeros((2, 128), np.float32)
    hsel2t[0, 0:64] = 1; hsel2t[1, 64:128] = 1
    d["hsel2"] = hsel2; d["hsel2t"] = hsel2t
    p_ = np.arange(128)[:, None]
    f_ = np.arange(128)[None, :]
    stripd = np.zeros((128, 256), BF)
    stripd[:, 0:128] = (f_ < p_)
    stripd[:, 128:256] = (f_ >= p_)
    d["stripd"] = stripd
    d.update(host_aux())
    invf = 1.0 / 10000.0 ** (np.arange(0, 64, 2, dtype=np.float64) / 64)
    percore = []
    for h in range(2):
        bpos = np.arange(1024, 4096)
        gpos = np.where(bpos < 2048, (1 - h) * T + bpos,
                        h * T + (bpos - 2048)).astype(np.float64)
        fr = gpos[:, None] * invf[None, :]
        emb = np.concatenate([fr, fr], axis=1)
        cos = np.cos(emb).astype(np.float32)
        sin = np.sin(emb).astype(np.float32)
        tc = np.tile(cos.T, (2, 1))
        tsn = np.tile(sin.T, (2, 1))
        bt = np.full((128, 8), -0.125, np.float32)
        if h == 0:
            bt[:, 0:4] = -0.125 - 1.0e30
        percore.append({"tcos": np.ascontiguousarray(tc),
                        "tsin": np.ascontiguousarray(tsn),
                        "biastab": bt,
                        "hflag": np.full((128, 1), float(h), np.float32)})
    return d, percore


def run_l1_attn_a(x, inputs):
    nc = get_prog("l1_attn_a", build_l1_attn_a)
    d, percore = l1_attn_a_inputs(inputs)
    in_maps = []
    for c in range(NCORES):
        b, h = c // 2, c % 2
        m = dict(d)
        m.update(percore[h])
        xrot = np.concatenate([x[b, (1 - h) * T:(2 - h) * T],
                               x[b, h * T:(h + 1) * T]], axis=0)
        m["xbuf"] = fm_pack(xrot[1024:4096])
        m["xown"] = fm_pack(x[b, h * T:(h + 1) * T])
        in_maps.append(m)
    res = run_spmd_cached(nc, in_maps, core_ids=list(range(NCORES)))
    out = np.empty_like(x)
    for c in range(NCORES):
        b, h = c // 2, c % 2
        out[b, h * T:(h + 1) * T] = fm_unpack(res.results[c]["yfm"], T)
    return out




L0F_KT = 2560
L0F_QT0 = 384
L0F_NQT = 17
L0F_AOW = L0F_NQT * 128


def build_l0f_pre():
    p = Prog("l0f_pre")
    xi = p.inp("xi", [128, NCH * 2048])
    yo = p.outp("yo", [128, NCH * 2048])
    aux = emit_aux(p)
    pb = p.psum("pb", [128, 4096], F32)
    af = Arena(p, "AF", 22100, F32)
    XI = af.take(NCH * 2048)
    lc = p.dma("sync", XI[:, 0:NCH * 2048], xi, "ld")
    p.wait("scalar", "ld", lc)
    p.wait("tensor", "ld", lc)
    p.wait("vector", "ld", lc)
    for half in range(2):
        t0 = half * 1024
        ev = emit_ln(p, aux,
                     lambda c, a, b, t0=t0: XI[:, c * 2048 + t0 + a:
                                               c * 2048 + t0 + b],
                     lambda c, t0=t0: XI[:, c * 2048 + t0:c * 2048 + t0 + 1024],
                     1024, pb, af, "ln")
    p.wait("sync", "ev", p.cnt("ev"))
    p.dma("sync", yo, XI[:, 0:NCH * 2048], "st")
    p.wait("sync", "st", p.cnt("st"))
    return p


def build_l0f_attn():
    p = Prog("l0f_attn")
    KT = L0F_KT
    xo = p.inp("xo", [128, NCH * 2050])
    xh = p.inp("xh", [128, NCH * 512])
    wqkv = p.inp("wqkv", [NCH, 3, 128, NCH * 128])
    wo = p.inp("wo", [NCH, 128, NCH * 128])
    qs = p.inp("qs", [128, 1])
    ks = p.inp("ks", [128, 1])
    epsc = p.inp("epsc", [128, 1])
    tcq = p.inp("tcq", [128, KT])
    tsq = p.inp("tsq", [128, KT])
    tck = p.inp("tck", [128, KT])
    tsk = p.inp("tsk", [128, KT])
    perm = p.inp("perm", [128, 128])
    hs2 = p.inp("hs2", [128, 2])
    hs2t = p.inp("hs2t", [2, 128])
    maskA = p.inp("maskA", [128, 128])
    maskB = p.inp("maskB", [128, 128])
    bval = p.inp("bval", [128, 8])
    yout = p.outp("yfm", [128, NCH * 2049])

    aux = emit_aux(p)
    pb = p.psum("pb", [128, 4096], F32)
    ah = Arena(p, "AH", 51400, BF16)
    af = Arena(p, "AF", 16500, F32)

    XT = ah.take(NCH * KT)
    QKp = ah.take(2 * KT)
    K65 = ah.take(KT)
    Q65 = ah.take(2 * 128)
    PTL = ah.take(2 * 128)
    VT = ah.take(2 * 20 * 65)
    WB = ah.take(2 * 1024)
    AO = ah.take(NCH * L0F_AOW)

    TCQ = af.take(KT)
    TSQ = af.take(KT)
    TCK = af.take(KT)
    TSK = af.take(KT)
    PRS = af.take(2 * 512)
    SQ = af.take(2 * 512)
    XR = af.take(2 * 512)
    VSL = af.take(512)
    SC = af.take(2 * 384)
    NB = af.take(128)
    MM = af.take(272)
    XRS = af.take(2 * 512)
    MKA = af.take(128)
    MKB = af.take(128)
    PERMS = p.sbuf("perms", [128, 128], F32)
    qscol = p.sbuf("qscol", [128, 1], F32)
    kscol = p.sbuf("kscol", [128, 1], F32)
    epss = p.sbuf("epss", [128, 1], F32)
    bvals = p.sbuf("bvals", [128, 8], F32)
    hs2s = p.sbuf("hs2s", [128, 2], F32)
    hs2ts = p.sbuf("hs2ts", [2, 128], F32)

    for dst, srcap in ((TCQ, tcq), (TSQ, tsq), (TCK, tck), (TSK, tsk)):
        p.dma("sync", dst[:, 0:KT], srcap, "ld")
    p.dma("sync", MKA[:, 0:128], maskA, "ld")
    p.dma("sync", MKB[:, 0:128], maskB, "ld")
    p.dma("sync", PERMS[:], perm, "ld")
    p.dma("sync", qscol[:], qs, "ld")
    p.dma("sync", kscol[:], ks, "ld")
    p.dma("sync", epss[:], epsc, "ld")
    p.dma("sync", bvals[:], bval, "ld")
    p.dma("sync", hs2s[:], hs2, "ld")
    lc = p.dma("sync", hs2ts[:], hs2t, "ld")
    for c in range(NCH):
        s = 0 if c < 4 else 1
        p.dma("gpsimd", XT[:, c * KT:c * KT + 510 + s],
              xh[:, c * 512 + 2 - s:(c + 1) * 512], "xt")
        xc = p.dma("gpsimd", XT[:, c * KT + 510 + s:(c + 1) * KT],
                   xo[:, c * 2050:c * 2050 + 2050 - s], "xt")
    for eng in ("tensor", "vector", "scalar"):
        p.wait(eng, "ld", lc)
    p.wait("tensor", "xt", xc)
    for t in range(40):
        p.ins("vector", lambda e, t=t: e.memset(
            VT[:, t * 65 + 64:t * 65 + 65], 1.0))
    ev = p.ins("vector", lambda e: e.memset(K65[64:65, 0:KT], 1.0), "ev")

    gwq = {}
    gq = [0]
    evpar = {}

    def wq_guard():
        p.wait("gpsimd", "mm", gwq.get(gq[0] - 2, 0))

    def wq_done():
        gwq[gq[0]] = p.cnt("mm")
        gq[0] += 1

    for hc in range(NCH):
        for ti in (2, 0, 1):
            wbuf = gq[0] % 2
            wq_guard()
            wc = p.dma("gpsimd", WB[:, wbuf * 1024:(wbuf + 1) * 1024],
                       wqkv[hc, ti], f"wq{wbuf}")
            p.wait("tensor", f"wq{wbuf}", wc)
            for nt in range(5):
                pbk = nt % 2
                p.banks_req([pbk])
                for c in range(NCH):
                    mmc = p.ins("tensor", lambda e, nt=nt, c=c, pbk=pbk,
                                wbuf=wbuf: e.matmul(
                        pb[:, pbk * 512:pbk * 512 + 512],
                        WB[:, wbuf * 1024 + c * 128:
                           wbuf * 1024 + c * 128 + 128],
                        XT[:, c * KT + nt * 512:c * KT + (nt + 1) * 512],
                        start=(c == 0), stop=(c == NCH - 1)),
                        "mm" if c == NCH - 1 else None)
                p.wait("scalar", "mm", mmc)
                if ti == 2:
                    ac = p.ins("scalar", lambda e, pbk=pbk: e.activation(
                        VSL[:, 0:512], pb[:, pbk * 512:pbk * 512 + 512],
                        AF.Copy), "ac")
                    p.banks_done([pbk], "ac", ac)
                    p.wait("tensor", "ac", ac)
                    p.banks_req([2])
                    for t4 in range(4):
                        mmc = p.ins("tensor", lambda e, t4=t4: e.transpose(
                            pb[:, 1024 + t4 * 128:1024 + (t4 + 1) * 128],
                            VSL[:, t4 * 128:(t4 + 1) * 128],
                            aux["eye_f"][:]), "mm")
                    p.wait("vector", "mm", mmc)
                    for t4 in range(4):
                        tt = nt * 4 + t4
                        p.ins("vector", lambda e, t4=t4, tt=tt: e.tensor_copy(
                            VT[:, tt * 65:tt * 65 + 64],
                            pb[:, 1024 + t4 * 128:1024 + t4 * 128 + 64]))
                        ev = p.ins("vector", lambda e, t4=t4, tt=tt:
                                   e.tensor_copy(
                            VT[:, (20 + tt) * 65:(20 + tt) * 65 + 64],
                            pb[:, 1024 + t4 * 128 + 64:
                               1024 + (t4 + 1) * 128]), "ev")
                    p.banks_done([2], "ev", ev)
                else:
                    dsto = (0 if ti == 0 else KT) + nt * 512
                    scol = qscol if ti == 0 else kscol
                    TC = TCQ if ti == 0 else TCK
                    TS = TSQ if ti == 0 else TSK
                    b3 = 2 + 3 * (nt % 2)
                    so = (nt % 2) * 512
                    p.wait("scalar", "ev", evpar.get(nt % 2, 0))
                    ac = p.ins("scalar", lambda e, pbk=pbk, so=so:
                               e.activation(
                        SQ[:, so:so + 512], pb[:, pbk * 512:pbk * 512 + 512],
                        AF.Square), "ac")
                    ac = p.ins("scalar", lambda e, pbk=pbk, so=so:
                               e.activation(
                        PRS[:, so:so + 512], pb[:, pbk * 512:pbk * 512 + 512],
                        AF.Copy), "ac")
                    p.banks_done([pbk], "ac", ac)
                    p.wait("tensor", "ac", ac)
                    p.banks_req([b3])
                    mmc = p.ins("tensor", lambda e, b3=b3, so=so: e.matmul(
                        pb[0:2, b3 * 512:b3 * 512 + 512], hs2s[:],
                        SQ[:, so:so + 512], start=True, stop=True), "mm")
                    p.wait("scalar", "mm", mmc)
                    ac = p.ins("scalar", lambda e, b3=b3, so=so: e.activation(
                        SQ[0:2, so:so + 512], pb[0:2, b3 * 512:b3 * 512 + 512],
                        AF.Sqrt, bias=epss[0:2, 0:1]), "ac")
                    p.banks_done([b3], "ac", ac)
                    p.wait("vector", "ac", ac)
                    ev = p.ins("vector", lambda e, so=so: e.reciprocal(
                        SQ[0:2, so:so + 512], SQ[0:2, so:so + 512]), "ev")
                    p.wait("tensor", "ev", ev)
                    p.banks_req([b3 + 1])
                    mmc = p.ins("tensor", lambda e, b3=b3, so=so: e.matmul(
                        pb[:, (b3 + 1) * 512:(b3 + 1) * 512 + 512],
                        hs2ts[:], SQ[0:2, so:so + 512], start=True,
                        stop=True), "mm")
                    p.wait("vector", "mm", mmc)
                    p.ins("vector", lambda e, b3=b3, scol=scol, so=so:
                          e.scalar_tensor_tensor(
                        SQ[:, so:so + 512], PRS[:, so:so + 512], scol[:, 0:1],
                        pb[:, (b3 + 1) * 512:(b3 + 1) * 512 + 512],
                        ALU.mult, ALU.mult))
                    ev = p.ins("vector", lambda e, TS=TS, nt=nt, so=so:
                               e.tensor_mul(
                        XR[:, so:so + 512], SQ[:, so:so + 512],
                        TS[:, nt * 512:(nt + 1) * 512]), "ev")
                    p.banks_done([b3 + 1], "ev", ev)
                    p.wait("tensor", "ev", ev)
                    p.banks_req([b3 + 2])
                    mmc = p.ins("tensor", lambda e, b3=b3, so=so: e.matmul(
                        pb[:, (b3 + 2) * 512:(b3 + 2) * 512 + 512],
                        PERMS[:], XR[:, so:so + 512], start=True,
                        stop=True), "mm")
                    p.ins("vector", lambda e, TC=TC, nt=nt, so=so:
                          e.tensor_mul(
                        PRS[:, so:so + 512], SQ[:, so:so + 512],
                        TC[:, nt * 512:(nt + 1) * 512]))
                    p.wait("vector", "mm", mmc)
                    ev = p.ins("vector", lambda e, b3=b3, dsto=dsto, so=so:
                               e.tensor_add(
                        QKp[:, dsto:dsto + 512], PRS[:, so:so + 512],
                        pb[:, (b3 + 2) * 512:(b3 + 2) * 512 + 512]), "ev")
                    p.banks_done([b3 + 2], "ev", ev)
                    evpar[nt % 2] = ev
            wq_done()
        for hh in range(2):
            base = hh * 64
            p.wait("vector", "mm", p.cnt("mm"))
            ev = p.ins("vector", lambda e, base=base: e.tensor_copy(
                K65[0:64, 0:KT], QKp[base:base + 64, KT:2 * KT]), "ev")
            p.wait("tensor", "ev", ev)
            for qt in range(L0F_NQT):
                ks_ = 256 * ((qt + 1) // 2)
                ac0 = 1 if qt % 2 == 0 else 0
                sbk = qt % 2
                qb = qt % 2
                p.banks_req([sbk])
                mmc = p.ins("tensor", lambda e, qt=qt, ks_=ks_, ac0=ac0,
                            base=base, sbk=sbk: e.matmul(
                    pb[:, sbk * 512:sbk * 512 + 384],
                    QKp[base:base + 64,
                        L0F_QT0 + qt * 128:L0F_QT0 + (qt + 1) * 128],
                    QKp[base:base + 64,
                        KT + ks_ + ac0 * 128:KT + ks_ + ac0 * 128 + 384],
                    start=True, stop=True), "mm")
                p.wait("vector", "mm", mmc)
                p.ins("vector", lambda e, sbk=sbk: e.tensor_add(
                    SC[:, sbk * 384:sbk * 384 + 128],
                    pb[:, sbk * 512:sbk * 512 + 128], MKA[:, 0:128]))
                p.ins("vector", lambda e, sbk=sbk: e.tensor_copy(
                    SC[:, sbk * 384 + 128:sbk * 384 + 256],
                    pb[:, sbk * 512 + 128:sbk * 512 + 256]))
                ev = p.ins("vector", lambda e, sbk=sbk: e.tensor_add(
                    SC[:, sbk * 384 + 256:sbk * 384 + 384],
                    pb[:, sbk * 512 + 256:sbk * 512 + 384], MKB[:, 0:128]),
                    "ev")
                p.banks_done([sbk], "ev", ev)
                p.ins("vector", lambda e, sbk=sbk: e.tensor_reduce(
                    MM[:, sbk:sbk + 1], SC[:, sbk * 384:sbk * 384 + 384],
                    AX.X, ALU.max))
                ev = p.ins("vector", lambda e, sbk=sbk: e.tensor_scalar_mul(
                    MM[:, sbk:sbk + 1], MM[:, sbk:sbk + 1], -1.0), "ev")
                p.wait("tensor", "ev", ev)
                b23 = 2 + (qt % 2)
                p.banks_req([b23])
                mmc = p.ins("tensor", lambda e, sbk=sbk, b23=b23:
                            e.transpose(
                    pb[0:1, b23 * 512:b23 * 512 + 128], MM[:, sbk:sbk + 1],
                    aux["eye_f"][:]), "mm")
                p.wait("vector", "mm", mmc)
                p.ins("vector", lambda e, qb=qb, b23=b23: e.tensor_copy(
                    Q65[64:65, qb * 128:qb * 128 + 128],
                    pb[0:1, b23 * 512:b23 * 512 + 128]))
                ev = p.ins("vector", lambda e, qb=qb, base=base, qt=qt:
                           e.tensor_copy(
                    Q65[0:64, qb * 128:qb * 128 + 128],
                    QKp[base:base + 64,
                        L0F_QT0 + qt * 128:L0F_QT0 + (qt + 1) * 128]), "ev")
                p.banks_done([b23], "ev", ev)
                p.wait("tensor", "ev", ev)
                pv0 = p.cnt("pv")
                for c3 in range(3):
                    bkk = 4 + c3 % 2
                    sidx = min(ks_ // 128 + ac0 + c3, 7)
                    p.banks_req([bkk])
                    mmc = p.ins("tensor", lambda e, c3=c3, ks_=ks_, ac0=ac0,
                                bkk=bkk, qb=qb: e.matmul(
                        pb[:, bkk * 512:bkk * 512 + 128],
                        K65[0:65, ks_ + (ac0 + c3) * 128:
                            ks_ + (ac0 + c3) * 128 + 128],
                        Q65[0:65, qb * 128:qb * 128 + 128],
                        start=True, stop=True), "mm")
                    p.wait("scalar", "mm", mmc)
                    if c3 != 1:
                        p.wait("vector", "mm", mmc)
                        msk = MKB if c3 == 0 else MKA
                        sv = p.ins("vector", lambda e, bkk=bkk, msk=msk:
                                   e.tensor_add(
                            pb[:, bkk * 512:bkk * 512 + 128],
                            pb[:, bkk * 512:bkk * 512 + 128],
                            msk[:, 0:128]), "sv")
                        p.wait("scalar", "sv", sv)
                    p.wait("scalar", "pv", pv0 + 1 if c3 == 2 else pv0)
                    ac = p.ins("scalar", lambda e, c3=c3, bkk=bkk,
                               sidx=sidx: e.activation(
                        PTL[:, (c3 % 2) * 128:(c3 % 2) * 128 + 128],
                        pb[:, bkk * 512:bkk * 512 + 128], AF.Exp,
                        scale=8.0, bias=bvals[:, sidx:sidx + 1]), "ac")
                    p.banks_done([bkk], "ac", ac)
                    p.wait("tensor", "ac", ac)
                    b67 = 6 + (qt % 2)
                    if c3 == 0:
                        p.banks_req([b67])
                    p.ins("tensor", lambda e, c3=c3, ks_=ks_, ac0=ac0,
                          hh=hh, b67=b67: e.matmul(
                        pb[0:65, b67 * 512:b67 * 512 + 128],
                        VT[:, (hh * 20 + ks_ // 128 + ac0 + c3) * 65:
                           (hh * 20 + ks_ // 128 + ac0 + c3) * 65 + 65],
                        PTL[:, (c3 % 2) * 128:(c3 % 2) * 128 + 128],
                        start=(c3 == 0), stop=(c3 == 2)), "pv")
                p.wait("vector", "pv", p.cnt("pv"))
                mmoff = 8 + (qt % 2) * 128
                p.ins("vector", lambda e, b67=b67, mmoff=mmoff:
                      e.tensor_copy(
                    MM[0:1, mmoff:mmoff + 128],
                    pb[64:65, b67 * 512:b67 * 512 + 128]))
                p.ins("vector", lambda e, mmoff=mmoff: e.tensor_scalar_add(
                    MM[0:1, mmoff:mmoff + 128],
                    MM[0:1, mmoff:mmoff + 128], 1e-30))
                p.ins("vector", lambda e, mmoff=mmoff: e.reciprocal(
                    MM[0:1, mmoff:mmoff + 128], MM[0:1, mmoff:mmoff + 128]))
                ev = p.ins("vector", lambda e, b67=b67: e.tensor_copy(
                    NB[64:128, 0:128],
                    pb[0:64, b67 * 512:b67 * 512 + 128]), "ev")
                p.banks_done([b67], "ev", ev)
                p.wait("tensor", "ev", ev)
                p.banks_req([b23])
                mmc = p.ins("tensor", lambda e, b23=b23, mmoff=mmoff:
                            e.matmul(
                    pb[0:64, b23 * 512 + 128:b23 * 512 + 256],
                    aux["onesr_f"][0:1, 0:64],
                    MM[0:1, mmoff:mmoff + 128], start=True, stop=True), "mm")
                p.wait("vector", "mm", mmc)
                ev = p.ins("vector", lambda e, hc=hc, base=base, qt=qt,
                           b23=b23: e.tensor_mul(
                    AO[base:base + 64,
                       hc * L0F_AOW + qt * 128:hc * L0F_AOW + qt * 128 + 128],
                    NB[64:128, 0:128],
                    pb[0:64, b23 * 512 + 128:b23 * 512 + 256]), "ev")
                p.banks_done([b23], "ev", ev)

    p.wait("tensor", "ev", p.cnt("ev"))
    stbuf = {0: 0, 1: 0}
    k_it = 0
    for m in range(NCH):
        wbuf = gq[0] % 2
        wq_guard()
        wc = p.dma("gpsimd", WB[:, wbuf * 1024:(wbuf + 1) * 1024], wo[m],
                   f"wq{wbuf}")
        p.wait("tensor", f"wq{wbuf}", wc)
        for nt in range(5):
            a0 = nt * 512 if nt < 4 else 2048
            nn = 512 if nt < 4 else 128
            lo = 127 if nt == 0 else 0
            xb = k_it % 2
            p.wait("sync", f"st{xb}", stbuf[xb])
            lc = p.dma("sync", XRS[:, xb * 512:xb * 512 + nn - lo],
                       xo[:, m * 2050 + a0 + lo - 126:
                          m * 2050 + a0 + nn - 126], f"xld{xb}")
            bkk = k_it % 4
            p.banks_req([bkk])
            for c in range(NCH):
                mmc = p.ins("tensor", lambda e, m=m, c=c, bkk=bkk, wbuf=wbuf,
                            a0=a0, nn=nn: e.matmul(
                    pb[:, bkk * 512:bkk * 512 + nn],
                    WB[:, wbuf * 1024 + c * 128:wbuf * 1024 + c * 128 + 128],
                    AO[:, c * L0F_AOW + a0:c * L0F_AOW + a0 + nn],
                    start=(c == 0), stop=(c == NCH - 1)),
                    "mm" if c == NCH - 1 else None)
            p.wait("vector", "mm", mmc)
            p.wait("vector", f"xld{xb}", lc)
            ev = p.ins("vector", lambda e, bkk=bkk, xb=xb, lo=lo, nn=nn:
                       e.tensor_add(
                XRS[:, xb * 512:xb * 512 + nn - lo],
                pb[:, bkk * 512 + lo:bkk * 512 + nn],
                XRS[:, xb * 512:xb * 512 + nn - lo]), "ev")
            p.banks_done([bkk], "ev", ev)
            p.wait("sync", "ev", ev)
            stbuf[xb] = p.dma(
                "sync",
                yout[:, m * 2049 + a0 + lo - 127:m * 2049 + a0 + nn - 127],
                XRS[:, xb * 512:xb * 512 + nn - lo], f"st{xb}")
            k_it += 1
        wq_done()
    p.wait("sync", "st0", p.cnt("st0"))
    p.wait("sync", "st1", p.cnt("st1"))
    return p


def build_l0f_ffn(addres=False):
    p = Prog("l0f_ffn" + ("_r" if addres else ""))
    NMH = 22
    xi = p.inp("xi", [128, NCH * 2049])
    w1a = p.inp("w1a", [NMH, 128, NCH * 128])
    w1g = p.inp("w1g", [NMH, 128, NCH * 128])
    w2 = p.inp("w2", [NCH, 128, NMH * 128])
    b1a = p.inp("b1a", [128, NMH])
    b1g = p.inp("b1g", [128, NMH])
    b2 = p.inp("b2", [128, NCH])
    lng = p.inp("lng", [128, NCH])
    lnb = p.inp("lnb", [128, NCH])
    if addres:
        xres = p.inp("xres", [128, NCH * 2048])
    yo = p.outp("yo", [128, NCH * 2048])

    aux = emit_aux(p)
    pb = p.psum("pb", [128, 4096], F32)
    ah = Arena(p, "AH", 54300, BF16)
    af = Arena(p, "AF", 21600, F32)

    XN = ah.take(NCH * 2048)
    H = ah.take(NMH * 512)
    W2R = ah.take(NCH * NMH * 128)
    WA = ah.take(2 * 1024)
    WG = ah.take(2 * 1024)
    XI = af.take(NCH * 2049)
    GELF = af.take(512)
    YS = af.take(2 * 512)
    XR2 = af.take(2 * 512)
    bcols = p.sbuf("bcols", [128, 3 * NMH + 3 * NCH], F32)
    B1A, B1G = 0, NMH
    B2C, LNG, LNB = 3 * NMH, 3 * NMH + NCH, 3 * NMH + 2 * NCH

    lc = p.dma("sync", XI[:, 0:NCH * 2049], xi, "ld")
    p.dma("sync", bcols[:, B1A:B1A + NMH], b1a, "ld")
    p.dma("sync", bcols[:, B1G:B1G + NMH], b1g, "ld")
    p.dma("sync", bcols[:, B2C:B2C + NCH], b2, "ld")
    p.dma("sync", bcols[:, LNG:LNG + NCH], lng, "ld")
    lc = p.dma("sync", bcols[:, LNB:LNB + NCH], lnb, "ld")
    for m in range(NCH):
        wl = p.dma("gpsimd", W2R[:, m * NMH * 128:(m + 1) * NMH * 128],
                   w2[m], "wld")
    for eng in ("tensor", "vector", "scalar"):
        p.wait(eng, "ld", lc)

    for tq in range(4):
        t0 = tq * 512
        ev = emit_ln(
            p, aux,
            lambda c, a, b, t0=t0: XI[:, c * 2049 + t0 + a +
                                      (1 if c < 4 else 0):
                                      c * 2049 + t0 + b +
                                      (1 if c < 4 else 0)],
            lambda c, t0=t0: XN[:, c * 2048 + t0:c * 2048 + t0 + 512],
            512, pb, af, "ln",
            g_col=lambda c: bcols[:, LNG + c:LNG + c + 1],
            b_col=lambda c: bcols[:, LNB + c:LNB + c + 1])

    gwq = {}
    gq = [0]

    def wq_guard():
        p.wait("gpsimd", "mm", gwq.get(gq[0] - 2, 0))

    def wq_done():
        gwq[gq[0]] = p.cnt("mm")
        gq[0] += 1

    p.wait("tensor", "wld", p.cnt("wld"))
    p.wait("tensor", "ev", p.cnt("ev"))
    stbuf = {0: 0, 1: 0}
    k_it = 0
    for tg in range(4):
        t0 = tg * 512
        for mh in range(NMH):
            wbuf = gq[0] % 2
            wq_guard()
            p.dma("gpsimd", WA[:, wbuf * 1024:(wbuf + 1) * 1024],
                  w1a[mh], f"wq{wbuf}")
            wc = p.dma("gpsimd", WG[:, wbuf * 1024:(wbuf + 1) * 1024],
                       w1g[mh], f"wq{wbuf}")
            p.wait("tensor", f"wq{wbuf}", wc)
            pa = (mh % 2) * 2
            pg = pa + 1
            p.banks_req([pa, pg])
            for c in range(NCH):
                p.ins("tensor", lambda e, c=c, pa=pa, wbuf=wbuf, t0=t0:
                      e.matmul(
                    pb[:, pa * 512:pa * 512 + 512],
                    WA[:, wbuf * 1024 + c * 128:wbuf * 1024 + c * 128 + 128],
                    XN[:, c * 2048 + t0:c * 2048 + t0 + 512],
                    start=(c == 0), stop=(c == NCH - 1)))
            for c in range(NCH):
                mmc = p.ins("tensor", lambda e, c=c, pg=pg, wbuf=wbuf,
                            t0=t0: e.matmul(
                    pb[:, pg * 512:pg * 512 + 512],
                    WG[:, wbuf * 1024 + c * 128:wbuf * 1024 + c * 128 + 128],
                    XN[:, c * 2048 + t0:c * 2048 + t0 + 512],
                    start=(c == 0), stop=(c == NCH - 1)),
                    "mm" if c == NCH - 1 else None)
            wq_done()
            p.wait("scalar", "mm", mmc)
            p.wait("scalar", "ev", p.cnt("ev"))
            ac = p.ins("scalar", lambda e, pg=pg, mh=mh: e.activation(
                GELF[:, 0:512], pb[:, pg * 512:pg * 512 + 512], AF.Gelu,
                bias=bcols[:, B1G + mh:B1G + mh + 1]), "ac")
            p.wait("vector", "ac", ac)
            ev = p.ins("vector", lambda e, pa=pa, mh=mh:
                       e.scalar_tensor_tensor(
                H[:, mh * 512:(mh + 1) * 512],
                pb[:, pa * 512:pa * 512 + 512],
                bcols[:, B1A + mh:B1A + mh + 1], GELF[:, 0:512],
                ALU.add, ALU.mult), "ev")
            p.banks_done([pa, pg], "ev", ev)
        p.wait("tensor", "ev", p.cnt("ev"))
        for m in range(NCH):
            ys = k_it % 2
            p.wait("sync", f"st{ys}", stbuf[ys])
            if addres:
                lc = p.dma("sync", XR2[:, ys * 512:(ys + 1) * 512],
                           xres[:, m * 2048 + t0:m * 2048 + t0 + 512],
                           f"xld{ys}")
            bkk = 4 + k_it % 4
            p.banks_req([bkk])
            for kc in range(NMH):
                mmc = p.ins("tensor", lambda e, m=m, kc=kc, bkk=bkk:
                            e.matmul(
                    pb[:, bkk * 512:bkk * 512 + 512],
                    W2R[:, m * NMH * 128 + kc * 128:
                        m * NMH * 128 + (kc + 1) * 128],
                    H[:, kc * 512:(kc + 1) * 512],
                    start=(kc == 0), stop=(kc == NMH - 1)),
                    "mm" if kc == NMH - 1 else None)
            p.wait("vector", "mm", mmc)
            p.wait("vector", f"st{ys}", stbuf[ys])
            sev = None if addres else "ev"
            ev = p.ins("vector", lambda e, m=m, bkk=bkk, t0=t0, ys=ys:
                       e.scalar_tensor_tensor(
                YS[:, ys * 512:(ys + 1) * 512],
                pb[:, bkk * 512:bkk * 512 + 512],
                bcols[:, B2C + m:B2C + m + 1],
                XI[:, m * 2049 + 1 + t0:m * 2049 + 1 + t0 + 512],
                ALU.add, ALU.add), sev)
            if addres:
                p.wait("vector", f"xld{ys}", lc)
                ev = p.ins("vector", lambda e, ys=ys: e.tensor_add(
                    YS[:, ys * 512:(ys + 1) * 512],
                    YS[:, ys * 512:(ys + 1) * 512],
                    XR2[:, ys * 512:(ys + 1) * 512]), "ev")
            p.banks_done([bkk], "ev", ev)
            p.wait("sync", "ev", ev)
            stbuf[ys] = p.dma(
                "sync", yo[:, m * 2048 + t0:m * 2048 + t0 + 512],
                YS[:, ys * 512:(ys + 1) * 512], f"st{ys}")
            k_it += 1
    p.wait("sync", "st0", p.cnt("st0"))
    p.wait("sync", "st1", p.cnt("st1"))
    return p




def l0f_weight_inputs(inputs):
    Wqkv = np.asarray(inputs["L0_ffn_Wqkv"], np.float32)
    qsc = np.asarray(inputs["L0_ffn_q_scale"], np.float32)
    ksc = np.asarray(inputs["L0_ffn_k_scale"], np.float32)
    Wo = np.asarray(inputs["L0_ffn_Wo"], np.float32)
    lng = np.asarray(inputs["L0_ffn_ln_g"], np.float32)
    lnb = np.asarray(inputs["L0_ffn_ln_b"], np.float32)
    W1 = np.asarray(inputs["L0_ffn_W1"], np.float32)
    b1 = np.asarray(inputs["L0_ffn_b1"], np.float32)
    W2 = np.asarray(inputs["L0_ffn_W2"], np.float32)
    b2 = np.asarray(inputs["L0_ffn_b2"], np.float32)
    layers = []
    for i in range(6):
        d = {}
        W = Wqkv[i]
        wq = W.reshape(8, 128, 3, 8, 128).transpose(3, 2, 1, 0, 4)
        d["wqkv"] = np.ascontiguousarray(
            wq.reshape(8, 3, 128, 1024)).astype(np.float32)
        d["wo"] = pack_wio(Wo[i])
        d["qs"] = np.tile(qsc[i], 2).reshape(128, 1).astype(np.float32)
        d["ks"] = np.tile(ksc[i], 2).reshape(128, 1).astype(np.float32)
        W1a = np.zeros((1024, 2816), np.float32)
        W1a[:, :2730] = W1[i][:, :2730]
        W1g = np.zeros((1024, 2816), np.float32)
        W1g[:, :2730] = W1[i][:, 2730:]
        W2p = np.zeros((2816, 1024), np.float32)
        W2p[:2730] = W2[i]
        d["w1a"] = pack_wio(W1a)
        d["w1g"] = pack_wio(W1g)
        d["w2"] = pack_wio(W2p)
        b1a = np.zeros(2816, np.float32)
        b1a[:2730] = b1[i][:2730]
        b1g = np.zeros(2816, np.float32)
        b1g[:2730] = b1[i][2730:]
        d["b1a"] = np.ascontiguousarray(b1a.reshape(22, 128).T)
        d["b1g"] = np.ascontiguousarray(b1g.reshape(22, 128).T)
        d["b2"] = col_pack(b2[i])
        d["lng"] = col_pack(lng[i])
        d["lnb"] = col_pack(lnb[i])
        layers.append(d)
    return layers


def l0f_attn_consts():
    i = np.arange(128)[:, None]
    j = np.arange(128)[None, :]
    shared = {
        "maskA": np.where(i > j, -1e30, 0).astype(np.float32),
        "maskB": np.where(i < j, -1e30, 0).astype(np.float32),
        "epsc": np.full((128, 1), 1e-24, np.float32),
    }
    perm = np.zeros((128, 128), np.float32)
    for d_ in range(128):
        if d_ % 64 < 32:
            perm[d_ + 32, d_] = -1.0
        else:
            perm[d_ - 32, d_] = 1.0
    shared["perm"] = perm
    hs2 = np.zeros((128, 2), np.float32)
    hs2[:64, 0] = 1.0
    hs2[64:, 1] = 1.0
    shared["hs2"] = hs2
    shared["hs2t"] = np.ascontiguousarray(hs2.T)
    half = np.arange(0, 64, 2, dtype=np.float32)
    invf = (1.0 / 10000.0 ** (half / 64)).astype(np.float32)
    basesc = ((half + 0.4 * 64) / (1.4 * 64)).astype(np.float32)
    per = {}
    for h in range(2):
        start = h * 2048
        t = np.arange(L0F_KT, dtype=np.float32) + np.float32(start - 512)
        fr = (t[:, None] * invf[None, :]).astype(np.float32)
        power = ((t - np.float32(2048.0)) / np.float32(128.0))
        sc = (basesc[None, :] ** power[:, None]).astype(np.float32)
        cos = np.cos(fr).astype(np.float32)
        sin = np.sin(fr).astype(np.float32)

        def expand(v):
            r = np.concatenate([v, v], axis=1)
            r = np.concatenate([r, r], axis=1)
            return np.ascontiguousarray(r.T.astype(np.float32))
        bv = np.zeros((128, 8), np.float32)
        if h == 0:
            bv[:, :4] = -1e30
        per[h] = {"tcq": expand(cos * sc), "tsq": expand(sin * sc),
                  "tck": expand(cos / sc), "tsk": expand(sin / sc),
                  "bval": bv}
    return shared, per


def run_l0f_pre(z):
    nc = get_prog("l0f_pre", build_l0f_pre)
    in_maps = []
    for c in range(NCORES):
        b, h = c // 2, c % 2
        in_maps.append({"xi": fm_pack(z[b, h * T:(h + 1) * T]),
                        **host_aux()})
    res = run_spmd_cached(nc, in_maps, core_ids=list(range(NCORES)))
    out = np.empty_like(z)
    for c in range(NCORES):
        b, h = c // 2, c % 2
        out[b, h * T:(h + 1) * T] = fm_unpack(res.results[c]["yo"], T)
    return out


def run_l0f_attn(z, wl, consts):
    nc = get_prog("l0f_attn", build_l0f_attn)
    shared, per = consts
    in_maps = []
    for c in range(NCORES):
        b, h = c // 2, c % 2
        start = h * T
        xo = np.zeros((4096, 1024), np.float32)[0:2050]
        lo = start - 2
        if lo >= 0:
            xo[:] = z[b, lo:start + 2048]
        else:
            xo[2:] = z[b, 0:2048]
        xh = np.zeros((512, 1024), np.float32)
        hlo = start - 514
        if hlo >= 0:
            xh[:] = z[b, hlo:start - 2]
        m = {"xo": fm_pack(xo), "xh": fm_pack(xh),
             "wqkv": wl["wqkv"], "wo": wl["wo"],
             "qs": wl["qs"], "ks": wl["ks"], **shared, **per[h],
             **host_aux()}
        in_maps.append(m)
    res = run_spmd_cached(nc, in_maps, core_ids=list(range(NCORES)))
    znew = np.empty_like(z)
    xi_list = []
    for c in range(NCORES):
        b, h = c // 2, c % 2
        y = res.results[c]["yfm"]
        xi_list.append(y)
        full = fm_unpack(np.ascontiguousarray(
            y.reshape(128, 8, 2049)[:, :, 1:].reshape(128, 8 * 2048)), T)
        znew[b, h * T:(h + 1) * T] = full
    return znew, xi_list


def run_l0f_ffn(xi_list, wl, zouter=None):
    addres = zouter is not None
    name = "l0f_ffn_r" if addres else "l0f_ffn"
    nc = get_prog(name, lambda: build_l0f_ffn(addres))
    in_maps = []
    for c in range(NCORES):
        b, h = c // 2, c % 2
        m = {"xi": xi_list[c], "w1a": wl["w1a"], "w1g": wl["w1g"],
             "w2": wl["w2"], "b1a": wl["b1a"], "b1g": wl["b1g"],
             "b2": wl["b2"], "lng": wl["lng"], "lnb": wl["lnb"],
             **host_aux()}
        if addres:
            m["xres"] = fm_pack(zouter[b, h * T:(h + 1) * T])
        in_maps.append(m)
    res = run_spmd_cached(nc, in_maps, core_ids=list(range(NCORES)))
    out = np.empty((4, 4096, 1024), np.float32)
    for c in range(NCORES):
        b, h = c // 2, c % 2
        out[b, h * T:(h + 1) * T] = fm_unpack(res.results[c]["yo"], T)
    return out


def run_l0f(x, inputs, layers=6, consts=None, wls=None):
    if wls is None:
        wls = l0f_weight_inputs(inputs)
    if consts is None:
        consts = l0f_attn_consts()
    z = run_l0f_pre(x)
    for i in range(layers):
        z, xi_list = run_l0f_attn(z, wls[i], consts)
        z = run_l0f_ffn(xi_list, wls[i], zouter=x if i == layers - 1 else None)
    return z


def kernel(**inputs):
    x = np.asarray(inputs["L0_attn_a_x"], dtype=np.float32)
    wls = l0f_weight_inputs(inputs)
    consts = l0f_attn_consts()
    x = run_attn_a(x, inputs)
    x = run_l0f(x, inputs, consts=consts, wls=wls)
    x = run_attn_b(x, inputs)
    x = run_l0f(x, inputs, consts=consts, wls=wls)
    x = run_l1_attn_a(x, inputs)
    x = run_l1ffn(x, inputs)
    x = run_l1_attn_b(x, inputs)
    x = run_l1ffn(x, inputs)
    return x
